# version 07
# baseline (speedup 1.0000x reference)
import numpy as np
import ml_dtypes
import concourse.bass as bass
import concourse.mybir as mybir
from concourse.bass_utils import run_bass_kernel_spmd

F32 = mybir.dt.float32
BF16 = mybir.dt.bfloat16
I32 = mybir.dt.int32
AF = mybir.ActivationFunctionType
ALU = mybir.AluOpType
AX = mybir.AxisListType

D = 1024
DC = D // 128
T = 4096
TT = 512
NTT = T // TT
ENGINE_KINDS = ("sync", "tensor", "vector", "scalar", "gpsimd")


class Ctx:

    def __init__(self, nc):
        self.nc = nc
        self.sems = {}
        self.kind = None
        self.eng = None
        self.cnt = {}
        self.pb = []
        self.pbt = None
        self.tiles = {}
        self.alias = {}
        self.sem_usage = {}
        self.swdge_names = set()
        self.dma_names = set()

    def S(self, name):
        name = self.alias.get(name, name)
        if name not in self.sems:
            self.sems[name] = self.nc.alloc_semaphore(name)
        return self.sems[name]

    def _use(self, name):
        if self.kind == "probe":
            self.sem_usage.setdefault(getattr(self, "phase", "g"), set()).add(name)

    def sb(self, name, shape, dtype):
        if name in self.tiles:
            return self.tiles[name]
        if getattr(self, "arena_start", None) is None:
            self.tiles[name] = self.nc.alloc_sbuf_tensor(name, shape, dtype)
            return self.tiles[name]
        key = (self.phase, name)
        if key in self.arena_tiles:
            h, end = self.arena_tiles[key]
            self.arena_bump = end
            self.tiles[name] = h
            return h
        esz = {F32: 4, I32: 4, BF16: 2}[dtype]
        bpp = int(np.prod(shape[1:])) * esz
        off = (self.arena_bump + 31) // 32 * 32
        assert off + bpp <= self.arena_end, f"arena overflow: {name} in {self.phase}"
        h = self.nc.alloc_sbuf_tensor_at(f"{name}__{self.phase}", shape, dtype, offset=off)
        self.arena_bump = off + bpp
        self.arena_tiles[key] = (h, self.arena_bump)
        self.tiles[name] = h
        return h

    def begin_arena(self, size):
        self.persistent = set(self.tiles.keys())
        base = self.nc.sbuf_base
        self.arena_start = base
        self.arena_bump = base
        self.arena_end = base + size
        self.arena_tiles = getattr(self, "arena_tiles", {})
        self.phase = "p0"
        if not getattr(self, "_arena_reserved", False):
            self.nc.alloc_sbuf_tensor("arena_resv", [128, size], mybir.dt.int8)
            self._arena_reserved = True

    def new_phase(self, tag):
        for k in list(self.tiles.keys()):
            if k not in self.persistent:
                del self.tiles[k]
        self.arena_bump = self.arena_start
        self.phase = tag

    def n(self, sem):
        sem = self.alias.get(sem, sem)
        return self.cnt.get(sem, 0)

    def bump(self, sem, v):
        self._use(sem)
        sem = self.alias.get(sem, sem)
        self.cnt[sem] = self.cnt.get(sem, 0) + v
        return self.cnt[sem]

    def dma(self, q, out, in_, sem, **kw):
        if self.kind == "probe":
            self.dma_names.add(sem)
            if q == "gpsimd":
                self.swdge_names.add(sem)
        self.bump(sem, 16)
        if self.kind == q:
            self.eng.dma_start(out=out, in_=in_, **kw).then_inc(self.S(sem), 16)
        return self.n(sem)

    def dmat(self, q, out, in_, sem):
        if self.kind == "probe":
            self.dma_names.add(sem)
        self.bump(sem, 16)
        if self.kind == q:
            self.eng.dma_start_transpose(out, in_).then_inc(self.S(sem), 16)
        return self.n(sem)

    def op(self, who, name, *a, sem=None, **kw):
        if sem is not None:
            self.bump(sem, 1)
        if self.kind == who:
            r = getattr(self.eng, name)(*a, **kw)
            if sem is not None:
                r.then_inc(self.S(sem), 1)

    def mark(self, who, sem):
        self.bump(sem, 1)
        if self.kind == who:
            self.eng.nop().then_inc(self.S(sem), 1)

    def wait(self, who, sem, val=None):
        self._use(sem)
        if self.kind == who:
            v = self.n(sem) if val is None else max(0, val)
            self.eng.wait_ge(self.S(sem), v)

    def wait_all(self, sem, val=None):
        for k in ENGINE_KINDS:
            self.wait(k, sem, val)


def run_ctx(nc, emitters, n_psum_banks=8, with_pbt=False):
    c = Ctx(nc)
    for i in range(n_psum_banks):
        c.pb.append(nc.alloc_psum_tensor(f"pb{i}", [128, 512], F32))
    if with_pbt:
        c.pbt = c.pb[6][:, :].bitcast(BF16)
    c.kind = "probe"
    c.eng = None
    c.cnt = {}
    for em in emitters:
        em(c)
    names_order = []
    seen = set()
    for ph in c.sem_usage:
        for nm in sorted(c.sem_usage[ph]):
            if nm not in seen:
                seen.add(nm)
                names_order.append(nm)
    conflicts = {nm: set() for nm in seen}
    for ph, nms in c.sem_usage.items():
        for a in nms:
            for b in nms:
                if a != b:
                    conflicts[a].add(b)
    color = {}
    comp = [nm for nm in names_order if nm not in c.swdge_names and nm not in c.dma_names]
    for nm in comp:
        used = {color[o] for o in conflicts[nm] if o in color}
        k = 0
        while k in used:
            k += 1
        color[nm] = k
    base = (max(color.values()) + 1) if color else 0
    hwd = [nm for nm in names_order if nm in c.dma_names and nm not in c.swdge_names]
    color2 = {}
    for nm in hwd:
        used = {color2[o] for o in conflicts[nm] if o in color2}
        k = 0
        while k in used:
            k += 1
        color2[nm] = k
    for nm, k in color2.items():
        color[nm] = base + k
    base2 = base + ((max(color2.values()) + 1) if color2 else 0)
    for i, nm in enumerate(sorted(c.swdge_names)):
        color[nm] = base2 + i
    c.alias = {nm: f"sem{color[nm]}" for nm in names_order}
    ncolors = max(color.values()) + 1 if color else 0
    assert ncolors <= 100, f"too many semaphores: {ncolors}"
    with nc.Block() as block:
        def mk(kind):
            def body(eng):
                c.kind = kind
                c.eng = eng
                c.cnt = {}
                for em in emitters:
                    em(c)
            return body
        block.sync(mk("sync"))
        block.tensor(mk("tensor"))
        block.vector(mk("vector"))
        block.scalar(mk("scalar"))
        block.gpsimd(mk("gpsimd"))
    return c



def alloc_consts(nc, c):
    if "ident_dram" not in c.tiles:
        eye = np.eye(128, dtype=np.float32)
        c.tiles["ident_dram"] = nc.inline_tensor(eye.astype(ml_dtypes.bfloat16), name="ident_c").ap()
        c.tiles["ones_row_dram"] = nc.inline_tensor(np.ones((1, 128), np.float32), name="ones_row_c").ap()
        c.tiles["ones_col_dram"] = nc.inline_tensor(np.ones((128, 1), np.float32), name="ones_col_c").ap()
    if "masks_dram" not in c.tiles:
        nc = c.nc
        m = np.zeros((128, 4 * 512), np.float32)
        for r in range(4):
            for p in range(128):
                q0 = 128 * r + p
                if q0 < 512:
                    m[p, r * 512 + q0:(r + 1) * 512] = 1.0
        c.tiles["masks_dram"] = nc.inline_tensor(m.astype(ml_dtypes.bfloat16), name="masks_c").ap()
        hs = np.zeros((128, 2), np.float32)
        hs[0:64, 0] = 1.0
        hs[64:128, 1] = 1.0
        c.tiles["hsel_dram"] = nc.inline_tensor(hs.astype(ml_dtypes.bfloat16), name="hsel_c").ap()
        c.tiles["ones_colb_dram"] = nc.inline_tensor(np.ones((128, 1), ml_dtypes.bfloat16), name="ones_colb_c").ap()
        c.tiles["ones_rowb_dram"] = nc.inline_tensor(np.ones((1, 512), ml_dtypes.bfloat16), name="ones_rowb_c").ap()
        c.tiles["ones_rowb128_dram"] = nc.inline_tensor(np.ones((1, 128), ml_dtypes.bfloat16), name="ones_rowb128_c").ap()
    c.sb("ones_row", [1, 128], F32)
    c.sb("ones_col", [128, 1], F32)
    c.sb("ident", [128, 128], BF16)
    c.sb("masks", [128, 4 * 512], BF16)
    c.sb("hsel", [128, 2], BF16)
    c.sb("ones_colb", [128, 1], BF16)
    c.sb("ones_rowb", [1, 512], BF16)
    c.sb("ones_rowb128", [1, 128], BF16)
    c.tiles["identb8"] = c.sb("ident", [128, 128], BF16)


def emit_load_consts(c):
    t = c.tiles
    c.dma("sync", t["ones_row"][:], t["ones_row_dram"], "cst")
    c.dma("sync", t["ones_col"][:], t["ones_col_dram"], "cst")
    c.dma("sync", t["ident"][:], t["ident_dram"], "cst")
    c.dma("sync", t["masks"][:], t["masks_dram"], "cst")
    c.dma("sync", t["hsel"][:], t["hsel_dram"], "cst")
    c.dma("sync", t["ones_colb"][:], t["ones_colb_dram"], "cst")
    c.dma("sync", t["ones_rowb"][:], t["ones_rowb_dram"], "cst")
    c.dma("sync", t["ones_rowb128"][:], t["ones_rowb128_dram"], "cst")
    c.wait_all("cst")


def chunked(ap_dram, p=128):
    return ap_dram.rearrange("(c p) w -> p c w", p=p)


def load_wtile(c, q, sbuf_tile, tiled_ap, fc, nch, sem):
    c.dma(q, sbuf_tile[:, 0:nch * 128].rearrange("p (cc w) -> p cc w", cc=nch),
          tiled_ap[fc].rearrange("p (cc w) -> p cc w", cc=nch), sem)


def load_vtile(c, q, sbuf_tile, tiled_ap, f0, nf, nch, sem, W=128):
    c.dma(q, sbuf_tile[:, 0:nch * nf * W].rearrange("p (cc fj w) -> p cc fj w",
                                                    cc=nch, fj=nf, w=W),
          tiled_ap[f0:f0 + nf].rearrange("f p (cc w) -> p cc f w", w=W), sem)


def emit_cast_tiled(c, src, dst, R, C, W=128):
    srcv = src.rearrange("(cc p) (fc w) -> fc p cc w", p=128, w=W)
    for fc in range(C // W):
        c.dma("gpsimd", dst[fc].rearrange("p (cc w) -> p cc w", w=W),
              srcv[fc], "pcs")


def load_chunks(c, q, sbuf_tile, dram_ap, col_slice, nch, sem):
    W = col_slice.stop - col_slice.start
    c.dma(q, sbuf_tile[:, 0:nch * W].rearrange("p (cc w) -> p cc w", cc=nch),
          chunked(dram_ap)[:, :, col_slice], sem)


def store_chunks(c, q, dram_ap, col_slice, sbuf_tile, nch, sem):
    W = col_slice.stop - col_slice.start
    c.dma(q, chunked(dram_ap)[:, :, col_slice],
          sbuf_tile[:, 0:nch * W].rearrange("p (cc w) -> p cc w", cc=nch), sem)



def emit_cast_dram(c, src, dst):
    c.dma("gpsimd", dst[:, :], src[:, :], "pcs")


def emit_transpose_scratch(c, src_bf, dst, R, C):
    strip = c.sb("at_K", [128, DC * T], BF16)[:, 0:3072]
    for cc in range(C // 128):
        c.wait("scalar", "pts", None)
        for rc in range(R // 128):
            c.dmat("scalar", strip[:, rc * 128:(rc + 1) * 128],
                   src_bf[rc * 128:(rc + 1) * 128, cc * 128:(cc + 1) * 128], "ptt")
        c.wait("sync", "ptt")
        c.dma("sync", dst.rearrange("f p (cc w) -> p f cc w", w=128)[:, :, cc, :],
              strip[:, 0:R], "pts")
    c.wait_all("pts")


def emit_vec_rows(c, src_row, dst_tile, n):
    row = c.sb("ln_r1", [1, TT], F32)
    nch = n // 128
    src2 = src_row if len(src_row.shape) == 2 else src_row.rearrange("(o n) -> o n", o=1)
    c.wait("tensor", "vrc", None)
    for seg in range(n // 512):
        c.wait("sync", "vrm", None)
        c.dma("sync", row[0:1, :], src2[0:1, seg * 512:(seg + 1) * 512], "vrl")
        c.wait("tensor", "vrl")
        for ch in range(4):
            c.op("tensor", "matmul", c.pb[5][:, seg * 4 + ch:seg * 4 + ch + 1],
                 row[0:1, ch * 128:(ch + 1) * 128], c.tiles["ones_row"][0:1, 0:1],
                 sem="vrm", start=True, stop=True)
    c.wait("vector", "vrm")
    c.op("vector", "tensor_copy", dst_tile[:, 0:nch], c.pb[5][:, 0:nch], sem="vrc")
    c.wait_all("vrc")



def emit_ln_stats(c, xT, rowA, rowB, eps=1e-5):
    xb = c.sb("ln_xb", [128, 2 * TT], F32)
    sq = c.sb("ln_sq", [128, 2 * TT], BF16)
    r1 = c.sb("ln_r1", [1, TT], F32)
    r2 = c.sb("ln_r2", [1, TT], F32)
    xc = chunked(xT)
    for tt in range(NTT):
        sl = slice(tt * TT, (tt + 1) * TT)
        c.wait("tensor", "lnb", None)
        c.wait("sync", "lnm", None)
        for cc in range(DC):
            pi = cc % 2
            xbt = xb[:, pi * TT:(pi + 1) * TT]
            sqt = sq[:, pi * TT:(pi + 1) * TT]
            c.wait("sync", "lnxm", c.n("lnxm") - 1)
            c.wait("sync", "lns", c.n("lns") - 1)
            c.dma("sync", xbt, xc[:, cc, sl], f"lnx{pi}")
            c.wait("tensor", f"lnx{pi}")
            c.op("tensor", "matmul", c.pb[0][0:1, :], c.tiles["ones_col"][:, 0:1],
                 xbt, sem=("lnm" if cc == DC - 1 else "lnxm"), start=(cc == 0),
                 stop=(cc == DC - 1))
            c.wait("scalar", f"lnx{pi}")
            c.wait("scalar", "lnq", c.n("lnq") - 1)
            c.op("scalar", "activation", sqt, xbt, AF.Square, sem="lns")
            c.wait("tensor", "lns")
            c.op("tensor", "matmul", c.pb[1][0:1, :], c.tiles["ones_colb"][:, 0:1],
                 sqt, sem="lnq", start=(cc == 0), stop=(cc == DC - 1))
        c.wait("vector", "lnm")
        c.wait("vector", "lnq")
        inv = 1.0 / D
        c.op("vector", "tensor_scalar", r1[:, :], c.pb[0][0:1, :], inv, None, ALU.mult)
        c.op("vector", "tensor_scalar", r2[:, :], c.pb[1][0:1, :], inv, None, ALU.mult, sem="lnv")
        t1r = c.sb("z_t1", [128, TT], F32)[0:1, :]
        c.wait("vector", "lnv")
        c.op("vector", "tensor_tensor", t1r, r1[:, :], r1[:, :], ALU.mult, sem="lnv")
        c.wait("vector", "lnv")
        c.op("vector", "tensor_tensor", r2[:, :], r2[:, :], t1r, ALU.subtract, sem="lnv")
        c.wait("vector", "lnv")
        c.op("vector", "tensor_scalar", r2[:, :], r2[:, :], eps, None, ALU.add, sem="lnv")
        c.wait("vector", "lnv")
        c.op("vector", "reciprocal", r2[:, :], r2[:, :], sem="lnv")
        c.wait("scalar", "lnv")
        c.op("scalar", "activation", rowA[:, sl], r2[:, :], AF.Sqrt, sem="lnsq")
        c.wait("vector", "lnsq")
        c.op("vector", "tensor_tensor", t1r, r1[:, :], rowA[:, sl], ALU.mult, sem="lnv")
        c.wait("vector", "lnv")
        c.op("vector", "tensor_scalar", rowB[:, sl], t1r, -1.0, None, ALU.mult, sem="lnb")
    c.wait_all("lnb")
    c.wait_all("lns")
    c.wait_all("lnq")


def emit_z_tile_stream(c, xT, rowA, rowB, tt, zset, nchunks=DC, bankA=4, bankB=5,
                       zstride=None, zoff=0):
    if zstride is None:
        zstride = TT
    sl = slice(tt * TT, (tt + 1) * TT)
    xb = c.sb("ln_xb", [128, 2 * TT], F32)
    t1 = c.sb("z_t1", [128, TT], F32)
    xc = chunked(xT)
    c.op("tensor", "matmul", c.pb[bankA][:, :], c.tiles["ones_rowb128"][0:1, :], rowA[0:1, sl],
         sem="zbm", start=True, stop=True)
    c.op("tensor", "matmul", c.pb[bankB][:, :], c.tiles["ones_rowb128"][0:1, :], rowB[0:1, sl],
         sem="zbm", start=True, stop=True)
    c.wait("vector", "zbm")
    for cc in range(nchunks):
        pi = cc % 2
        xbt = xb[:, pi * TT:(pi + 1) * TT]
        c.wait("sync", "zbc", c.n("zbc") - 3)
        c.dma("sync", xbt, xc[:, cc, sl], f"zxl{pi}")
        c.wait("vector", f"zxl{pi}")
        c.wait("vector", "zbc")
        c.op("vector", "tensor_tensor", t1[:, :], xbt, c.pb[bankA][:, :], ALU.mult, sem="zbc")
        c.wait("vector", "zbc")
        c.op("vector", "tensor_tensor", zset[:, cc * zstride + zoff:cc * zstride + zoff + TT],
             t1[:, :], c.pb[bankB][:, :], ALU.add, sem="zbc")


def emit_z_tile(c, xs_tile, rowA, rowB, tt, zset, nchunks=DC, bankA=4, bankB=5,
                zstride=None, zoff=0):
    if zstride is None:
        zstride = TT
    sl = slice(tt * TT, (tt + 1) * TT)
    t1 = c.sb("z_t1", [128, TT], F32)
    c.op("tensor", "matmul", c.pb[bankA][:, :], c.tiles["ones_rowb128"][0:1, :], rowA[0:1, sl],
         sem="zbm", start=True, stop=True)
    c.op("tensor", "matmul", c.pb[bankB][:, :], c.tiles["ones_rowb128"][0:1, :], rowB[0:1, sl],
         sem="zbm", start=True, stop=True)
    c.wait("vector", "zbm")
    for cc in range(nchunks):
        c.wait("vector", "zbc")
        c.op("vector", "tensor_tensor", t1[:, :],
             xs_tile[:, cc * zstride + zoff:cc * zstride + zoff + TT],
             c.pb[bankA][:, :], ALU.mult, sem="zbc")
        c.wait("vector", "zbc")
        c.op("vector", "tensor_tensor", zset[:, cc * zstride + zoff:cc * zstride + zoff + TT],
             t1[:, :], c.pb[bankB][:, :], ALU.add, sem="zbc")



def build_l0_ffn(nc, xT, yT, wgT, wuT, wdT):
    H = 3072
    HC = H // 128
    s = 1.0 / 127.0

    NTP = 2
    NPS = NTT // NTP
    W = NTP * TT

    def emit(c):
        rowA = c.sb("rowA", [1, T], BF16)
        rowB = c.sb("rowB", [1, T], BF16)
        zset = c.sb("zset", [128, DC * W], BF16)
        hset = c.sb("hset", [128, HC * W], BF16)
        h1 = c.sb("h1", [128, 2 * W], BF16)
        wgb = [c.sb(f"wgb{i}", [128, DC * 128], BF16) for i in range(2)]
        wub = [c.sb(f"wub{i}", [128, DC * 128], BF16) for i in range(2)]
        wdb = [c.sb(f"wdb{i}", [128, HC * 128], BF16) for i in range(2)]
        xs = c.sb("ffn_x", [128, DC * W], F32)
        emit_ln_stats(c, xT, rowA, rowB)
        for ps in range(NPS):
            sl = slice(ps * W, (ps + 1) * W)
            c.wait("sync", "res", None)
            load_chunks(c, "sync", xs, xT, sl, DC, "xl")
            c.wait("vector", "xl")
            c.wait("vector", "gum", None)
            c.wait("vector", "dnm", None)
            c.wait("tensor", "zbc", None)
            c.wait("tensor", "sil", None)
            c.wait("tensor", "hev", None)
            for t in range(NTP):
                if t > 0:
                    c.wait("tensor", "zbc", None)
                emit_z_tile(c, xs, rowA, rowB, ps * NTP + t, zset, zstride=W, zoff=t * TT)
            gum0 = c.n("gum")
            sil0 = c.n("sil")
            hev0 = c.n("hev")
            for fc in range(HC):
                pi = fc % 2
                gbk = (c.pb[0 + 4 * pi], c.pb[1 + 4 * pi])
                ubk = (c.pb[2 + 4 * pi], c.pb[3 + 4 * pi])
                c.wait("sync", "gum", gum0 + fc - 1)
                load_wtile(c, "sync", wgb[pi], wgT, fc, DC, f"wgl{pi}")
                load_wtile(c, "sync", wub[pi], wuT, fc, DC, f"wul{pi}")
                c.wait("tensor", f"wgl{pi}")
                c.wait("tensor", f"wul{pi}")
                if fc == 0:
                    c.wait("tensor", "dnv", None)
                    c.wait("tensor", "zbc")
                elif fc == 1:
                    c.wait("tensor", "zbc", None)
                else:
                    c.wait("tensor", "sil", sil0 + fc - 1)
                    c.wait("tensor", "hev", hev0 + 2 * fc - 2)
                for t in range(NTP):
                    for cc in range(DC):
                        c.op("tensor", "matmul", gbk[t][:, :], wgb[pi][:, cc * 128:(cc + 1) * 128],
                             zset[:, cc * W + t * TT:cc * W + (t + 1) * TT],
                             start=(cc == 0), stop=(cc == DC - 1))
                for t in range(NTP):
                    for cc in range(DC):
                        c.op("tensor", "matmul", ubk[t][:, :], wub[pi][:, cc * 128:(cc + 1) * 128],
                             zset[:, cc * W + t * TT:cc * W + (t + 1) * TT],
                             sem=("gum" if (cc == DC - 1 and t == NTP - 1) else None),
                             start=(cc == 0), stop=(cc == DC - 1))
                c.wait("scalar", "gum", gum0 + fc + 1)
                c.wait("scalar", "hev", hev0 + 2 * fc - 2)
                for t in range(NTP):
                    c.op("scalar", "activation", h1[:, (pi * NTP + t) * TT:(pi * NTP + t + 1) * TT],
                         gbk[t][:, :], AF.Silu, sem=("sil" if t == NTP - 1 else None), scale=s)
                c.wait("vector", "sil", sil0 + fc + 1)
                for t in range(NTP):
                    c.op("vector", "tensor_tensor", hset[:, fc * W + t * TT:fc * W + (t + 1) * TT],
                         ubk[t][:, :], h1[:, (pi * NTP + t) * TT:(pi * NTP + t + 1) * TT],
                         ALU.mult, sem="hev")
            c.wait("tensor", "hev")
            dnm0 = c.n("dnm")
            dnv0 = c.n("dnv")
            for dc in range(DC):
                pj = dc % 2
                dbk = (c.pb[0 + 2 * pj], c.pb[1 + 2 * pj])
                c.wait("sync", "dnm", dnm0 + dc - 1)
                load_wtile(c, "sync", wdb[pj], wdT, dc, HC, f"wdl{pj}")
                c.wait("tensor", f"wdl{pj}")
                if dc <= 1:
                    c.wait("tensor", "sil", None)
                    c.wait("tensor", "hev", None)
                else:
                    c.wait("tensor", "dnv", dnv0 + 2 * dc - 2)
                for t in range(NTP):
                    for hc in range(HC):
                        c.op("tensor", "matmul", dbk[t][:, :], wdb[pj][:, hc * 128:(hc + 1) * 128],
                             hset[:, hc * W + t * TT:hc * W + (t + 1) * TT],
                             sem=("dnm" if (hc == HC - 1 and t == NTP - 1) else None),
                             start=(hc == 0), stop=(hc == HC - 1))
                c.wait("vector", "dnm", dnm0 + dc + 1)
                for t in range(NTP):
                    c.op("vector", "scalar_tensor_tensor",
                         xs[:, dc * W + t * TT:dc * W + (t + 1) * TT],
                         dbk[t][:, :], s * s, xs[:, dc * W + t * TT:dc * W + (t + 1) * TT],
                         ALU.mult, ALU.add, sem="dnv")
            c.wait("sync", "dnv")
            store_chunks(c, "sync", yT, sl, xs, DC, "res")
        c.wait_all("res")

    return emit


def build_l0_ffn_pre(nc, wg_i, wu_i, wd_i, wgT, wuT, wdT, wg_bf, wu_bf, wd_bf):
    def emit(c):
        emit_cast_dram(c, wg_i, wg_bf)
        emit_cast_dram(c, wu_i, wu_bf)
        emit_cast_dram(c, wd_i, wd_bf)
        c.wait_all("pcs")
        emit_transpose_scratch(c, wg_bf, wgT, 3072, 1024)
        emit_transpose_scratch(c, wu_bf, wuT, 3072, 1024)
        emit_transpose_scratch(c, wd_bf, wdT, 1024, 3072)
    return emit





def emit_permute_qkv_pre(c, src, dst):
    pool = c.sb("at_Q", [128, 4096], BF16)
    tin = pool[:, 0:1024]
    tout = pool[:, 1024:2048]
    for cc in range(8):
        c.wait("gpsimd", "ppqv", None)
        c.dma("gpsimd", tin[:], src[cc * 128:(cc + 1) * 128, :], "ppql")
        c.wait("vector", "ppql")
        c.wait("vector", "ppqs", None)
        c.op("vector", "tensor_copy", tout[:].rearrange("p (h i) -> p i h", h=16, i=64),
             tin[:].rearrange("p (i h) -> p i h", i=64, h=16), sem="ppqv")
        c.wait("sync", "ppqv")
        c.dma("sync", dst.rearrange("f p (cc w) -> p f cc w", w=128)[:, :, cc, :],
              tout[:], "ppqs")
    c.wait_all("ppqs")


def emit_gather_ow_pre(c, src, dst):
    tin = c.sb("at_Q", [128, 4096], BF16)[:, 0:1024]
    srcg = src.rearrange("(i h) k -> h i k", i=64, h=16)
    for cc in range(8):
        c.wait("gpsimd", "ppos", None)
        c.dma("gpsimd", tin[:], srcg[2 * cc:2 * cc + 2, :, :], "ppol")
        c.wait("sync", "ppol")
        c.dma("sync", dst.rearrange("f p (cc w) -> p f cc w", w=128)[:, :, cc, :],
              tin[:], "ppos")
    c.wait_all("ppos")


def rope_apply(c, dst_sb, tabC, tabS, bank, msem, vsem):
    tmp = c.sb("at_qsq", [128, TT], BF16)
    P2 = c.tiles["ropeP2"]
    c.op("tensor", "matmul", c.pb[bank][:, :], P2[:, :], dst_sb, start=True, stop=True,
         sem=msem)
    c.wait("vector", vsem)
    c.op("vector", "tensor_tensor", tmp[:, :], dst_sb, tabC, ALU.mult, sem=vsem)
    c.wait("vector", msem)
    c.op("vector", "tensor_tensor", c.pb[bank][:, :], c.pb[bank][:, :], tabS, ALU.mult,
         sem=vsem)
    c.wait("vector", vsem)
    c.op("vector", "tensor_tensor", dst_sb, tmp[:, :], c.pb[bank][:, :], ALU.add, sem=vsem)


def emit_causal_attn(c, P):
    nc = c.nc
    rowA = c.sb("rowA", [1, T], BF16)
    rowB = c.sb("rowB", [1, T], BF16)
    zq = c.sb("at_z", [128, DC * TT], BF16)
    Ksb = c.sb("at_K", [128, DC * T], BF16)
    Vsb = c.sb("at_V", [128, (T // 128) * 1024], BF16)
    Qsb = c.sb("at_Q", [128, DC * TT], BF16)
    att = zq
    Ebuf = c.sb("at_E", [128, 4 * TT], BF16)
    qsq = c.sb("at_qsq", [128, TT], BF16)
    kmg = c.sb("at_kmg", [1, 2], F32)
    qmx = c.sb("at_qmx", [1, 16], F32)
    crow = c.sb("at_crow", [1, 16], F32)
    biasc = c.sb("at_bias", [128, 16], F32)
    rA_row = c.sb("ln_r1", [1, TT], F32)
    rB_row = c.sb("ln_r2", [1, TT], F32)
    rbc = c.sb("at_rbc", [128, TT], BF16)
    wpool = c.sb("at_wpool", [128, DC * 512 + 2 * DC * 128], BF16)
    vwb = [wpool[:, 0:DC * 512] for i in range(2)]
    wb = [wpool[:, DC * 512 + i * DC * 128:DC * 512 + (i + 1) * DC * 128] for i in range(2)]
    rxb = c.sb("ln_xb", [128, 2 * TT], F32)
    hsel = c.tiles["hsel"]
    onesr = c.tiles["ones_row"]
    masks = c.tiles["masks"]
    rope = P.get("rope")

    emit_ln_stats(c, P["xT"], rowA, rowB)
    c.op("vector", "memset", kmg[:, :], 0.0, sem="akg")

    for tt in range(NTT):
        sl = slice(tt * TT, (tt + 1) * TT)
        c.wait("vector", "akm", None)
        c.wait("vector", "avm", None)
        c.wait("tensor", "zbc", None)
        emit_z_tile_stream(c, P["xT"], rowA, rowB, tt, zq, bankA=4, bankB=5)
        for fc in range(DC):
            pi = fc % 2
            kbank = c.pb[0 + pi]
            if fc == 0:
                c.wait("sync", "avm", None)
            c.wait("sync", "akm", c.n("akm") - 1)
            load_wtile(c, "sync", wb[pi], P["kw"], fc, DC, f"akw{pi}")
            c.wait("tensor", f"akw{pi}")
            if fc == 0:
                c.wait("tensor", "zbc")
            c.wait("tensor", "akev", c.n("akev") - 1)
            last = (P.get("kb_sb") is None)
            for cc in range(DC):
                c.op("tensor", "matmul", kbank[:, :], wb[pi][:, cc * 128:(cc + 1) * 128],
                     zq[:, cc * TT:(cc + 1) * TT], start=(cc == 0),
                     stop=(cc == DC - 1 and last),
                     sem=("akm" if (cc == DC - 1 and last) else None))
            if P.get("kb_sb") is not None:
                c.op("tensor", "matmul", kbank[:, :], P["kb_sb"][0:1, fc * 128:(fc + 1) * 128],
                     c.tiles["ones_rowb"][0:1, :], start=False, stop=True, sem="akm")
            c.wait("scalar", "akm")
            c.op("scalar", "activation", Ksb[:, fc * T + tt * TT:fc * T + (tt + 1) * TT],
                 kbank[:, :], AF.Copy, sem="akev")
        c.wait("sync", "akm", None)
        for kb in range(TT // 128):
            for half in range(2):
                pi = (kb * 2 + half) % 2
                vbank = c.pb[2 + pi]
                c.wait("sync", "avm", None)
                load_vtile(c, "sync", vwb[pi], P["vw"], half * 4, 4, DC, f"avw{pi}")
                c.wait("tensor", f"avw{pi}")
                c.wait("tensor", "avev", c.n("avev") - 1)
                last = (P.get("vb_sb") is None)
                for cc in range(DC):
                    c.op("tensor", "matmul", vbank[:, :],
                         zq[:, cc * TT + kb * 128:cc * TT + (kb + 1) * 128],
                         vwb[pi][:, cc * 512:(cc + 1) * 512],
                         start=(cc == 0), stop=(cc == DC - 1 and last),
                         sem=("avm" if (cc == DC - 1 and last) else None))
                if P.get("vb_sb") is not None:
                    c.op("tensor", "matmul", vbank[:, :], c.tiles["ones_rowb"][0:1, 0:128],
                         P["vb_sb"][0:1, half * 512:(half + 1) * 512], start=False, stop=True,
                         sem="avm")
                c.wait("vector", "avm")
                vcol = (tt * 4 + kb) * 1024 + half * 512
                c.op("vector", "tensor_copy", Vsb[:, vcol:vcol + 512], vbank[:, :], sem="avev")
    if rope is not None:
        tabC = c.sb("ln_sq", [128, 2 * TT], BF16)[:, 0:TT]
        tabS = c.sb("ln_sq", [128, 2 * TT], BF16)[:, TT:2 * TT]
        c.wait("vector", "akev")
        c.wait("scalar", "akev")
        c.wait("tensor", "akev")
        for tt in range(NTT):
            c.wait("sync", "akr", None)
            c.dma("sync", tabC, rope[0, :, tt * TT:(tt + 1) * TT], "artl")
            c.dma("sync", tabS, rope[1, :, tt * TT:(tt + 1) * TT], "artl")
            c.wait("vector", "artl")
            c.wait("tensor", "artl")
            for fc in range(DC):
                bank = fc % 2
                c.wait("tensor", "akr", c.n("akr") - 3)
                rope_apply(c, Ksb[:, fc * T + tt * TT:fc * T + (tt + 1) * TT],
                           tabC, tabS, bank, "akrm", "akr")
    c.wait("scalar", "akev")
    c.wait("tensor", "akev")
    if rope is not None:
        c.wait("scalar", "akr")
    c.wait_all("avev")
    for fc in range(DC):
        for tt in range(NTT):
            ks = Ksb[:, fc * T + tt * TT:fc * T + (tt + 1) * TT]
            c.wait("scalar", "akn", None)
            c.op("scalar", "activation", qsq[:, :], ks, AF.Square, sem="aks")
            c.wait("tensor", "aks")
            c.wait("tensor", "aknv", None)
            c.op("tensor", "matmul", c.pb[0][0:1, :], hsel[:, 0:1], qsq[:, :], start=True, stop=True)
            c.op("tensor", "matmul", c.pb[1][0:1, :], hsel[:, 1:2], qsq[:, :], sem="akn",
                 start=True, stop=True)
            c.wait("vector", "akn")
            c.op("vector", "reduce_max", rA_row[0:1, 0:1], c.pb[0][0:1, :], AX.X)
            c.op("vector", "reduce_max", rA_row[0:1, 1:2], c.pb[1][0:1, :], AX.X, sem="aknr")
            c.wait("vector", "aknr")
            c.wait("vector", "akg", None)
            c.op("vector", "tensor_max", kmg[0:1, 0:1], kmg[0:1, 0:1], rA_row[0:1, 0:1], sem="aknv")
            c.wait("vector", "aknv")
            c.op("vector", "tensor_max", kmg[0:1, 0:1], kmg[0:1, 0:1], rA_row[0:1, 1:2], sem="aknv")
    c.wait_all("aknv")
    c.wait_all("akn")
    c.wait_all("aks")

    for qt in range(NTT):
        sl = slice(qt * TT, (qt + 1) * TT)
        c.wait("vector", "aqm", None)
        c.wait("vector", "aom", None)
        c.wait("tensor", "zbc", None)
        c.wait("tensor", "aav", None)
        c.wait("tensor", "alv", None)
        c.wait("tensor", "aov", None)
        c.wait("sync", "aov", None)
        c.wait("sync", "ares0", None)
        c.wait("sync", "ares1", None)
        emit_z_tile_stream(c, P["xT"], rowA, rowB, qt, zq, bankA=4, bankB=5)
        c.wait("scalar", "asc", None)
        for fc in range(DC):
            pi = fc % 2
            qbank = c.pb[0 + pi]
            if fc == 0:
                c.wait("sync", "avm", None)
                c.wait("sync", "aom", None)
            c.wait("sync", "aqm", c.n("aqm") - 1)
            load_wtile(c, "sync", wb[pi], P["qw"], fc, DC, f"aqw{pi}")
            c.wait("tensor", f"aqw{pi}")
            if fc == 0:
                c.wait("tensor", "zbc")
            c.wait("tensor", "aqev", c.n("aqev") - 1)
            last = (P.get("qb_sb") is None)
            for cc in range(DC):
                c.op("tensor", "matmul", qbank[:, :], wb[pi][:, cc * 128:(cc + 1) * 128],
                     zq[:, cc * TT:(cc + 1) * TT], start=(cc == 0),
                     stop=(cc == DC - 1 and last),
                     sem=("aqm" if (cc == DC - 1 and last) else None))
            if P.get("qb_sb") is not None:
                c.op("tensor", "matmul", qbank[:, :], P["qb_sb"][0:1, fc * 128:(fc + 1) * 128],
                     c.tiles["ones_rowb"][0:1, :], start=False, stop=True, sem="aqm")
            c.wait("scalar", "aqm")
            c.op("scalar", "activation", Qsb[:, fc * TT:(fc + 1) * TT], qbank[:, :],
                 AF.Copy, sem="aqev", scale=P["scale"])
        if rope is not None:
            tabC = c.sb("ln_sq", [128, 2 * TT], BF16)[:, 0:TT]
            tabS = c.sb("ln_sq", [128, 2 * TT], BF16)[:, TT:2 * TT]
            c.wait("vector", "aqev")
            c.wait("vector", "aqn", None)
            c.wait("tensor", "aqn", None)
            c.wait("sync", "aqr", None)
            c.dma("sync", tabC, rope[0, :, sl], "artl")
            c.dma("sync", tabS, rope[1, :, sl], "artl")
            c.wait("vector", "artl")
            c.wait("tensor", "artl")
            c.wait("tensor", "aqev", None)
            for fc in range(DC):
                bank = 2 + fc % 2
                c.wait("tensor", "aqr", c.n("aqr") - 3)
                rope_apply(c, Qsb[:, fc * TT:(fc + 1) * TT], tabC, tabS, bank, "aqrm", "aqr")
            c.wait("scalar", "aqr")
        else:
            c.wait("scalar", "aqev")
        for fc in range(DC):
            c.wait("scalar", "aqn", None)
            c.op("scalar", "activation", qsq[:, :], Qsb[:, fc * TT:(fc + 1) * TT], AF.Square, sem="aqs")
            c.wait("tensor", "aqs")
            c.wait("tensor", "aqnr", None)
            c.op("tensor", "matmul", c.pb[2][0:1, :], hsel[:, 0:1], qsq[:, :], start=True, stop=True)
            c.op("tensor", "matmul", c.pb[3][0:1, :], hsel[:, 1:2], qsq[:, :], sem="aqn",
                 start=True, stop=True)
            c.wait("vector", "aqn")
            c.op("vector", "reduce_max", qmx[0:1, 2 * fc:2 * fc + 1], c.pb[2][0:1, :], AX.X)
            c.op("vector", "reduce_max", qmx[0:1, 2 * fc + 1:2 * fc + 2], c.pb[3][0:1, :], AX.X,
                 sem="aqnr")
        c.wait("scalar", "aqnr")
        c.op("scalar", "activation", crow[0:1, 0:16], qmx[0:1, 0:16], AF.Ln, sem="acr")
        c.wait("scalar", "acr")
        c.op("scalar", "activation", crow[0:1, 0:16], crow[0:1, 0:16], AF.Exp, sem="acr", scale=0.5)
        c.op("scalar", "activation", kmg[0:1, 1:2], kmg[0:1, 0:1], AF.Ln, sem="acr")
        c.wait("scalar", "acr")
        c.op("scalar", "activation", kmg[0:1, 1:2], kmg[0:1, 1:2], AF.Exp, sem="acr", scale=0.5)
        c.wait("vector", "acr")
        c.op("vector", "tensor_scalar", crow[0:1, 0:16], crow[0:1, 0:16], kmg[0:1, 1:2], -1.0,
             ALU.mult, ALU.mult, sem="acv")
        c.wait("tensor", "acv")
        c.wait("tensor", "alv", None)
        c.wait("tensor", "abv", None)
        for h in range(16):
            c.op("tensor", "matmul", c.pb[6][:, h:h + 1], onesr[0:1, :], crow[0:1, h:h + 1],
                 sem=("abm" if h == 15 else None), start=True, stop=True)
        c.wait("vector", "abm")
        c.op("vector", "tensor_copy", biasc[:, 0:16], c.pb[6][:, 0:16], sem="abv")
        c.wait("tensor", "abv")
        c.wait("tensor", "aqnr", None)
        nkc = 4 * qt + 4
        for hp in range(DC):
            hA, hB = 2 * hp, 2 * hp + 1
            c.wait("tensor", "aav", None)
            c.wait("tensor", "alv", None)
            for kc in range(nkc):
                pi = kc % 2
                sA = c.pb[0 + pi]
                sB = c.pb[2 + pi]
                if kc == 0:
                    c.wait("tensor", "arb", None)
                c.wait("tensor", "aex", c.n("aex") - 1)
                c.op("tensor", "matmul", sA[:, :],
                     Ksb[0:64, hp * T + kc * 128:hp * T + (kc + 1) * 128],
                     Qsb[0:64, hp * TT:(hp + 1) * TT], start=True, stop=True,
                     tile_position=(0, 0))
                c.op("tensor", "matmul", sB[:, :],
                     Ksb[64:128, hp * T + kc * 128:hp * T + (kc + 1) * 128],
                     Qsb[64:128, hp * TT:(hp + 1) * TT], sem="asc", start=True, stop=True,
                     tile_position=(64, 0))
                c.wait("scalar", "asc")
                c.wait("scalar", "aeu", c.n("aeu") - 1)
                eA = Ebuf[:, (pi * 2) * TT:(pi * 2 + 1) * TT]
                eB = Ebuf[:, (pi * 2 + 1) * TT:(pi * 2 + 2) * TT]
                c.op("scalar", "activation", eA, sA[:, :], AF.Exp, bias=biasc[:, hA:hA + 1])
                c.op("scalar", "activation", eB, sB[:, :], AF.Exp, bias=biasc[:, hB:hB + 1],
                     sem="aex")
                r = kc - 4 * qt
                if r >= 0:
                    c.wait("vector", "aex")
                    c.op("vector", "tensor_tensor", eA, eA, masks[:, r * 512:(r + 1) * 512], ALU.mult)
                    c.op("vector", "tensor_tensor", eB, eB, masks[:, r * 512:(r + 1) * 512], ALU.mult,
                         sem="aem")
                    c.wait("tensor", "aem")
                else:
                    c.wait("tensor", "aex")
                vcol = kc * 1024 + hp * 128
                c.op("tensor", "matmul", c.pb[4][0:64, :], Vsb[:, vcol:vcol + 64], eA,
                     start=(kc == 0), stop=(kc == nkc - 1), tile_position=(0, 0))
                c.op("tensor", "matmul", c.pb[5][64:128, :], Vsb[:, vcol + 64:vcol + 128], eB,
                     start=(kc == 0), stop=(kc == nkc - 1), tile_position=(0, 64),
                     sem=("aam" if kc == nkc - 1 else None))
                c.op("tensor", "matmul", c.pb[6][0:1, :], c.tiles["ones_colb"][:, 0:1], eA,
                     start=(kc == 0), stop=(kc == nkc - 1))
                c.op("tensor", "matmul", c.pb[7][0:1, :], c.tiles["ones_colb"][:, 0:1], eB,
                     sem="aeu", start=(kc == 0), stop=(kc == nkc - 1))
            c.wait("vector", "aeu")
            c.op("vector", "reciprocal", rA_row[0:1, :], c.pb[6][0:1, :], sem="alv")
            c.wait("vector", "alv")
            c.op("vector", "reciprocal", rB_row[0:1, :], c.pb[7][0:1, :], sem="alv")
            c.wait("tensor", "alv")
            c.wait("tensor", "aex", None)
            c.op("tensor", "matmul", c.pb[2][0:64, :], onesr[0:1, 0:64], rA_row[0:1, :],
                 start=True, stop=True, tile_position=(0, 0))
            c.op("tensor", "matmul", c.pb[2][64:128, :], onesr[0:1, 0:64], rB_row[0:1, :],
                 sem="arm", start=True, stop=True, tile_position=(0, 64))
            c.wait("vector", "arm")
            c.op("vector", "tensor_copy", rbc[:, :], c.pb[2][:, :], sem="arb")
            c.wait("vector", "arb")
            c.op("vector", "tensor_tensor", att[0:64, hp * TT:(hp + 1) * TT],
                 c.pb[4][0:64, :], rbc[0:64, :], ALU.mult)
            c.op("vector", "tensor_tensor", att[64:128, hp * TT:(hp + 1) * TT],
                 c.pb[5][64:128, :], rbc[64:128, :], ALU.mult, sem="aav")
        c.wait("tensor", "aex", None)
        c.wait("vector", "aav")
        xc_in = chunked(P["xT"])
        yc_out = chunked(P["yT"])
        for oc in range(DC):
            pi = oc % 2
            obank = c.pb[0 + pi]
            if oc == 0:
                c.wait("sync", "aqm", None)
                c.wait("sync", "zbc", None)
            c.wait("sync", "aom", c.n("aom") - 1)
            load_wtile(c, "sync", wb[pi], P["ow"], oc, DC, f"aow{pi}")
            c.wait("sync", f"ares{pi}", None)
            c.dma("sync", rxb[:, pi * TT:(pi + 1) * TT], xc_in[:, oc, sl], f"arxl{pi}")
            c.wait("tensor", f"aow{pi}")
            if oc == 0:
                c.wait("tensor", "aav")
            c.wait("tensor", "aov", c.n("aov") - 1)
            last = (P.get("ob_sb") is None)
            for cc in range(DC):
                c.op("tensor", "matmul", obank[:, :], wb[pi][:, cc * 128:(cc + 1) * 128],
                     att[:, cc * TT:(cc + 1) * TT], start=(cc == 0),
                     stop=(cc == DC - 1 and last),
                     sem=("aom" if (cc == DC - 1 and last) else None))
            if P.get("ob_sb") is not None:
                c.op("tensor", "matmul", obank[:, :], P["ob_sb"][0:1, oc * 128:(oc + 1) * 128],
                     c.tiles["ones_rowb"][0:1, :], start=False, stop=True, sem="aom")
            c.wait("vector", "aom")
            c.wait("vector", f"arxl{pi}")
            c.op("vector", "tensor_tensor", rxb[:, pi * TT:(pi + 1) * TT],
                 rxb[:, pi * TT:(pi + 1) * TT], obank[:, :], ALU.add, sem="aov")
            c.wait("sync", "aov")
            c.dma("sync", yc_out[:, oc, sl], rxb[:, pi * TT:(pi + 1) * TT], f"ares{pi}")
    c.wait_all("ares0")
    c.wait_all("ares1")



def build_step_l0_ffn():
    nc = bass.Bass("TRN2", target_bir_lowering=False)
    xT = nc.dram_tensor("xT", [D, T], F32, kind="ExternalInput").ap()
    wg = nc.dram_tensor("wg", [3072, 1024], I32, kind="ExternalInput").ap()
    wu = nc.dram_tensor("wu", [3072, 1024], I32, kind="ExternalInput").ap()
    wd = nc.dram_tensor("wd", [1024, 3072], I32, kind="ExternalInput").ap()
    yT = nc.dram_tensor("yT", [D, T], F32, kind="ExternalOutput").ap()
    wg_bf = nc.dram_tensor("wg_bf", [3072, 1024], BF16, kind="Internal").ap()
    wu_bf = nc.dram_tensor("wu_bf", [3072, 1024], BF16, kind="Internal").ap()
    wd_bf = nc.dram_tensor("wd_bf", [1024, 3072], BF16, kind="Internal").ap()
    wgT = nc.dram_tensor("wgT", [1024, 3072], BF16, kind="Internal").ap()
    wuT = nc.dram_tensor("wuT", [1024, 3072], BF16, kind="Internal").ap()
    wdT = nc.dram_tensor("wdT", [3072, 1024], BF16, kind="Internal").ap()

    def setup(c):
        alloc_consts(nc, c)
        emit_load_consts(c)
    ems = [setup,
           build_l0_ffn_pre(nc, wg, wu, wd, wgT, wuT, wdT, wg_bf, wu_bf, wd_bf),
           build_l0_ffn(nc, xT, yT, wgT, wuT, wdT)]
    run_ctx(nc, ems)
    return nc


def build_step_l0_attn_a():
    nc = bass.Bass("TRN2", target_bir_lowering=False)
    xT = nc.dram_tensor("xT", [D, T], F32, kind="ExternalInput").ap()
    qw = nc.dram_tensor("qw", [1024, 1024], F32, kind="ExternalInput").ap()
    kw = nc.dram_tensor("kw", [1024, 1024], F32, kind="ExternalInput").ap()
    vw = nc.dram_tensor("vw", [1024, 1024], F32, kind="ExternalInput").ap()
    ow = nc.dram_tensor("ow", [1024, 1024], F32, kind="ExternalInput").ap()
    yT = nc.dram_tensor("yT", [D, T], F32, kind="ExternalOutput").ap()
    qwp = nc.dram_tensor("qwp", [1024, 1024], BF16, kind="Internal").ap()
    kwp = nc.dram_tensor("kwp", [1024, 1024], BF16, kind="Internal").ap()
    vwp = nc.dram_tensor("vwp", [1024, 1024], BF16, kind="Internal").ap()
    owp = nc.dram_tensor("owp", [1024, 1024], BF16, kind="Internal").ap()

    def setup(c):
        alloc_consts(nc, c)
        emit_load_consts(c)
        emit_permute_qkv_pre(c, qw, qwp)
        emit_permute_qkv_pre(c, kw, kwp)
        emit_permute_qkv_pre(c, vw, vwp)
        emit_gather_ow_pre(c, ow, owp)

    P = {"xT": xT, "yT": yT, "qw": qwp, "kw": kwp, "vw": vwp, "ow": owp,
         "scale": 1.0 / 32.0, "rope": None}
    ems = [setup, lambda c: emit_causal_attn(c, P)]
    run_ctx(nc, ems)
    return nc



def moe_sel_np():
    s = np.zeros((8, 8 * 128), np.float32)
    for e in range(8):
        s[e, e * 128:(e + 1) * 128] = 1.0
    return s.astype(ml_dtypes.bfloat16)


def emit_moe(c, P):
    nc = c.nc
    E = 8
    H = 2048
    FCH = H // 128
    NTP = 2
    NPS = NTT // NTP
    W = NTP * TT
    rowA = c.sb("rowA", [1, T], BF16)
    rowB = c.sb("rowB", [1, T], BF16)
    zq = c.sb("mo_z", [128, DC * W], BF16)
    xacc = c.sb("mo_xacc", [128, DC * W], F32)
    hset = c.sb("mo_h", [128, FCH * W], BF16)
    h1 = c.sb("mo_h1", [128, 2 * W], BF16)
    dtmp = c.sb("mo_dt", [128, 2 * W], F32)
    gebc = c.sb("mo_ge", [128, W], F32)
    rsb = c.sb("mo_rsb", [8, 1024], F32)
    routerT = c.sb("mo_routerT", [128, DC * 8], BF16)
    lgf = c.sb("mo_lgf", [8, W], F32)
    lgtm = c.sb("mo_lgtm", [128, 8 * 8], F32)
    gts = c.sb("mo_gts", [128, 64], F32)
    gesb = c.sb("mo_gesb", [8, W], BF16)
    selsb = c.sb("mo_sel", [8, 8 * 128], BF16)
    wgb = [c.sb(f"mo_wgb{i}", [128, DC * 128], BF16) for i in range(2)]
    wub = [c.sb(f"mo_wub{i}", [128, DC * 128], BF16) for i in range(2)]
    wdb = [c.sb(f"mo_wdb{i}", [128, FCH * 128], BF16) for i in range(2)]
    ident = c.tiles["ident"]

    emit_ln_stats(c, P["xT"], rowA, rowB)

    c.dma("sync", rsb[:, :], P["router"], "mrl")
    c.dma("sync", selsb[:, :], c.tiles["moe_sel_dram"], "mrl")
    c.wait("tensor", "mrl")
    idf = c.sb("mo_idf", [128, 128], F32)
    c.op("vector", "tensor_copy", idf[:, :], ident[:, :], sem="mid")
    c.wait("tensor", "mid")
    for cc in range(DC):
        c.wait("tensor", "mrc", None)
        c.op("tensor", "transpose", c.pb[0][:, 0:8], rsb[:, cc * 128:(cc + 1) * 128],
             idf[0:8, 0:8], sem="mrt")
        c.wait("vector", "mrt")
        c.op("vector", "tensor_copy", routerT[:, cc * 8:(cc + 1) * 8], c.pb[0][:, 0:8], sem="mrc")
    c.wait_all("mrc")

    for ps in range(NPS):
        sl = slice(ps * W, (ps + 1) * W)
        c.wait("vector", "mgum", None)
        c.wait("vector", "mdnm", None)
        c.wait("tensor", "zbc", None)
        c.wait("tensor", "msil", None)
        c.wait("tensor", "mhev", None)
        c.wait("tensor", "mgec2", None)
        for t in range(NTP):
            if t > 0:
                c.wait("tensor", "zbc", None)
            emit_z_tile_stream(c, P["xT"], rowA, rowB, ps * NTP + t, zq,
                               bankA=4, bankB=5, zstride=W, zoff=t * TT)
        c.wait("sync", "zbc", None)
        c.wait("sync", "mres", None)
        load_chunks(c, "sync", xacc, P["xT"], sl, DC, "mxl")
        c.wait("tensor", "mrc", None)
        c.wait("tensor", "mlg", None)
        c.wait("tensor", "mdnv", None)
        c.wait("tensor", "zbc")
        for t in range(NTP):
            for cc in range(DC):
                c.op("tensor", "matmul", c.pb[t][0:8, :], routerT[:, cc * 8:(cc + 1) * 8],
                     zq[:, cc * W + t * TT:cc * W + (t + 1) * TT],
                     start=(cc == 0), stop=(cc == DC - 1),
                     sem=("mlgm" if (cc == DC - 1 and t == NTP - 1) else None))
        c.wait("vector", "mlgm")
        for t in range(NTP):
            c.op("vector", "tensor_copy", lgf[:, t * TT:(t + 1) * TT], c.pb[t][0:8, :],
                 sem=("mlg" if t == NTP - 1 else None))
        c.wait("tensor", "mlg")
        c.wait("tensor", "mlgc", None)
        for g in range(NTP * 4):
            c.op("tensor", "transpose", c.pb[2][:, g * 8:(g + 1) * 8],
                 lgf[:, g * 128:(g + 1) * 128], idf[0:8, 0:8],
                 sem=("mlgt" if g == NTP * 4 - 1 else None))
        c.wait("vector", "mlgt")
        c.op("vector", "tensor_copy", lgtm[:, :], c.pb[2][:, 0:NTP * 4 * 8], sem="mlgc")
        c.wait("vector", "mlgc")
        c.wait("vector", "mgb", None)
        BIG = 1.0e30
        for gh in range(NTP):
            for g4 in range(4):
                g = gh * 4 + g4
                lg = lgtm[:, g * 8:(g + 1) * 8]
                v1 = gts[:, 0:1]; v2 = gts[:, 1:2]; eq1 = gts[:, 8:16]; eq2 = gts[:, 16:24]
                lg2 = gts[:, 24:32]; gg = gts[:, 2:4]; srs = gts[:, 4:5]
                a1 = gts[:, 5:6]; a2 = gts[:, 6:7]; tge = gts[:, 32:40]
                c.wait("vector", "mgg", None)
                c.wait("vector", "mget", None)
                c.op("vector", "reduce_max", v1, lg, AX.X, sem="mgg")
                c.wait("vector", "mgg")
                c.op("vector", "tensor_scalar", eq1, lg, v1, None, ALU.is_equal, sem="mgg")
                c.wait("vector", "mgg")
                c.op("vector", "scalar_tensor_tensor", lg2, eq1, -BIG, lg, ALU.mult, ALU.add, sem="mgg")
                c.wait("vector", "mgg")
                c.op("vector", "reduce_max", v2, lg2, AX.X, sem="mgg")
                c.wait("vector", "mgg")
                c.op("vector", "tensor_scalar", eq2, lg, v2, None, ALU.is_equal, sem="mgg")
                c.wait("scalar", "mgg")
                c.op("scalar", "activation", gg, gts[:, 0:2], AF.Sigmoid, sem="mgs")
                c.wait("vector", "mgs")
                c.op("vector", "reduce_sum", srs, gg, AX.X, sem="mgg")
                c.wait("vector", "mgg")
                c.op("vector", "tensor_scalar", srs, srs, 1.0e-10, None, ALU.add, sem="mgg")
                c.wait("vector", "mgg")
                c.op("vector", "reciprocal", srs, srs, sem="mgg")
                c.wait("vector", "mgg")
                c.op("vector", "tensor_scalar", a1, gg[:, 0:1], srs, None, ALU.mult)
                c.op("vector", "tensor_scalar", a2, gg[:, 1:2], srs, None, ALU.mult, sem="mgg")
                c.wait("vector", "mgg")
                c.op("vector", "tensor_scalar", tge, eq1, a1, None, ALU.mult, sem="mgg")
                c.wait("vector", "mgg")
                c.op("vector", "scalar_tensor_tensor", gts[:, 40:48], eq2, a2, tge,
                     ALU.mult, ALU.add, sem="mgg")
                c.wait("tensor", "mgg")
                c.op("tensor", "transpose", c.pb[6 + gh][0:8, g4 * 128:(g4 + 1) * 128],
                     gts[:, 40:48], idf[:, :], sem="mget")
            c.wait("vector", "mget")
            c.op("vector", "tensor_copy", gesb[:, gh * TT:(gh + 1) * TT],
                 c.pb[6 + gh][0:8, 0:512], sem="mgec")
        for e in range(E):
            sil_e = c.n("msil")
            hev_e = c.n("mhev")
            gum_e = c.n("mgum")
            for fc in range(FCH):
                pi = fc % 2
                gbk = (c.pb[0 + 4 * pi], c.pb[1 + 4 * pi])
                ubk = (c.pb[2 + 4 * pi], c.pb[3 + 4 * pi])
                c.wait("sync", "mgum", c.n("mgum") - 1)
                load_wtile(c, "sync", wgb[pi], P["wg"], e * FCH + fc, DC, f"mwg{pi}")
                load_wtile(c, "sync", wub[pi], P["wu"], e * FCH + fc, DC, f"mwu{pi}")
                c.wait("tensor", f"mwg{pi}")
                c.wait("tensor", f"mwu{pi}")
                if fc == 0:
                    c.wait("tensor", "mdnv", None)
                    if e == 0:
                        c.wait("tensor", "mlg")
                        c.wait("tensor", "mlgc")
                elif fc == 1:
                    c.wait("tensor", "msil", sil_e)
                    c.wait("tensor", "mhev", hev_e)
                    c.wait("tensor", "mgec2", c.n("mgec2"))
                    if e == 0:
                        c.wait("tensor", "zbc", None)
                        c.wait("tensor", "mgec")
                else:
                    c.wait("tensor", "msil", sil_e + fc - 1)
                    c.wait("tensor", "mhev", hev_e + 2 * fc - 2)
                for t in range(NTP):
                    for cc in range(DC):
                        c.op("tensor", "matmul", gbk[t][:, :], wgb[pi][:, cc * 128:(cc + 1) * 128],
                             zq[:, cc * W + t * TT:cc * W + (t + 1) * TT],
                             start=(cc == 0), stop=(cc == DC - 1))
                for t in range(NTP):
                    for cc in range(DC):
                        c.op("tensor", "matmul", ubk[t][:, :], wub[pi][:, cc * 128:(cc + 1) * 128],
                             zq[:, cc * W + t * TT:cc * W + (t + 1) * TT],
                             sem=("mgum" if (cc == DC - 1 and t == NTP - 1) else None),
                             start=(cc == 0), stop=(cc == DC - 1))
                c.wait("scalar", "mgum", gum_e + fc + 1)
                c.wait("scalar", "mhev", hev_e + 2 * fc - 2)
                for t in range(NTP):
                    c.op("scalar", "activation",
                         h1[:, (pi * NTP + t) * TT:(pi * NTP + t + 1) * TT],
                         gbk[t][:, :], AF.Silu, sem=("msil" if t == NTP - 1 else None))
                c.wait("vector", "msil", sil_e + fc + 1)
                for t in range(NTP):
                    c.op("vector", "tensor_tensor",
                         hset[:, fc * W + t * TT:fc * W + (t + 1) * TT], ubk[t][:, :],
                         h1[:, (pi * NTP + t) * TT:(pi * NTP + t + 1) * TT],
                         ALU.mult, sem="mhev")
            c.wait("tensor", "mgec")
            c.wait("tensor", "mhev", None)
            for t in range(NTP):
                c.op("tensor", "matmul", c.pb[6 + t][:, :], selsb[:, e * 128:(e + 1) * 128],
                     gesb[:, t * TT:(t + 1) * TT], start=True, stop=True,
                     sem=("mgb" if t == NTP - 1 else None))
            c.wait("vector", "mgb")
            for t in range(NTP):
                c.op("vector", "tensor_copy", gebc[:, t * TT:(t + 1) * TT], c.pb[6 + t][:, :],
                     sem=("mgec2" if t == NTP - 1 else None))
            dn0 = c.n("mdnm")
            dv0 = c.n("mdnv")
            for dc in range(DC):
                pj = dc % 2
                dbk = (c.pb[0 + 2 * pj], c.pb[1 + 2 * pj])
                c.wait("sync", "mdnm", c.n("mdnm") - 1)
                load_wtile(c, "sync", wdb[pj], P["wd"], e * DC + dc, FCH, f"mwd{pj}")
                c.wait("tensor", f"mwd{pj}")
                if dc <= 1:
                    c.wait("tensor", "msil", None)
                    c.wait("tensor", "mhev", None)
                else:
                    c.wait("tensor", "mdnv", dv0 + 2 * dc - 2)
                for t in range(NTP):
                    for hc in range(FCH):
                        c.op("tensor", "matmul", dbk[t][:, :], wdb[pj][:, hc * 128:(hc + 1) * 128],
                             hset[:, hc * W + t * TT:hc * W + (t + 1) * TT],
                             sem=("mdnm" if (hc == FCH - 1 and t == NTP - 1) else None),
                             start=(hc == 0), stop=(hc == FCH - 1))
                c.wait("vector", "mdnm", dn0 + dc + 1)
                if dc == 0:
                    c.wait("vector", "mgec2")
                if e == 0 and dc == 0:
                    c.wait("vector", "mxl")
                for t in range(NTP):
                    c.op("vector", "tensor_tensor", dtmp[:, (pj * NTP + t) * TT:(pj * NTP + t + 1) * TT],
                         dbk[t][:, :], gebc[:, t * TT:(t + 1) * TT], ALU.mult,
                         sem=("mdge" if t == NTP - 1 else None))
                c.wait("vector", "mdge")
                for t in range(NTP):
                    c.op("vector", "tensor_tensor", xacc[:, dc * W + t * TT:dc * W + (t + 1) * TT],
                         xacc[:, dc * W + t * TT:dc * W + (t + 1) * TT],
                         dtmp[:, (pj * NTP + t) * TT:(pj * NTP + t + 1) * TT],
                         ALU.add, sem="mdnv")
        c.wait("sync", "mdnv", None)
        store_chunks(c, "sync", P["yT"], sl, xacc, DC, "mres")
    c.wait_all("mres")


def rope_table_np():
    half = 32
    inv_freq = 1.0 / 10000.0 ** (np.arange(half, dtype=np.float64) / half)
    pos = np.arange(T, dtype=np.float64)
    ang = np.outer(inv_freq, pos)
    C = np.tile(np.cos(ang), (4, 1)).astype(np.float32)
    S = np.tile(np.sin(ang), (4, 1)).astype(np.float32)
    return np.stack([C, S]).astype(ml_dtypes.bfloat16)


def rope_swap_np():
    P2 = np.zeros((128, 128), np.float32)
    for hb in (0, 64):
        for j in range(32):
            P2[hb + 32 + j, hb + j] = -1.0
            P2[hb + j, hb + 32 + j] = 1.0
    return P2.astype(ml_dtypes.bfloat16)


def build_step_l0_attn_b():
    nc = bass.Bass("TRN2", target_bir_lowering=False)
    xT = nc.dram_tensor("xT", [D, T], F32, kind="ExternalInput").ap()
    qkv_w = nc.dram_tensor("qkv_w", [3072, 1024], F32, kind="ExternalInput").ap()
    qkv_b = nc.dram_tensor("qkv_b", [3072], F32, kind="ExternalInput").ap()
    out_w = nc.dram_tensor("out_w", [1024, 1024], F32, kind="ExternalInput").ap()
    out_b = nc.dram_tensor("out_b", [1024], F32, kind="ExternalInput").ap()
    yT = nc.dram_tensor("yT", [D, T], F32, kind="ExternalOutput").ap()
    qkv_bf = nc.dram_tensor("qkv_bf", [3072, 1024], BF16, kind="Internal").ap()
    qkvT = nc.dram_tensor("qkvT", [1024, 3072], BF16, kind="Internal").ap()
    ow_bf = nc.dram_tensor("ow_bf", [1024, 1024], BF16, kind="Internal").ap()
    owT = nc.dram_tensor("owT", [1024, 1024], BF16, kind="Internal").ap()
    ropetab = nc.inline_tensor(rope_table_np(), name="ropetab").ap()
    ropeP2 = nc.inline_tensor(rope_swap_np(), name="ropeP2").ap()

    def setup(c):
        alloc_consts(nc, c)
        c.tiles["ropeP2"] = c.sb("ropeP2_sb", [128, 128], BF16)
        c.dma("sync", c.tiles["ropeP2"][:], ropeP2, "cst")
        c.wait_all("cst")
        emit_load_consts(c)
        emit_cast_dram(c, qkv_w, qkv_bf)
        emit_cast_dram(c, out_w, ow_bf)
        bias_row = c.sb("ab_bias", [1, 4096], BF16)
        c.dma("gpsimd", bias_row[0:1, 0:3072], qkv_b.rearrange("(o n) -> o n", o=1), "pcs")
        c.dma("gpsimd", bias_row[0:1, 3072:4096], out_b.rearrange("(o n) -> o n", o=1), "pcs")
        c.wait_all("pcs")
        emit_transpose_scratch(c, qkv_bf, qkvT, 3072, 1024)
        emit_transpose_scratch(c, ow_bf, owT, 1024, 1024)

    def main(c):
        bias_row = c.sb("ab_bias", [1, 4096], BF16)
        P = {"xT": xT, "yT": yT,
             "qw": qkvT[:, 0:1024], "kw": qkvT[:, 1024:2048], "vw": qkvT[:, 2048:3072],
             "ow": owT,
             "qb_sb": bias_row[:, 0:1024], "kb_sb": bias_row[:, 1024:2048],
             "vb_sb": bias_row[:, 2048:3072], "ob_sb": bias_row[:, 3072:4096],
             "scale": 1.0 / 8.0, "rope": ropetab}
        emit_causal_attn(c, P)

    run_ctx(nc, [setup, main])
    return nc


def build_step_l1_ffn():
    nc = bass.Bass("TRN2", target_bir_lowering=False)
    xT = nc.dram_tensor("xT", [D, T], F32, kind="ExternalInput").ap()
    router = nc.dram_tensor("router", [8, 1024], F32, kind="ExternalInput").ap()
    wg = nc.dram_tensor("wg", [8, 1024, 2048], F32, kind="ExternalInput").ap()
    wu = nc.dram_tensor("wu", [8, 1024, 2048], F32, kind="ExternalInput").ap()
    wd = nc.dram_tensor("wd", [8, 2048, 1024], F32, kind="ExternalInput").ap()
    yT = nc.dram_tensor("yT", [D, T], F32, kind="ExternalOutput").ap()
    wg_bf = nc.dram_tensor("wg_bf", [128, 128, 1024], BF16, kind="Internal").ap()
    wu_bf = nc.dram_tensor("wu_bf", [128, 128, 1024], BF16, kind="Internal").ap()
    wd_bf = nc.dram_tensor("wd_bf", [64, 128, 2048], BF16, kind="Internal").ap()
    sel = nc.inline_tensor(moe_sel_np(), name="moe_sel").ap()

    def setup(c):
        alloc_consts(nc, c)
        c.tiles["moe_sel_dram"] = sel
        emit_load_consts(c)
        for wnm, src in [(wg_bf, wg), (wu_bf, wu), (wd_bf, wd)]:
            srcv = src.rearrange("e (cc p) (fc w) -> e fc p cc w", p=128, w=128)
            nfc = srcv.shape[1]
            for e in range(8):
                for fc in range(nfc):
                    c.dma("gpsimd", wnm[e * nfc + fc].rearrange("p (cc w) -> p cc w", w=128),
                          srcv[e, fc], "pcs")
        c.wait_all("pcs")

    P = {"xT": xT, "yT": yT, "router": router, "wg": wg_bf, "wu": wu_bf, "wd": wd_bf}
    run_ctx(nc, [setup, lambda c: emit_moe(c, P)])
    return nc



def emit_perf(c, P):
    nc = c.nc
    H = 16
    M = 256
    FFC = 32
    scale = float(64) ** (-0.25)
    LN16 = float(np.log(16.0))
    rowA = c.sb("rowA", [1, T], BF16)
    rowB = c.sb("rowB", [1, T], BF16)
    zq = c.sb("pf_z", [128, DC * TT], BF16)
    qfm = c.sb("pf_qfm", [128, DC * TT], BF16)
    qtm = c.sb("pf_qtm", [128, 4 * 1024], BF16)
    vext = c.sb("pf_vext", [128, 4 * 1024], BF16)
    sqc = c.sb("pf_sqc", [128, 4 * 16], F32)
    pq = c.sb("pf_pq", [128, 256], BF16)
    kv = c.sb("pf_kv", [128, 2048], BF16)
    zb = c.sb("pf_zb", [1, 4096], F32)
    zcol = c.sb("pf_zcol", [128, 32], BF16)
    att = c.sb("pf_att", [128, DC * TT], BF16)
    vbuf = c.sb("pf_v", [128, DC * TT], F32)
    ubuf = c.sb("pf_u", [128, DC * TT], BF16)
    hh = c.sb("pf_hh", [128, FFC * TT], BF16)
    omt = c.sb("pf_omt", [128, 256], BF16)
    bcol = c.sb("pf_bcol", [128, 4], F32)
    riA = c.sb("pf_riA", [1, TT], F32)
    riB = c.sb("pf_riB", [1, TT], F32)
    gvec = c.sb("pf_gvec", [128, 4 * DC], F32)
    b1v = c.sb("pf_b1v", [128, FFC], F32)
    b2r = c.sb("pf_b2r", [1, 1024], BF16)
    wb = [c.sb(f"pf_wb{i}", [128, DC * 128], BF16) for i in range(2)]
    w2b = [c.sb(f"pf_w2b{i}", [128, FFC * 128], BF16) for i in range(2)]
    vwb = c.sb("pf_vwb", [128, DC * 512], BF16)
    rxb = c.sb("ln_xb", [128, 2 * TT], F32)
    onesr = c.tiles["ones_row"]
    r1 = c.sb("ln_r1", [1, TT], F32)
    r2 = c.sb("ln_r2", [1, TT], F32)
    t1s = c.sb("z_t1", [128, TT], F32)

    emit_ln_stats(c, P["xT"], rowA, rowB)

    c.dma("gpsimd", b2r[0:1, :], P["b2"].rearrange("(o n) -> o n", o=1), "pol")
    emit_vec_rows(c, P["ln1_g"], gvec[:, 0 * DC:1 * DC], 1024)
    emit_vec_rows(c, P["ln1_b"], gvec[:, 1 * DC:2 * DC], 1024)
    emit_vec_rows(c, P["ln2_g"], gvec[:, 2 * DC:3 * DC], 1024)
    emit_vec_rows(c, P["ln2_b"], gvec[:, 3 * DC:4 * DC], 1024)
    emit_vec_rows(c, P["b1"], b1v[:, 0:FFC], 4096)
    c.wait_all("pol")

    def fm_proj(w_ap, out_sb, s, chainsem, evsem):
        for fc in range(DC):
            pi = fc % 2
            bank = c.pb[4 + pi]
            c.wait("sync", chainsem, c.n(chainsem) - 1)
            load_wtile(c, "sync", wb[pi], w_ap, fc, DC, f"pkw{pi}")
            c.wait("tensor", f"pkw{pi}")
            if fc == 0:
                c.wait("tensor", "zbc")
            c.wait("tensor", evsem, c.n(evsem) - 1)
            for cc in range(DC):
                c.op("tensor", "matmul", bank[:, :], wb[pi][:, cc * 128:(cc + 1) * 128],
                     zq[:, cc * TT:(cc + 1) * TT], start=(cc == 0), stop=(cc == DC - 1),
                     sem=(chainsem if cc == DC - 1 else None))
            c.wait("scalar", chainsem)
            c.op("scalar", "activation", out_sb[:, fc * TT:(fc + 1) * TT], bank[:, :],
                 AF.Copy, sem=evsem, scale=s)

    def tm_proj(w_ap, out_sb, s):
        c.wait("tensor", "pkev", None)
        for half in range(2):
            c.wait("sync", "pvm", None)
            load_vtile(c, "sync", vwb, w_ap, half * 4, 4, DC, "pvw")
            c.wait("tensor", "pvw")
            for kb in range(4):
                bank = c.pb[4 + kb % 2]
                c.wait("tensor", "pvev", c.n("pvev") - 1)
                for cc in range(DC):
                    c.op("tensor", "matmul", bank[:, :],
                         zq[:, cc * TT + kb * 128:cc * TT + (kb + 1) * 128],
                         vwb[:, cc * 512:(cc + 1) * 512],
                         start=(cc == 0), stop=(cc == DC - 1),
                         sem=("pvm" if cc == DC - 1 else None))
                c.wait("scalar", "pvm")
                c.op("scalar", "activation", out_sb[:, kb * 1024 + half * 512:
                                                    kb * 1024 + (half + 1) * 512],
                     bank[:, :], AF.Copy, sem="pvev", scale=s)

    def sq_cols(src_sb, scr_off):
        for kb in range(4):
            c.wait("scalar", "phsq", None)
            c.wait("scalar", "psqr", None)
            c.wait("scalar", "pwom", None)
            c.op("scalar", "activation", att[:, scr_off:scr_off + 1024],
                 src_sb[:, kb * 1024:(kb + 1) * 1024], AF.Square, sem="phsq")
            c.wait("vector", "phsq")
            for h in range(H):
                c.op("vector", "reduce_sum", sqc[:, kb * 16 + h:kb * 16 + h + 1],
                     att[:, scr_off + h * 64:scr_off + (h + 1) * 64], AX.X,
                     sem=("psqr" if h == H - 1 else None))
            c.wait("vector", "psqr")
            c.op("vector", "tensor_scalar", sqc[:, kb * 16:(kb + 1) * 16],
                 sqc[:, kb * 16:(kb + 1) * 16], 0.5, LN16, ALU.mult, ALU.add, sem="psqs")

    def phi_head(kb, h, consume):
        hc = h // 2
        hb = (h % 2) * 64
        c.wait("tensor", "pex", None)
        c.wait("tensor", "pzv", None)
        c.wait("tensor", "pkev", None)
        c.wait("tensor", "ptqc", None)
        c.op("tensor", "matmul", c.pb[6][:, 0:256],
             qfm[hb:hb + 64, hc * TT + kb * 128:hc * TT + (kb + 1) * 128],
             omt[hb:hb + 64, :], start=True, stop=True, sem="ppj")
        c.wait("vector", "ppj")
        c.op("vector", "reduce_max", bcol[:, 0:1], c.pb[6][:, 0:256], AX.X, sem="pst")
        c.wait("vector", "pst")
        c.wait("vector", "psqs", None)
        c.op("vector", "tensor_tensor", bcol[:, 1:2], bcol[:, 0:1],
             sqc[:, kb * 16 + h:kb * 16 + h + 1], ALU.add, sem="pst")
        c.wait("vector", "pst")
        c.op("vector", "tensor_scalar", bcol[:, 2:3], bcol[:, 1:2], -1.0, None,
             ALU.mult, sem="pst")
        c.wait("scalar", "pst")
        c.wait("scalar", "ppk", None)
        c.op("scalar", "activation", pq[:, :], c.pb[6][:, 0:256], AF.Exp,
             bias=bcol[:, 2:3], sem="pex")
        c.wait("vector", "pex")
        c.op("vector", "tensor_scalar", pq[:, :], pq[:, :], 1.0e-6, None,
             ALU.add, sem="ppk")
        consume(kb, h)

    c.op("vector", "memset", zb[:, :], 0.0, sem="pzb")

    def consume_k(kb, h):
        c.wait("tensor", "ppk")
        first = (c._tt == 0 and kb == 0)
        lastk = (c._tt == NTT - 1 and kb == 3)
        for mc in range(2):
            slot = mc * 16 + h
            bk = c.pb[slot // 8]
            off = (slot % 8) * 64
            c.op("tensor", "matmul", bk[:, off:off + 64],
                 pq[:, mc * 128:(mc + 1) * 128], vext[:, kb * 1024 + h * 64:kb * 1024 + (h + 1) * 64],
                 start=(first and slot % 8 == 0), stop=lastk, skip_group_check=True)
        c.op("tensor", "matmul", c.pb[6][0:1, 256:512], c.tiles["ones_colb"][:, 0:1],
             pq[:, :], start=True, stop=True, sem="pzm")
        c.wait("vector", "pzm")
        c.wait("vector", "pzb", None)
        c.op("vector", "tensor_tensor", zb[0:1, h * 256 + 0:h * 256 + 256],
             zb[0:1, h * 256 + 0:h * 256 + 256], c.pb[6][0:1, 256:512], ALU.add, sem="pzv")

    for tt in range(NTT):
        c._tt = tt
        c.wait("vector", "pchain", None)
        c.wait("vector", "pvm", None)
        c.wait("tensor", "zbc", None)
        c.wait("tensor", "pvev", None)
        c.wait("tensor", "pkev", None)
        emit_z_tile_stream(c, P["xT"], rowA, rowB, tt, zq, bankA=4, bankB=5)
        fm_proj(P["wk"], qfm, scale, "pchain", "pkev")
        tm_proj(P["wk"], qtm, scale)
        tm_proj(P["wv"], vext, 1.0)
        sq_cols(qtm, 0)
        for kb in range(4):
            for h in range(H):
                phi_head(kb, h, consume_k)
    c.wait_all("pzv")
    c.wait_all("ppk")
    c.wait("vector", "pex", None)
    for b in range(4):
        c.op("vector", "tensor_copy", kv[:, b * 512:(b + 1) * 512], c.pb[b][:, :], sem="pkvc")
    c.wait_all("pkvc")
    c.wait("tensor", "pkvc")
    c.wait("tensor", "pzv", None)
    for mc in range(2):
        for h in range(16):
            c.op("tensor", "matmul", c.pb[0][:, mc * 16 + h:mc * 16 + h + 1],
                 zb[0:1, h * 256 + mc * 128:h * 256 + (mc + 1) * 128],
                 onesr[0:1, 0:1], start=True, stop=True,
                 sem=("pzc" if (mc == 1 and h == 15) else None))
    c.wait("vector", "pzc")
    c.op("vector", "tensor_copy", zcol[:, 0:32], c.pb[0][:, 0:32], sem="pzcc")
    c.wait_all("pzcc")

    def consume_q(kb, h):
        c.wait("tensor", "ppk")
        c.wait("tensor", "ptqc", None)
        for mc in range(2):
            c.op("tensor", "transpose", c.pbt[:, mc * 128:(mc + 1) * 128],
                 pq[:, mc * 128:(mc + 1) * 128], c.tiles["ident"][:, :],
                 sem=("ptq" if mc == 1 else None))
        c.wait("vector", "ptq")
        for mc in range(2):
            c.op("vector", "tensor_copy",
                 hh[:, h * 1024 + mc * TT + kb * 128:h * 1024 + mc * TT + (kb + 1) * 128],
                 c.pbt[:, mc * 128:(mc + 1) * 128],
                 sem=("ptqc" if mc == 1 else None))

    for qt in range(NTT):
        c._tt = qt
        sl = slice(qt * TT, (qt + 1) * TT)
        c.wait("vector", "pchain", None)
        c.wait("vector", "pvm", None)
        c.wait("tensor", "zbc", None)
        c.wait("tensor", "presv", None)
        c.wait("tensor", "pvev", None)
        c.wait("tensor", "pkev", None)
        c.wait("tensor", "pgel", None)
        c.wait("sync", "pres0", None)
        c.wait("sync", "pres1", None)
        c.wait("sync", "pmlm", None)
        emit_z_tile_stream(c, P["xT"], rowA, rowB, qt, zq, bankA=4, bankB=5)
        fm_proj(P["wq"], qfm, scale, "pchain", "pkev")
        tm_proj(P["wq"], qtm, scale)
        sq_cols(qtm, 0)
        c.wait("vector", "pw2m", None)
        for kb in range(4):
            for h in range(H):
                phi_head(kb, h, consume_q)
        c.wait("tensor", "ptqc")
        c.wait("tensor", "pvev", None)
        c.wait("tensor", "pkev", None)
        for h in range(H):
            hb = (h % 2) * 64
            hc = h // 2
            c.wait("tensor", "patt", c.n("patt") - 1)
            for mc in range(2):
                c.op("tensor", "matmul", c.pb[6][hb:hb + 64, :],
                     kv[:, (mc * 16 + h) * 64:(mc * 16 + h + 1) * 64],
                     hh[:, h * 1024 + mc * TT:h * 1024 + (mc + 1) * TT],
                     start=(mc == 0), stop=(mc == 1), tile_position=(0, hb),
                     skip_group_check=True)
            for mc in range(2):
                c.op("tensor", "matmul", c.pb[5][0:1, :],
                     zcol[:, mc * 16 + h:mc * 16 + h + 1],
                     hh[:, h * 1024 + mc * TT:h * 1024 + (mc + 1) * TT],
                     start=(mc == 0), stop=(mc == 1),
                     sem=("pden" if mc == 1 else None))
            c.wait("vector", "pden")
            c.op("vector", "reciprocal", r1[0:1, :], c.pb[5][0:1, :], sem="pdr")
            c.wait("tensor", "pdr")
            c.op("tensor", "matmul", c.pb[4][hb:hb + 64, :], onesr[0:1, 0:64],
                 r1[0:1, :], start=True, stop=True, tile_position=(0, hb), sem="pdb")
            c.wait("scalar", "pdb")
            c.op("scalar", "activation", att[hb:hb + 64, hc * TT:(hc + 1) * TT],
                 c.pb[6][hb:hb + 64, :], AF.Copy, sem="pnev")
            c.wait("vector", "pnev")
            c.op("vector", "tensor_tensor", att[hb:hb + 64, hc * TT:(hc + 1) * TT],
                 att[hb:hb + 64, hc * TT:(hc + 1) * TT], c.pb[4][hb:hb + 64, :],
                 ALU.mult, sem="patt")
        c.wait("tensor", "patt", None)
        c.wait("sync", "pchain", None)
        c.wait("sync", "pvm", None)
        for oc in range(DC):
            pi = oc % 2
            bank = c.pb[4 + pi]
            c.wait("sync", "pwom", c.n("pwom") - 1)
            load_wtile(c, "sync", wb[pi], P["wo"], oc, DC, f"pkw{pi}")
            c.wait("tensor", f"pkw{pi}")
            c.wait("tensor", "pvoev", c.n("pvoev") - 1)
            for cc in range(DC):
                c.op("tensor", "matmul", bank[:, :], wb[pi][:, cc * 128:(cc + 1) * 128],
                     att[:, cc * TT:(cc + 1) * TT], start=(cc == 0), stop=(cc == DC - 1),
                     sem=("pwom" if cc == DC - 1 else None))
            c.wait("vector", "pwom")
            c.op("vector", "tensor_tensor", vbuf[:, oc * TT:(oc + 1) * TT],
                 zq[:, oc * TT:(oc + 1) * TT], bank[:, :], ALU.add, sem="pvoev")

        def inner_ln(src, dst, gcol0, bcol0, ssem, rdy):
            c.wait("scalar", rdy, None)
            c.wait("tensor", rdy, None)
            c.wait("vector", rdy, None)
            c.wait("tensor", ssem + "b", None)
            c.wait("tensor", "pl1v", None)
            c.wait("tensor", "pl2v", None)
            for cc in range(DC):
                c.wait("scalar", ssem + "q", None)
                c.wait("scalar", ssem + "s", None)
                c.op("scalar", "activation", t1s[:, :], src[:, cc * TT:(cc + 1) * TT],
                     AF.Square, sem=ssem + "s")
                c.wait("tensor", ssem + "s")
                c.op("tensor", "matmul", c.pb[0][0:1, :], c.tiles["ones_col"][:, 0:1],
                     src[:, cc * TT:(cc + 1) * TT], start=(cc == 0), stop=(cc == DC - 1),
                     sem=(ssem + "m" if cc == DC - 1 else None))
                c.op("tensor", "matmul", c.pb[1][0:1, :], c.tiles["ones_col"][:, 0:1],
                     t1s[:, :], start=(cc == 0), stop=(cc == DC - 1),
                     sem=ssem + "q")
            c.wait("vector", ssem + "m")
            c.wait("vector", ssem + "q")
            inv = 1.0 / D
            c.op("vector", "tensor_scalar", riA[0:1, :], c.pb[0][0:1, :], inv, None, ALU.mult)
            c.op("vector", "tensor_scalar", riB[0:1, :], c.pb[1][0:1, :], inv, None,
                 ALU.mult, sem=ssem + "v")
            c.wait("vector", ssem + "v")
            c.op("vector", "tensor_tensor", r2[0:1, :], riA[0:1, :], riA[0:1, :],
                 ALU.mult, sem=ssem + "v")
            c.wait("vector", ssem + "v")
            c.op("vector", "tensor_tensor", riB[0:1, :], riB[0:1, :], r2[0:1, :],
                 ALU.subtract, sem=ssem + "v")
            c.wait("vector", ssem + "v")
            c.op("vector", "tensor_scalar", riB[0:1, :], riB[0:1, :], 1.0e-6, None,
                 ALU.add, sem=ssem + "v")
            c.wait("vector", ssem + "v")
            c.op("vector", "reciprocal", riB[0:1, :], riB[0:1, :], sem=ssem + "v")
            c.wait("scalar", ssem + "v")
            c.op("scalar", "activation", r2[0:1, :], riB[0:1, :], AF.Sqrt, sem=ssem + "sq")
            c.wait("vector", ssem + "sq")
            c.op("vector", "tensor_tensor", riB[0:1, :], riA[0:1, :], r2[0:1, :],
                 ALU.mult, sem=ssem + "v")
            c.wait("vector", ssem + "v")
            c.op("vector", "tensor_scalar", riB[0:1, :], riB[0:1, :], -1.0, None,
                 ALU.mult, sem=ssem + "v")
            c.wait("tensor", ssem + "v")
            c.wait("tensor", "pv2ev", None)
            c.wait("tensor", "pl1b", None)
            c.wait("tensor", "pl2b", None)
            c.op("tensor", "matmul", c.pb[2][:, :], onesr[0:1, :], r2[0:1, :],
                 start=True, stop=True)
            c.op("tensor", "matmul", c.pb[3][:, :], onesr[0:1, :], riB[0:1, :],
                 start=True, stop=True, sem=ssem + "bm")
            c.wait("vector", ssem + "bm")
            for cc in range(DC):
                c.wait("vector", ssem + "b")
                c.op("vector", "tensor_tensor", t1s[:, :], src[:, cc * TT:(cc + 1) * TT],
                     c.pb[2][:, :], ALU.mult, sem=ssem + "b")
                c.wait("vector", ssem + "b")
                c.op("vector", "tensor_tensor", t1s[:, :], t1s[:, :], c.pb[3][:, :],
                     ALU.add, sem=ssem + "b")
                c.wait("vector", ssem + "b")
                c.op("vector", "tensor_scalar", dst[:, cc * TT:(cc + 1) * TT], t1s[:, :],
                     gvec[:, gcol0 + cc:gcol0 + cc + 1], gvec[:, bcol0 + cc:bcol0 + cc + 1],
                     ALU.mult, ALU.add, sem=ssem + "b")

        inner_ln(vbuf, ubuf, 0, DC, "pl1", "pvoev")
        c.wait("tensor", "pvoev", None)
        c.wait("sync", "pwom", None)
        for ffc in range(FFC):
            pi = ffc % 2
            bank = c.pb[4 + pi]
            c.wait("sync", "pmlm", c.n("pmlm") - 1)
            load_wtile(c, "sync", wb[pi], P["w1"], ffc, DC, f"pkw{pi}")
            c.wait("tensor", f"pkw{pi}")
            if ffc == 0:
                c.wait("tensor", "pl1b", None)
            c.wait("tensor", "pgel", c.n("pgel") - 1)
            for cc in range(DC):
                c.op("tensor", "matmul", bank[:, :], wb[pi][:, cc * 128:(cc + 1) * 128],
                     ubuf[:, cc * TT:(cc + 1) * TT], start=(cc == 0), stop=(cc == DC - 1),
                     sem=("pmlm" if cc == DC - 1 else None))
            c.wait("scalar", "pmlm")
            c.op("scalar", "activation", hh[:, ffc * TT:(ffc + 1) * TT], bank[:, :],
                 AF.Gelu, bias=b1v[:, ffc:ffc + 1], sem="pgel")
        c.wait("tensor", "pgel", None)
        for oc in range(DC):
            pi = oc % 2
            bank = c.pb[2 + pi]
            c.wait("sync", "pw2m", c.n("pw2m") - 1)
            load_wtile(c, "sync", w2b[pi], P["w2"], oc, FFC, f"pw2w{pi}")
            c.wait("tensor", f"pw2w{pi}")
            c.wait("tensor", "pv2ev", c.n("pv2ev") - 1)
            if oc <= 1:
                c.wait("tensor", "pl1b", None)
            for cc in range(FFC):
                c.op("tensor", "matmul", bank[:, :], w2b[pi][:, cc * 128:(cc + 1) * 128],
                     hh[:, cc * TT:(cc + 1) * TT], start=(cc == 0), stop=False)
            c.op("tensor", "matmul", bank[:, :], b2r[0:1, oc * 128:(oc + 1) * 128],
                 c.tiles["ones_rowb"][0:1, :], start=False, stop=True, sem="pw2m")
            c.wait("vector", "pw2m")
            c.op("vector", "tensor_tensor", vbuf[:, oc * TT:(oc + 1) * TT],
                 ubuf[:, oc * TT:(oc + 1) * TT], bank[:, :], ALU.add, sem="pv2ev")
        c.wait("vector", "pv2ev", None)
        inner_ln(vbuf, ubuf, 2 * DC, 3 * DC, "pl2", "pv2ev")
        xc_in = chunked(P["xT"])
        yc_out = chunked(P["yT"])
        c.wait("sync", "zbc", None)
        for oc in range(DC):
            pi = oc % 2
            c.wait("sync", f"pres{pi}", None)
            c.dma("sync", rxb[:, pi * TT:(pi + 1) * TT], xc_in[:, oc, sl], f"prxl{pi}")
            c.wait("vector", f"prxl{pi}")
            c.wait("vector", "pl2b", None)
            c.op("vector", "tensor_tensor", rxb[:, pi * TT:(pi + 1) * TT],
                 rxb[:, pi * TT:(pi + 1) * TT], ubuf[:, oc * TT:(oc + 1) * TT],
                 ALU.add, sem="presv")
            c.wait("sync", "presv")
            c.dma("sync", yc_out[:, oc, sl], rxb[:, pi * TT:(pi + 1) * TT], f"pres{pi}")
    c.wait_all("pres0")
    c.wait_all("pres1")


def build_step_l1_attn_a():
    nc = bass.Bass("TRN2", target_bir_lowering=False)
    xT = nc.dram_tensor("xT", [D, T], F32, kind="ExternalInput").ap()
    wq = nc.dram_tensor("wq", [1024, 16, 64], F32, kind="ExternalInput").ap()
    wk = nc.dram_tensor("wk", [1024, 16, 64], F32, kind="ExternalInput").ap()
    wv = nc.dram_tensor("wv", [1024, 16, 64], F32, kind="ExternalInput").ap()
    wo = nc.dram_tensor("wo", [1024, 1024], F32, kind="ExternalInput").ap()
    omega = nc.dram_tensor("omega", [256, 64], F32, kind="ExternalInput").ap()
    ln1_g = nc.dram_tensor("ln1_g", [1024], F32, kind="ExternalInput").ap()
    ln1_b = nc.dram_tensor("ln1_b", [1024], F32, kind="ExternalInput").ap()
    w1 = nc.dram_tensor("w1", [1024, 4096], F32, kind="ExternalInput").ap()
    b1 = nc.dram_tensor("b1", [4096], F32, kind="ExternalInput").ap()
    w2 = nc.dram_tensor("w2", [4096, 1024], F32, kind="ExternalInput").ap()
    b2 = nc.dram_tensor("b2", [1024], F32, kind="ExternalInput").ap()
    ln2_g = nc.dram_tensor("ln2_g", [1024], F32, kind="ExternalInput").ap()
    ln2_b = nc.dram_tensor("ln2_b", [1024], F32, kind="ExternalInput").ap()
    yT = nc.dram_tensor("yT", [D, T], F32, kind="ExternalOutput").ap()
    wq_bf = nc.dram_tensor("wq_bf", [1024, 1024], BF16, kind="Internal").ap()
    wk_bf = nc.dram_tensor("wk_bf", [1024, 1024], BF16, kind="Internal").ap()
    wv_bf = nc.dram_tensor("wv_bf", [1024, 1024], BF16, kind="Internal").ap()
    wo_bf = nc.dram_tensor("wo_bf", [1024, 1024], BF16, kind="Internal").ap()
    w1_bf = nc.dram_tensor("w1_bf", [1024, 4096], BF16, kind="Internal").ap()
    w2_bf = nc.dram_tensor("w2_bf", [4096, 1024], BF16, kind="Internal").ap()
    om_bf = nc.dram_tensor("om_bf", [256, 64], BF16, kind="Internal").ap()
    omtT_dram = nc.dram_tensor("omtT", [64, 256], BF16, kind="Internal").ap()

    def setup(c):
        alloc_consts(nc, c)
        emit_load_consts(c)
        emit_cast_dram(c, wq.rearrange("d h k -> d (h k)"), wq_bf)
        emit_cast_dram(c, wk.rearrange("d h k -> d (h k)"), wk_bf)
        emit_cast_dram(c, wv.rearrange("d h k -> d (h k)"), wv_bf)
        emit_cast_dram(c, wo, wo_bf)
        emit_cast_dram(c, w1, w1_bf)
        emit_cast_dram(c, w2, w2_bf)
        emit_cast_dram(c, omega, om_bf)
        c.wait_all("pcs")
        omt = c.sb("pf_omt", [128, 256], BF16)
        om_sb = c.sb("pf_omsb", [128, 128], BF16)
        c.dma("sync", om_sb[:, :].rearrange("p (cc w) -> p cc w", cc=2),
              om_bf.rearrange("(cc p) w -> p cc w", p=128), "pot")
        c.wait("tensor", "pot")
        for mc in range(2):
            c.op("tensor", "transpose", c.pbt[0:64, mc * 128:(mc + 1) * 128],
                 om_sb[:, mc * 64:(mc + 1) * 64], c.tiles["ident"][:, :],
                 sem=("potr" if mc == 1 else None))
        c.wait("vector", "potr")
        c.op("vector", "tensor_copy", omt[0:64, :], c.pbt[0:64, 0:256], sem="potc")
        c.wait("sync", "potc")
        c.dma("sync", omtT_dram[:, :], omt[0:64, :], "pot2")
        c.wait("sync", "pot2")
        c.dma("sync", omt[64:128, :], omtT_dram[:, :], "pot3")
        c.wait_all("pot3")
        c.wait_all("potc")

    P = {"xT": xT, "yT": yT, "wq": wq_bf, "wk": wk_bf, "wv": wv_bf, "wo": wo_bf,
         "w1": w1_bf, "w2": w2_bf, "omt": None, "b2": b2,
         "ln1_g": ln1_g, "ln1_b": ln1_b, "ln2_g": ln2_g, "ln2_b": ln2_b, "b1": b1}
    run_ctx(nc, [setup, lambda c: emit_perf(c, P)], with_pbt=True)
    return nc



def pe_table_np():
    S_, D_ = 8, 1024
    pos = np.arange(S_, dtype=np.float64)[:, None]
    div = np.exp(np.arange(0, D_, 2, dtype=np.float64) * (-np.log(10000.0) / D_))
    pe = np.zeros((S_, D_), np.float64)
    pe[:, 0::2] = np.sin(pos * div)
    pe[:, 1::2] = np.cos(pos * div)
    return np.ascontiguousarray(pe.T.astype(np.float32)).astype(ml_dtypes.bfloat16)


def emit_lab(c, P):
    nc = c.nc
    NH = 8
    EPS = 1.0e-6
    rowA = c.sb("rowA", [1, T], BF16)
    rowB = c.sb("rowB", [1, T], BF16)
    zf = c.sb("lb_z", [128, DC * T], BF16)
    qh = c.sb("lb_qh", [128, T], BF16)
    pkh = c.sb("lb_pkh", [128, T], BF16)
    vh = c.sb("lb_vh", [128, T], BF16)
    pqh = qh
    atth = c.sb("lb_atth", [128, T], BF16)
    numacc = c.sb("lb_num", [128, TT], F32)
    tbuf = c.sb("lb_tbuf", [128, TT], BF16)
    arow = c.sb("ln_r2", [1, TT], F32)
    prod = c.sb("lb_prod", [128, TT], BF16)
    pet = c.sb("lb_pet", [128, DC * 8], BF16)
    cq = c.sb("lb_cq", [128, 3 * 64], F32)
    bvec = c.sb("lb_bvec", [128, DC], F32)
    bor = c.sb("lb_bor", [1, 1024], BF16)
    ostage = c.sb("lb_ost", [128, TT], F32)
    wb = [c.sb(f"lb_wb{i}", [128, DC * 128], BF16) for i in range(2)]
    onesr = c.tiles["ones_row"]
    r1 = c.sb("ln_r1", [1, TT], F32)

    emit_ln_stats(c, P["xT"], rowA, rowB)

    c.dma("sync", pet[:, :].rearrange("p (cc w) -> p cc w", cc=DC),
          P["peT"].rearrange("(cc p) w -> p cc w", p=128), "lbl")
    c.dma("gpsimd", bor[0:1, :], P["bo"].rearrange("(o n) -> o n", o=1), "lblg")
    for wi, (wT, brow) in enumerate([(P["wqT"], P["bq"]), (P["wkT"], P["bk"]),
                                     (P["wvT"], P["bv"])]):
        emit_vec_rows(c, brow, bvec[:, 0:DC], 1024)
        for fc in range(DC):
            pi = fc % 2
            c.wait("sync", "lpw", c.n("lpw") - 1)
            load_wtile(c, "sync", wb[pi], wT, fc, DC, f"lbw{pi}")
            c.wait("tensor", f"lbw{pi}")
            c.wait("tensor", "lbl")
            c.wait("tensor", "lpc", c.n("lpc") - 1)
            for cc in range(DC):
                c.op("tensor", "matmul", c.pb[0 + pi][:, 0:8],
                     wb[pi][:, cc * 128:(cc + 1) * 128], pet[:, cc * 8:(cc + 1) * 8],
                     start=(cc == 0), stop=(cc == DC - 1),
                     sem=("lpw" if cc == DC - 1 else None))
            c.wait("vector", "lpw")
            c.wait("vector", "lbv", None)
            c.op("vector", "tensor_scalar", cq[:, wi * 64 + fc * 8:wi * 64 + (fc + 1) * 8],
                 c.pb[0 + pi][:, 0:8], bvec[:, fc:fc + 1], None, ALU.add, sem="lpc")
        c.wait_all("lpc")
        c.wait_all("lpw")
    c.wait_all("lbl")
    c.wait_all("lblg")

    xcz = chunked(P["xT"])
    xbz = c.sb("ln_xb", [128, 2 * TT], F32)
    t1z = c.sb("z_t1", [128, TT], F32)
    for tt in range(NTT):
        slz = slice(tt * TT, (tt + 1) * TT)
        c.wait("tensor", "zbc", None)
        c.op("tensor", "matmul", c.pb[4][:, :], c.tiles["ones_rowb128"][0:1, :],
             rowA[0:1, slz], sem="zbm", start=True, stop=True)
        c.op("tensor", "matmul", c.pb[5][:, :], c.tiles["ones_rowb128"][0:1, :],
             rowB[0:1, slz], sem="zbm", start=True, stop=True)
        c.wait("vector", "zbm")
        for cc in range(DC):
            pi = cc % 2
            xbt = xbz[:, pi * TT:(pi + 1) * TT]
            c.wait("sync", "zbc", c.n("zbc") - 3)
            c.dma("sync", xbt, xcz[:, cc, slz], f"zxl{pi}")
            c.wait("vector", f"zxl{pi}")
            c.wait("vector", "zbc")
            c.op("vector", "tensor_tensor", t1z[:, :], xbt, c.pb[4][:, :], ALU.mult, sem="zbc")
            c.wait("vector", "zbc")
            c.op("vector", "tensor_tensor", zf[:, cc * T + tt * TT:cc * T + (tt + 1) * TT],
                 t1z[:, :], c.pb[5][:, :], ALU.add, sem="zbc")

    c.dma("gpsimd", P["yT"][:, :], P["xT"][:, :], "lyc")
    c.wait_all("lyc")

    for h in range(NH):
        for wi, (wT, dst) in enumerate([(P["wqT"], qh), (P["wkT"], pkh), (P["wvT"], vh)]):
            pi = wi % 2
            c.wait("sync", "lqm", None)
            c.wait("sync", "lwom", None)
            load_wtile(c, "sync", wb[pi], wT, h, DC, f"lbw{pi}")
            c.wait("tensor", f"lbw{pi}")
            for tt in range(NTT):
                bank = c.pb[4 + (tt % 2)]
                c.wait("tensor", "lqev", c.n("lqev") - 1)
                if wi == 0 and tt == 0:
                    c.wait("tensor", "zbc", None)
                    c.wait("tensor", "lprod", None)
                    c.wait("tensor", "lelu", None)
                    c.wait("tensor", "labc", None)
                    c.wait("tensor", "ldbc", None)
                    c.wait("tensor", "loev", None)
                for cc in range(DC):
                    c.op("tensor", "matmul", bank[:, :],
                         wb[pi][:, cc * 128:(cc + 1) * 128],
                         zf[:, cc * T + tt * TT:cc * T + (tt + 1) * TT],
                         start=(cc == 0), stop=(cc == DC - 1),
                         sem=("lqm" if cc == DC - 1 else None))
                c.wait("vector", "lqm")
                c.op("vector", "tensor_scalar", dst[:, tt * TT:(tt + 1) * TT], bank[:, :],
                     cq[:, wi * 64 + h * 8 + tt:wi * 64 + h * 8 + tt + 1], None,
                     ALU.add, sem="lqev")
        for (srcb, dstb) in [(qh, pqh), (pkh, pkh)]:
            for tt in range(NTT):
                slt = slice(tt * TT, (tt + 1) * TT)
                c.wait("vector", "lqev", None)
                c.wait("vector", "lelu2", None)
                c.wait("vector", "lelu", None)
                c.op("vector", "tensor_scalar_min", tbuf[:, :], srcb[:, slt], 0.0, sem="lelu2")
                c.wait("scalar", "lelu2")
                c.op("scalar", "activation", prod[:, :], tbuf[:, :], AF.Exp, sem="lelu3")
                c.wait("vector", "lelu3")
                c.op("vector", "tensor_relu", tbuf[:, :], srcb[:, slt], sem="lelu2")
                c.wait("vector", "lelu2")
                c.op("vector", "tensor_tensor", dstb[:, slt], tbuf[:, :], prod[:, :],
                     ALU.add, sem="lelu")
        for s in range(NTT):
            ssl = slice(s * TT, (s + 1) * TT)
            if s == 0:
                c.wait("vector", "lwom", None)
            c.wait("vector", "lnum", None)
            c.wait("vector", "ldbc", None)
            c.op("vector", "memset", numacc[:, :], 0.0, sem="lnum")
            for sp in range(NTT):
                spl = slice(sp * TT, (sp + 1) * TT)
                c.wait("vector", "lelu", None)
                c.wait("vector", "lprodm", None)
                c.op("vector", "tensor_tensor", prod[:, :], pqh[:, ssl], pkh[:, spl],
                     ALU.mult, sem="lprod")
                c.wait("tensor", "lprod")
                c.wait("tensor", "larc", None)
                c.op("tensor", "matmul", c.pb[6][0:1, 0:512], c.tiles["ones_colb"][:, 0:1],
                     prod[:, :], start=True, stop=True, sem="lprodm")
                c.op("tensor", "matmul", c.pb[6][0:1, 0:512] if False else c.pb[5][0:1, 0:512],
                     c.tiles["ones_colb"][:, 0:1], prod[:, :],
                     start=(sp == 0), stop=(sp == NTT - 1), skip_group_check=True,
                     sem=("lden" if sp == NTT - 1 else None))
                c.wait("vector", "lprodm")
                c.op("vector", "tensor_copy", arow[0:1, :], c.pb[6][0:1, 0:512], sem="larc")
                c.wait("tensor", "larc")
                c.wait("tensor", "labc", None)
                c.wait("tensor", "ldbc", None)
                c.op("tensor", "matmul", c.pb[4][:, :], onesr[0:1, :], arow[0:1, :],
                     start=True, stop=True, sem="labm")
                c.wait("vector", "labm")
                c.op("vector", "tensor_tensor", tbuf[:, :], vh[:, spl], c.pb[4][:, :],
                     ALU.mult, sem="labc")
                c.wait("vector", "labc")
                c.op("vector", "tensor_tensor", numacc[:, :], numacc[:, :], tbuf[:, :],
                     ALU.add, sem="lnum")
            c.wait("vector", "lden")
            c.op("vector", "tensor_scalar", r1[0:1, :], c.pb[5][0:1, 0:512], EPS, None,
                 ALU.add, sem="lrec")
            c.wait("vector", "lrec")
            c.op("vector", "reciprocal", r1[0:1, :], r1[0:1, :], sem="lrec")
            c.wait("tensor", "lrec")
            c.wait("tensor", "ldbc", None)
            c.op("tensor", "matmul", c.pb[4][:, :], onesr[0:1, :], r1[0:1, :],
                 start=True, stop=True, sem="ldbm")
            c.wait("vector", "ldbm")
            c.wait("vector", "lnum", None)
            c.op("vector", "tensor_tensor", atth[:, ssl], numacc[:, :], c.pb[4][:, :],
                 ALU.mult, sem="ldbc")
        c.wait("sync", "lqm", None)
        c.wait("gpsimd", "lyc", None)
        c.wait("gpsimd", "lobs", None)
        for oc in range(DC):
            pi = oc % 2
            c.wait("sync", "lwom", c.n("lwom") - 1)
            load_wtile(c, "sync", wb[pi], P["woT"], oc, DC, f"lbw{pi}")
            c.wait("tensor", f"lbw{pi}")
            for s in range(NTT):
                sslc = slice(s * TT, (s + 1) * TT)
                bank = c.pb[4 + (s % 2)]
                c.wait("tensor", "ldbc", None)
                c.wait("tensor", "loev", c.n("loev") - 1)
                last = not (h == 0)
                c.op("tensor", "matmul", bank[:, :], wb[pi][:, h * 128:(h + 1) * 128],
                     atth[:, sslc], start=True, stop=last,
                     sem=("lwom" if last else None))
                if h == 0:
                    c.op("tensor", "matmul", bank[:, :], bor[0:1, oc * 128:(oc + 1) * 128],
                         c.tiles["ones_rowb"][0:1, :], start=False, stop=True, sem="lwom")
                c.wait("scalar", "lwom")
                c.wait("scalar", "lobs", None)
                c.op("scalar", "activation", ostage[:, :],
                     bank[:, :], AF.Copy, sem="loev")
                c.wait("gpsimd", "loev")
                c.dma("gpsimd", chunked(P["yT"])[:, oc, sslc], ostage[:, :], "lobs",
                      accum_op=ALU.add)
    c.wait_all("lobs")


def build_step_l1_attn_b():
    nc = bass.Bass("TRN2", target_bir_lowering=False)
    xT = nc.dram_tensor("xT", [D, T], F32, kind="ExternalInput").ap()
    Wq = nc.dram_tensor("Wq", [1024, 1024], F32, kind="ExternalInput").ap()
    bq = nc.dram_tensor("bq", [1024], F32, kind="ExternalInput").ap()
    Wk = nc.dram_tensor("Wk", [1024, 1024], F32, kind="ExternalInput").ap()
    bk = nc.dram_tensor("bk", [1024], F32, kind="ExternalInput").ap()
    Wv = nc.dram_tensor("Wv", [1024, 1024], F32, kind="ExternalInput").ap()
    bv = nc.dram_tensor("bv", [1024], F32, kind="ExternalInput").ap()
    Wo = nc.dram_tensor("Wo", [1024, 1024], F32, kind="ExternalInput").ap()
    bo = nc.dram_tensor("bo", [1024], F32, kind="ExternalInput").ap()
    yT = nc.dram_tensor("yT", [D, T], F32, kind="ExternalOutput").ap()
    scr = {}
    for nm in ("q", "k", "v", "o"):
        scr[nm + "_bf"] = nc.dram_tensor(nm + "_bf", [1024, 1024], BF16, kind="Internal").ap()
        scr[nm + "T"] = nc.dram_tensor(nm + "T", [1024, 1024], BF16, kind="Internal").ap()
    peT = nc.inline_tensor(pe_table_np(), name="peT").ap()

    def setup(c):
        alloc_consts(nc, c)
        emit_load_consts(c)
        for nm, w in (("q", Wq), ("k", Wk), ("v", Wv), ("o", Wo)):
            emit_cast_dram(c, w, scr[nm + "_bf"])
        c.wait_all("pcs")
        for nm in ("q", "k", "v", "o"):
            emit_transpose_scratch(c, scr[nm + "_bf"], scr[nm + "T"], 1024, 1024)

    P = {"xT": xT, "yT": yT, "wqT": scr["qT"], "wkT": scr["kT"], "wvT": scr["vT"],
         "woT": scr["oT"], "bq": bq, "bk": bk, "bv": bv, "bo": bo, "peT": peT}
    run_ctx(nc, [setup, lambda c: emit_lab(c, P)])
    return nc



def _persist_shared(c):
    c.sb("ln_xb", [128, 2 * TT], F32)
    c.sb("ln_r1", [1, TT], F32)
    c.sb("ln_r2", [1, TT], F32)
    c.sb("z_t1", [128, TT], F32)
    c.sb("ln_sq", [128, 2 * TT], BF16)
    c.sb("rowA", [1, T], BF16)
    c.sb("rowB", [1, T], BF16)


def build_fused_part1():
    nc = bass.Bass("TRN2", target_bir_lowering=False)
    I = {}
    for nm, shp, dt in [
        ("xT", [D, T], F32),
        ("L0_attn_a_qw", [1024, 1024], F32), ("L0_attn_a_kw", [1024, 1024], F32),
        ("L0_attn_a_vw", [1024, 1024], F32), ("L0_attn_a_ow", [1024, 1024], F32),
        ("L0_attn_b_qkv_w", [3072, 1024], F32), ("L0_attn_b_qkv_b", [3072], F32),
        ("L0_attn_b_out_w", [1024, 1024], F32), ("L0_attn_b_out_b", [1024], F32),
        ("L0_ffn_w_gate", [3072, 1024], I32), ("L0_ffn_w_up", [3072, 1024], I32),
        ("L0_ffn_w_down", [1024, 3072], I32),
        ("L1_attn_a_wq", [1024, 16, 64], F32), ("L1_attn_a_wk", [1024, 16, 64], F32),
        ("L1_attn_a_wv", [1024, 16, 64], F32), ("L1_attn_a_wo", [1024, 1024], F32),
        ("L1_attn_a_omega", [256, 64], F32),
        ("L1_attn_a_ln1_g", [1024], F32), ("L1_attn_a_ln1_b", [1024], F32),
        ("L1_attn_a_w1", [1024, 4096], F32), ("L1_attn_a_b1", [4096], F32),
        ("L1_attn_a_w2", [4096, 1024], F32), ("L1_attn_a_b2", [1024], F32),
        ("L1_attn_a_ln2_g", [1024], F32), ("L1_attn_a_ln2_b", [1024], F32),
        ("L1_ffn_router_w", [8, 1024], F32),
        ("L1_ffn_w_gate", [8, 1024, 2048], F32), ("L1_ffn_w_up", [8, 1024, 2048], F32),
        ("L1_ffn_w_down", [8, 2048, 1024], F32),
    ]:
        I[nm] = nc.dram_tensor(nm, shp, dt, kind="ExternalInput").ap()
    yT = nc.dram_tensor("yT", [D, T], F32, kind="ExternalOutput").ap()
    sA = nc.dram_tensor("sA", [D, T], F32, kind="Internal").ap()
    sB = nc.dram_tensor("sB", [D, T], F32, kind="Internal").ap()
    S = {}
    def scratch(nm, shp):
        S[nm] = nc.dram_tensor(nm, shp, BF16, kind="Internal").ap()
        return S[nm]
    for nm, shp in [("aa_qwp", [8, 128, 1024]), ("aa_kwp", [8, 128, 1024]),
                    ("aa_vwp", [8, 128, 1024]), ("aa_owp", [8, 128, 1024]),
                    ("f0_wg_bf", [3072, 1024]), ("f0_wu_bf", [3072, 1024]),
                    ("f0_wd_bf", [1024, 3072]),
                    ("f0_wgT", [24, 128, 1024]), ("f0_wuT", [24, 128, 1024]),
                    ("f0_wdT", [8, 128, 3072]),
                    ("ab_qkv_bf", [3072, 1024]), ("ab_qkvT", [24, 128, 1024]),
                    ("ab_ow_bf", [1024, 1024]), ("ab_owT", [8, 128, 1024]),
                    ("pf_wq_bf", [8, 128, 1024]), ("pf_wk_bf", [8, 128, 1024]),
                    ("pf_wv_bf", [8, 128, 1024]), ("pf_wo_bf", [8, 128, 1024]),
                    ("pf_w1_bf", [32, 128, 1024]), ("pf_w2_bf", [8, 128, 4096]),
                    ("pf_om_bf", [256, 64]), ("pf_omtT", [64, 256]),
                    ("mo_wg_bf", [128, 128, 1024]), ("mo_wu_bf", [128, 128, 1024]),
                    ("mo_wd_bf", [64, 128, 2048])]:
        scratch(nm, shp)
    sel = nc.inline_tensor(moe_sel_np(), name="moe_sel").ap()

    def main(c):
        alloc_consts(nc, c)
        c.tiles["moe_sel_dram"] = sel
        emit_load_consts(c)
        _persist_shared(c)
        ab_bias = c.sb("ab_bias", [1, 4096], BF16)
        pf_omt = c.sb("pf_omt", [128, 256], BF16)
        c.tiles["ropeP2"] = c.sb("ropeP2_sb", [128, 128], BF16)
        c.dma("sync", c.tiles["ropeP2"][:], _ROPEP2_AP[0], "cst")
        c.wait_all("cst")
        c.begin_arena(165 * 1024)
        c.new_phase("pre")
        emit_permute_qkv_pre(c, I["L0_attn_a_qw"], S["aa_qwp"])
        emit_permute_qkv_pre(c, I["L0_attn_a_kw"], S["aa_kwp"])
        emit_permute_qkv_pre(c, I["L0_attn_a_vw"], S["aa_vwp"])
        emit_gather_ow_pre(c, I["L0_attn_a_ow"], S["aa_owp"])
        emit_cast_dram(c, I["L0_ffn_w_gate"], S["f0_wg_bf"])
        emit_cast_dram(c, I["L0_ffn_w_up"], S["f0_wu_bf"])
        emit_cast_dram(c, I["L0_ffn_w_down"], S["f0_wd_bf"])
        emit_cast_dram(c, I["L0_attn_b_qkv_w"], S["ab_qkv_bf"])
        emit_cast_dram(c, I["L0_attn_b_out_w"], S["ab_ow_bf"])
        emit_cast_tiled(c, I["L1_attn_a_wq"].rearrange("d h k -> d (h k)"), S["pf_wq_bf"], 1024, 1024)
        emit_cast_tiled(c, I["L1_attn_a_wk"].rearrange("d h k -> d (h k)"), S["pf_wk_bf"], 1024, 1024)
        emit_cast_tiled(c, I["L1_attn_a_wv"].rearrange("d h k -> d (h k)"), S["pf_wv_bf"], 1024, 1024)
        emit_cast_tiled(c, I["L1_attn_a_wo"], S["pf_wo_bf"], 1024, 1024)
        emit_cast_tiled(c, I["L1_attn_a_w1"], S["pf_w1_bf"], 1024, 4096)
        emit_cast_tiled(c, I["L1_attn_a_w2"], S["pf_w2_bf"], 4096, 1024)
        emit_cast_dram(c, I["L1_attn_a_omega"], S["pf_om_bf"])
        for wnm, inm in [("mo_wg_bf", "L1_ffn_w_gate"), ("mo_wu_bf", "L1_ffn_w_up"),
                         ("mo_wd_bf", "L1_ffn_w_down")]:
            srcv = I[inm].rearrange("e (cc p) (fc w) -> e fc p cc w", p=128, w=128)
            nfc = srcv.shape[1]
            for e in range(8):
                for fc in range(nfc):
                    c.dma("gpsimd", S[wnm][e * nfc + fc].rearrange("p (cc w) -> p cc w", w=128),
                          srcv[e, fc], "pcs")
        c.dma("gpsimd", ab_bias[0:1, 0:3072],
              I["L0_attn_b_qkv_b"].rearrange("(o n) -> o n", o=1), "pcsb")
        c.dma("gpsimd", ab_bias[0:1, 3072:4096],
              I["L0_attn_b_out_b"].rearrange("(o n) -> o n", o=1), "pcsb")
        c.wait_all("pcs")
        c.wait_all("pcsb")
        emit_transpose_scratch(c, S["f0_wg_bf"], S["f0_wgT"], 3072, 1024)
        emit_transpose_scratch(c, S["f0_wu_bf"], S["f0_wuT"], 3072, 1024)
        emit_transpose_scratch(c, S["f0_wd_bf"], S["f0_wdT"], 1024, 3072)
        emit_transpose_scratch(c, S["ab_qkv_bf"], S["ab_qkvT"], 3072, 1024)
        emit_transpose_scratch(c, S["ab_ow_bf"], S["ab_owT"], 1024, 1024)
        om_sb = c.sb("pf_omsb", [128, 128], BF16)
        c.dma("sync", om_sb[:, :].rearrange("p (cc w) -> p cc w", cc=2),
              S["pf_om_bf"].rearrange("(cc p) w -> p cc w", p=128), "pot")
        c.wait("tensor", "pot")
        for mc in range(2):
            c.op("tensor", "transpose", c.pbt[0:64, mc * 128:(mc + 1) * 128],
                 om_sb[:, mc * 64:(mc + 1) * 64], c.tiles["ident"][:, :],
                 sem=("potr" if mc == 1 else None))
        c.wait("vector", "potr")
        c.op("vector", "tensor_copy", pf_omt[0:64, :], c.pbt[0:64, 0:256], sem="potc")
        c.wait("sync", "potc")
        c.dma("sync", S["pf_omtT"][:, :], pf_omt[0:64, :], "pot2")
        c.wait("sync", "pot2")
        c.dma("sync", pf_omt[64:128, :], S["pf_omtT"][:, :], "pot3")
        c.wait_all("pot3")
        c.wait_all("potc")
        c.new_phase("s1")
        emit_causal_attn(c, {"xT": I["xT"], "yT": sA, "qw": S["aa_qwp"], "kw": S["aa_kwp"],
                             "vw": S["aa_vwp"], "ow": S["aa_owp"], "scale": 1.0 / 32.0,
                             "rope": None})
        c.new_phase("s2")
        build_l0_ffn(nc, sA, sB, S["f0_wgT"], S["f0_wuT"], S["f0_wdT"])(c)
        c.new_phase("s3")
        emit_causal_attn(c, {"xT": sB, "yT": sA,
                             "qw": S["ab_qkvT"][0:8], "kw": S["ab_qkvT"][8:16],
                             "vw": S["ab_qkvT"][16:24], "ow": S["ab_owT"],
                             "qb_sb": ab_bias[:, 0:1024], "kb_sb": ab_bias[:, 1024:2048],
                             "vb_sb": ab_bias[:, 2048:3072], "ob_sb": ab_bias[:, 3072:4096],
                             "scale": 1.0 / 8.0, "rope": _ROPETAB_AP[0]})
        c.new_phase("s4")
        build_l0_ffn(nc, sA, sB, S["f0_wgT"], S["f0_wuT"], S["f0_wdT"])(c)
        c.new_phase("s5")
        emit_perf(c, {"xT": sB, "yT": sA, "wq": S["pf_wq_bf"], "wk": S["pf_wk_bf"],
                      "wv": S["pf_wv_bf"], "wo": S["pf_wo_bf"], "w1": S["pf_w1_bf"],
                      "w2": S["pf_w2_bf"], "b2": I["L1_attn_a_b2"],
                      "ln1_g": I["L1_attn_a_ln1_g"], "ln1_b": I["L1_attn_a_ln1_b"],
                      "ln2_g": I["L1_attn_a_ln2_g"], "ln2_b": I["L1_attn_a_ln2_b"],
                      "b1": I["L1_attn_a_b1"]})
        c.new_phase("s6")
        emit_moe(c, {"xT": sA, "yT": yT, "router": I["L1_ffn_router_w"],
                     "wg": S["mo_wg_bf"], "wu": S["mo_wu_bf"], "wd": S["mo_wd_bf"]})

    _ROPETAB_AP = [nc.inline_tensor(rope_table_np(), name="ropetab").ap()]
    _ROPEP2_AP = [nc.inline_tensor(rope_swap_np(), name="ropeP2").ap()]
    run_ctx(nc, [main], with_pbt=True)
    return nc


def build_fused_part2():
    nc = bass.Bass("TRN2", target_bir_lowering=False)
    I = {}
    for nm, shp, dt in [
        ("xT", [D, T], F32),
        ("L1_attn_b_Wq", [1024, 1024], F32), ("L1_attn_b_bq", [1024], F32),
        ("L1_attn_b_Wk", [1024, 1024], F32), ("L1_attn_b_bk", [1024], F32),
        ("L1_attn_b_Wv", [1024, 1024], F32), ("L1_attn_b_bv", [1024], F32),
        ("L1_attn_b_Wo", [1024, 1024], F32), ("L1_attn_b_bo", [1024], F32),
        ("L1_ffn_router_w", [8, 1024], F32),
        ("L1_ffn_w_gate", [8, 1024, 2048], F32), ("L1_ffn_w_up", [8, 1024, 2048], F32),
        ("L1_ffn_w_down", [8, 2048, 1024], F32),
    ]:
        I[nm] = nc.dram_tensor(nm, shp, dt, kind="ExternalInput").ap()
    yT = nc.dram_tensor("yT", [D, T], F32, kind="ExternalOutput").ap()
    sA = nc.dram_tensor("sA", [D, T], F32, kind="Internal").ap()
    S = {}
    for nm, shp in [("lb_q_bf", [1024, 1024]), ("lb_qT", [1024, 1024]),
                    ("lb_k_bf", [1024, 1024]), ("lb_kT", [1024, 1024]),
                    ("lb_v_bf", [1024, 1024]), ("lb_vT", [1024, 1024]),
                    ("lb_o_bf", [1024, 1024]), ("lb_oT", [1024, 1024]),
                    ("mo_wg_bf", [8 * 1024, 2048]), ("mo_wu_bf", [8 * 1024, 2048]),
                    ("mo_wd_bf", [8 * 2048, 1024])]:
        S[nm] = nc.dram_tensor(nm, shp, BF16, kind="Internal").ap()
    peT = nc.inline_tensor(pe_table_np(), name="peT").ap()
    sel = nc.inline_tensor(moe_sel_np(), name="moe_sel").ap()

    def main(c):
        alloc_consts(nc, c)
        c.tiles["moe_sel_dram"] = sel
        emit_load_consts(c)
        _persist_shared(c)
        c.begin_arena(165 * 1024)
        c.new_phase("pre")
        for pair in [("L1_attn_b_Wq", "q"), ("L1_attn_b_Wk", "k"),
                     ("L1_attn_b_Wv", "v"), ("L1_attn_b_Wo", "o")]:
            emit_cast_dram(c, I[pair[0]], S[f"lb_{pair[1]}_bf"])
        emit_cast_dram(c, I["L1_ffn_w_gate"].rearrange("e d f -> (e d) f"), S["mo_wg_bf"])
        emit_cast_dram(c, I["L1_ffn_w_up"].rearrange("e d f -> (e d) f"), S["mo_wu_bf"])
        emit_cast_dram(c, I["L1_ffn_w_down"].rearrange("e h f -> (e h) f"), S["mo_wd_bf"])
        c.wait_all("pcs")
        for nm in ("q", "k", "v", "o"):
            emit_transpose_scratch(c, S[f"lb_{nm}_bf"], S[f"lb_{nm}T"], 1024, 1024)
        c.new_phase("s7")
        emit_lab(c, {"xT": I["xT"], "yT": sA, "wqT": S["lb_qT"], "wkT": S["lb_kT"],
                     "wvT": S["lb_vT"], "woT": S["lb_oT"], "bq": I["L1_attn_b_bq"],
                     "bk": I["L1_attn_b_bk"], "bv": I["L1_attn_b_bv"],
                     "bo": I["L1_attn_b_bo"], "peT": peT})
        c.new_phase("s8")
        emit_moe(c, {"xT": sA, "yT": yT, "router": I["L1_ffn_router_w"],
                     "wg": S["mo_wg_bf"], "wu": S["mo_wu_bf"], "wd": S["mo_wd_bf"]})

    run_ctx(nc, [main])
    return nc



def build_fused_full():
    nc = bass.Bass("TRN2", target_bir_lowering=False)
    I = {}
    for nm, shp, dt in [
        ("xT", [D, T], F32),
        ("L0_attn_a_qw", [1024, 1024], F32), ("L0_attn_a_kw", [1024, 1024], F32),
        ("L0_attn_a_vw", [1024, 1024], F32), ("L0_attn_a_ow", [1024, 1024], F32),
        ("L0_attn_b_qkv_w", [3072, 1024], F32), ("L0_attn_b_qkv_b", [3072], F32),
        ("L0_attn_b_out_w", [1024, 1024], F32), ("L0_attn_b_out_b", [1024], F32),
        ("L0_ffn_w_gate", [3072, 1024], I32), ("L0_ffn_w_up", [3072, 1024], I32),
        ("L0_ffn_w_down", [1024, 3072], I32),
        ("L1_attn_a_wq", [1024, 16, 64], F32), ("L1_attn_a_wk", [1024, 16, 64], F32),
        ("L1_attn_a_wv", [1024, 16, 64], F32), ("L1_attn_a_wo", [1024, 1024], F32),
        ("L1_attn_a_omega", [256, 64], F32),
        ("L1_attn_a_ln1_g", [1024], F32), ("L1_attn_a_ln1_b", [1024], F32),
        ("L1_attn_a_w1", [1024, 4096], F32), ("L1_attn_a_b1", [4096], F32),
        ("L1_attn_a_w2", [4096, 1024], F32), ("L1_attn_a_b2", [1024], F32),
        ("L1_attn_a_ln2_g", [1024], F32), ("L1_attn_a_ln2_b", [1024], F32),
        ("L1_ffn_router_w", [8, 1024], F32),
        ("L1_ffn_w_gate", [8, 1024, 2048], F32), ("L1_ffn_w_up", [8, 1024, 2048], F32),
        ("L1_ffn_w_down", [8, 2048, 1024], F32),
        ("L1_attn_b_Wq", [1024, 1024], F32), ("L1_attn_b_bq", [1024], F32),
        ("L1_attn_b_Wk", [1024, 1024], F32), ("L1_attn_b_bk", [1024], F32),
        ("L1_attn_b_Wv", [1024, 1024], F32), ("L1_attn_b_bv", [1024], F32),
        ("L1_attn_b_Wo", [1024, 1024], F32), ("L1_attn_b_bo", [1024], F32),
    ]:
        I[nm] = nc.dram_tensor(nm, shp, dt, kind="ExternalInput").ap()
    yT = nc.dram_tensor("yT", [D, T], F32, kind="ExternalOutput").ap()
    sA = nc.dram_tensor("sA", [D, T], F32, kind="Internal").ap()
    sB = nc.dram_tensor("sB", [D, T], F32, kind="Internal").ap()
    x6a = nc.dram_tensor("x6a", [D, T], F32, kind="Internal").ap()
    xg = nc.dram_tensor("xg", [8 * D, T], F32, kind="Internal").ap()
    x7in = nc.dram_tensor("x7in", [D, T], F32, kind="Internal").ap()
    S = {}
    for nm, shp in [("aa_qwp", [8, 128, 1024]), ("aa_kwp", [8, 128, 1024]),
                    ("aa_vwp", [8, 128, 1024]), ("aa_owp", [8, 128, 1024]),
                    ("f0_wg_bf", [3072, 1024]), ("f0_wu_bf", [3072, 1024]),
                    ("f0_wd_bf", [1024, 3072]),
                    ("f0_wgT", [24, 128, 1024]), ("f0_wuT", [24, 128, 1024]),
                    ("f0_wdT", [8, 128, 3072]),
                    ("ab_qkv_bf", [3072, 1024]), ("ab_qkvT", [24, 128, 1024]),
                    ("ab_ow_bf", [1024, 1024]), ("ab_owT", [8, 128, 1024]),
                    ("pf_wq_bf", [8, 128, 1024]), ("pf_wk_bf", [8, 128, 1024]),
                    ("pf_wv_bf", [8, 128, 1024]), ("pf_wo_bf", [8, 128, 1024]),
                    ("pf_w1_bf", [32, 128, 1024]), ("pf_w2_bf", [8, 128, 4096]),
                    ("pf_om_bf", [256, 64]), ("pf_omtT", [64, 256]),
                    ("mo_wg_bf", [128, 128, 1024]), ("mo_wu_bf", [128, 128, 1024]),
                    ("mo_wd_bf", [64, 128, 2048]),
                    ("lb_q_bf", [1024, 1024]), ("lb_qT", [8, 128, 1024]),
                    ("lb_k_bf", [1024, 1024]), ("lb_kT", [8, 128, 1024]),
                    ("lb_v_bf", [1024, 1024]), ("lb_vT", [8, 128, 1024]),
                    ("lb_o_bf", [1024, 1024]), ("lb_oT", [8, 128, 1024])]:
        S[nm] = nc.dram_tensor(nm, shp, BF16, kind="Internal").ap()
    sel = nc.inline_tensor(moe_sel_np(), name="moe_sel").ap()
    ropetab = nc.inline_tensor(rope_table_np(), name="ropetab").ap()
    ropeP2 = nc.inline_tensor(rope_swap_np(), name="ropeP2").ap()
    peT = nc.inline_tensor(pe_table_np(), name="peT").ap()

    def main(c):
        alloc_consts(nc, c)
        c.tiles["moe_sel_dram"] = sel
        emit_load_consts(c)
        _persist_shared(c)
        ab_bias = c.sb("ab_bias", [1, 4096], BF16)
        pf_omt = c.sb("pf_omt", [128, 256], BF16)
        c.tiles["ropeP2"] = c.sb("ropeP2_sb", [128, 128], BF16)
        c.dma("sync", c.tiles["ropeP2"][:], ropeP2, "cst")
        c.wait_all("cst")
        c.begin_arena(165 * 1024)
        c.new_phase("pre")
        emit_permute_qkv_pre(c, I["L0_attn_a_qw"], S["aa_qwp"])
        emit_permute_qkv_pre(c, I["L0_attn_a_kw"], S["aa_kwp"])
        emit_permute_qkv_pre(c, I["L0_attn_a_vw"], S["aa_vwp"])
        emit_gather_ow_pre(c, I["L0_attn_a_ow"], S["aa_owp"])
        emit_cast_dram(c, I["L0_ffn_w_gate"], S["f0_wg_bf"])
        emit_cast_dram(c, I["L0_ffn_w_up"], S["f0_wu_bf"])
        emit_cast_dram(c, I["L0_ffn_w_down"], S["f0_wd_bf"])
        emit_cast_dram(c, I["L0_attn_b_qkv_w"], S["ab_qkv_bf"])
        emit_cast_dram(c, I["L0_attn_b_out_w"], S["ab_ow_bf"])
        emit_cast_dram(c, I["L1_attn_a_omega"], S["pf_om_bf"])
        emit_cast_dram(c, I["L1_attn_b_Wq"], S["lb_q_bf"])
        emit_cast_dram(c, I["L1_attn_b_Wk"], S["lb_k_bf"])
        emit_cast_dram(c, I["L1_attn_b_Wv"], S["lb_v_bf"])
        emit_cast_dram(c, I["L1_attn_b_Wo"], S["lb_o_bf"])
        snap_pre = c.n("pcs")
        c.dma("gpsimd", ab_bias[0:1, 0:3072],
              I["L0_attn_b_qkv_b"].rearrange("(o n) -> o n", o=1), "pcsb")
        c.dma("gpsimd", ab_bias[0:1, 3072:4096],
              I["L0_attn_b_out_b"].rearrange("(o n) -> o n", o=1), "pcsb")
        emit_cast_tiled(c, I["L1_attn_a_wq"].rearrange("d h k -> d (h k)"), S["pf_wq_bf"], 1024, 1024)
        emit_cast_tiled(c, I["L1_attn_a_wk"].rearrange("d h k -> d (h k)"), S["pf_wk_bf"], 1024, 1024)
        emit_cast_tiled(c, I["L1_attn_a_wv"].rearrange("d h k -> d (h k)"), S["pf_wv_bf"], 1024, 1024)
        emit_cast_tiled(c, I["L1_attn_a_wo"], S["pf_wo_bf"], 1024, 1024)
        emit_cast_tiled(c, I["L1_attn_a_w1"], S["pf_w1_bf"], 1024, 4096)
        emit_cast_tiled(c, I["L1_attn_a_w2"], S["pf_w2_bf"], 4096, 1024)
        for wnm, inm in [("mo_wg_bf", "L1_ffn_w_gate"), ("mo_wu_bf", "L1_ffn_w_up"),
                         ("mo_wd_bf", "L1_ffn_w_down")]:
            srcv = I[inm].rearrange("e (cc p) (fc w) -> e fc p cc w", p=128, w=128)
            nfc = srcv.shape[1]
            for e in range(8):
                for fc in range(nfc):
                    c.dma("gpsimd", S[wnm][e * nfc + fc].rearrange("p (cc w) -> p cc w", w=128),
                          srcv[e, fc], "pcs")
        snap_mo = c.n("pcs")
        c.wait_all("pcsb")
        c.wait("scalar", "pcs", snap_pre)
        c.wait("sync", "pcs", snap_pre)
        emit_transpose_scratch(c, S["f0_wg_bf"], S["f0_wgT"], 3072, 1024)
        emit_transpose_scratch(c, S["f0_wu_bf"], S["f0_wuT"], 3072, 1024)
        emit_transpose_scratch(c, S["f0_wd_bf"], S["f0_wdT"], 1024, 3072)
        emit_transpose_scratch(c, S["ab_qkv_bf"], S["ab_qkvT"], 3072, 1024)
        emit_transpose_scratch(c, S["ab_ow_bf"], S["ab_owT"], 1024, 1024)
        for nm in ("q", "k", "v", "o"):
            emit_transpose_scratch(c, S[f"lb_{nm}_bf"], S[f"lb_{nm}T"], 1024, 1024)
        om_sb = c.sb("pf_omsb", [128, 128], BF16)
        c.dma("sync", om_sb[:, :].rearrange("p (cc w) -> p cc w", cc=2),
              S["pf_om_bf"].rearrange("(cc p) w -> p cc w", p=128), "pot")
        c.wait("tensor", "pot")
        for mc in range(2):
            c.op("tensor", "transpose", c.pbt[0:64, mc * 128:(mc + 1) * 128],
                 om_sb[:, mc * 64:(mc + 1) * 64], c.tiles["ident"][:, :],
                 sem=("potr" if mc == 1 else None))
        c.wait("vector", "potr")
        c.op("vector", "tensor_copy", pf_omt[0:64, :], c.pbt[0:64, 0:256], sem="potc")
        c.wait("sync", "potc")
        c.dma("sync", S["pf_omtT"][:, :], pf_omt[0:64, :], "pot2")
        c.wait("sync", "pot2")
        c.dma("sync", pf_omt[64:128, :], S["pf_omtT"][:, :], "pot3")
        c.wait_all("pot3")
        c.wait_all("potc")
        c.new_phase("s1")
        emit_causal_attn(c, {"xT": I["xT"], "yT": sA, "qw": S["aa_qwp"], "kw": S["aa_kwp"],
                             "vw": S["aa_vwp"], "ow": S["aa_owp"], "scale": 1.0 / 32.0,
                             "rope": None})
        c.new_phase("s2")
        build_l0_ffn(nc, sA, sB, S["f0_wgT"], S["f0_wuT"], S["f0_wdT"])(c)
        c.new_phase("s3")
        emit_causal_attn(c, {"xT": sB, "yT": sA,
                             "qw": S["ab_qkvT"][0:8], "kw": S["ab_qkvT"][8:16],
                             "vw": S["ab_qkvT"][16:24], "ow": S["ab_owT"],
                             "qb_sb": ab_bias[:, 0:1024], "kb_sb": ab_bias[:, 1024:2048],
                             "vb_sb": ab_bias[:, 2048:3072], "ob_sb": ab_bias[:, 3072:4096],
                             "scale": 1.0 / 8.0, "rope": ropetab})
        c.new_phase("s4")
        build_l0_ffn(nc, sA, sB, S["f0_wgT"], S["f0_wuT"], S["f0_wdT"])(c)
        c.new_phase("s5")
        c.wait("sync", "pcs", snap_mo)
        emit_perf(c, {"xT": sB, "yT": sA, "wq": S["pf_wq_bf"], "wk": S["pf_wk_bf"],
                      "wv": S["pf_wv_bf"], "wo": S["pf_wo_bf"], "w1": S["pf_w1_bf"],
                      "w2": S["pf_w2_bf"], "b2": I["L1_attn_a_b2"],
                      "ln1_g": I["L1_attn_a_ln1_g"], "ln1_b": I["L1_attn_a_ln1_b"],
                      "ln2_g": I["L1_attn_a_ln2_g"], "ln2_b": I["L1_attn_a_ln2_b"],
                      "b1": I["L1_attn_a_b1"]})
        c.new_phase("s6")
        emit_moe(c, {"xT": sA, "yT": sB, "router": I["L1_ffn_router_w"],
                     "wg": S["mo_wg_bf"], "wu": S["mo_wu_bf"], "wd": S["mo_wd_bf"]})
        c.new_phase("rs")
        if c.kind == "probe":
            c.swdge_names.add("a2a")
        c.dma("sync", x6a[:, :], sB[:, :], "rsc")
        c.wait("gpsimd", "rsc")
        c.op("gpsimd", "collective_compute", "AllGather", ALU.bypass,
             [[0, 1, 2, 3, 4, 5, 6, 7]], [x6a.opt()], [xg.opt()], sem="a2a")
        c.wait("sync", "a2a")
        if c.kind == "sync":
            pid = nc.partition_id()
        for s in range(8):
            if c.kind == "sync":
                c.eng.dma_start(out=x7in[:, s * 512:(s + 1) * 512],
                                in_=xg[s * D:(s + 1) * D, bass.ts(pid, 512)]
                                ).then_inc(c.S("rsc"), 16)
            c.bump("rsc", 16)
        c.wait_all("rsc")
        c.wait_all("a2a")
        c.new_phase("s7")
        emit_lab(c, {"xT": x7in, "yT": sA, "wqT": S["lb_qT"], "wkT": S["lb_kT"],
                     "wvT": S["lb_vT"], "woT": S["lb_oT"], "bq": I["L1_attn_b_bq"],
                     "bk": I["L1_attn_b_bk"], "bv": I["L1_attn_b_bv"],
                     "bo": I["L1_attn_b_bo"], "peT": peT})
        c.new_phase("s8")
        emit_moe(c, {"xT": sA, "yT": yT, "router": I["L1_ffn_router_w"],
                     "wg": S["mo_wg_bf"], "wu": S["mo_wu_bf"], "wd": S["mo_wd_bf"]})

    run_ctx(nc, [main], with_pbt=True)
    return nc


_NC_CACHE = {}


def _cached(name, fn):
    if name not in _NC_CACHE:
        _NC_CACHE[name] = fn()
    return _NC_CACHE[name]


def kernel(**inputs):
    cores = list(range(8))
    x = np.asarray(inputs["L0_attn_a_x"], dtype=np.float32)
    keys = ["L0_attn_a_qw", "L0_attn_a_kw", "L0_attn_a_vw", "L0_attn_a_ow",
            "L0_attn_b_qkv_w", "L0_attn_b_qkv_b", "L0_attn_b_out_w", "L0_attn_b_out_b",
            "L0_ffn_w_gate", "L0_ffn_w_up", "L0_ffn_w_down",
            "L1_attn_a_wq", "L1_attn_a_wk", "L1_attn_a_wv", "L1_attn_a_wo",
            "L1_attn_a_omega", "L1_attn_a_ln1_g", "L1_attn_a_ln1_b",
            "L1_attn_a_w1", "L1_attn_a_b1", "L1_attn_a_w2", "L1_attn_a_b2",
            "L1_attn_a_ln2_g", "L1_attn_a_ln2_b",
            "L1_ffn_router_w", "L1_ffn_w_gate", "L1_ffn_w_up", "L1_ffn_w_down",
            "L1_attn_b_Wq", "L1_attn_b_bq", "L1_attn_b_Wk", "L1_attn_b_bk",
            "L1_attn_b_Wv", "L1_attn_b_bv", "L1_attn_b_Wo", "L1_attn_b_bo"]
    nc = _cached("full", build_fused_full)
    maps = []
    for i in range(8):
        m = {"xT": np.ascontiguousarray(x[i].T)}
        for k in keys:
            m[k] = inputs[k]
        maps.append(m)
    res = run_bass_kernel_spmd(nc, maps, core_ids=cores)
    out = np.empty((8, 4096, 1024), np.float32)
    for i in range(8):
        yT = np.asarray(res.results[i]["yT"])
        out[:, i * 512:(i + 1) * 512, :] = yT.reshape(1024, 8, 512).transpose(1, 2, 0)
    return out


# version 08
# speedup vs baseline: 1.0196x; 1.0196x over previous
import numpy as np
import ml_dtypes
import concourse.bass as bass
import concourse.mybir as mybir
from concourse.bass_utils import run_bass_kernel_spmd

F32 = mybir.dt.float32
BF16 = mybir.dt.bfloat16
I32 = mybir.dt.int32
AF = mybir.ActivationFunctionType
ALU = mybir.AluOpType
AX = mybir.AxisListType

D = 1024
DC = D // 128
T = 4096
TT = 512
NTT = T // TT
ENGINE_KINDS = ("sync", "tensor", "vector", "scalar", "gpsimd")


class Ctx:

    def __init__(self, nc):
        self.nc = nc
        self.sems = {}
        self.kind = None
        self.eng = None
        self.cnt = {}
        self.pb = []
        self.pbt = None
        self.tiles = {}
        self.alias = {}
        self.sem_usage = {}
        self.swdge_names = set()
        self.dma_names = set()

    def S(self, name):
        name = self.alias.get(name, name)
        if name not in self.sems:
            self.sems[name] = self.nc.alloc_semaphore(name)
        return self.sems[name]

    def _use(self, name):
        if self.kind == "probe":
            self.sem_usage.setdefault(getattr(self, "phase", "g"), set()).add(name)

    def sb(self, name, shape, dtype):
        if name in self.tiles:
            return self.tiles[name]
        if getattr(self, "arena_start", None) is None:
            self.tiles[name] = self.nc.alloc_sbuf_tensor(name, shape, dtype)
            return self.tiles[name]
        key = (self.phase, name)
        if key in self.arena_tiles:
            h, end = self.arena_tiles[key]
            self.arena_bump = end
            self.tiles[name] = h
            return h
        esz = {F32: 4, I32: 4, BF16: 2}[dtype]
        bpp = int(np.prod(shape[1:])) * esz
        off = (self.arena_bump + 31) // 32 * 32
        assert off + bpp <= self.arena_end, f"arena overflow: {name} in {self.phase}"
        h = self.nc.alloc_sbuf_tensor_at(f"{name}__{self.phase}", shape, dtype, offset=off)
        self.arena_bump = off + bpp
        self.arena_tiles[key] = (h, self.arena_bump)
        self.tiles[name] = h
        return h

    def begin_arena(self, size):
        self.persistent = set(self.tiles.keys())
        base = self.nc.sbuf_base
        self.arena_start = base
        self.arena_bump = base
        self.arena_end = base + size
        self.arena_tiles = getattr(self, "arena_tiles", {})
        self.phase = "p0"
        if not getattr(self, "_arena_reserved", False):
            self.nc.alloc_sbuf_tensor("arena_resv", [128, size], mybir.dt.int8)
            self._arena_reserved = True

    def new_phase(self, tag):
        for k in list(self.tiles.keys()):
            if k not in self.persistent:
                del self.tiles[k]
        self.arena_bump = self.arena_start
        self.phase = tag

    def n(self, sem):
        sem = self.alias.get(sem, sem)
        return self.cnt.get(sem, 0)

    def bump(self, sem, v):
        self._use(sem)
        sem = self.alias.get(sem, sem)
        self.cnt[sem] = self.cnt.get(sem, 0) + v
        return self.cnt[sem]

    def dma(self, q, out, in_, sem, **kw):
        if self.kind == "probe":
            self.dma_names.add(sem)
            if q == "gpsimd":
                self.swdge_names.add(sem)
        self.bump(sem, 16)
        if self.kind == q:
            self.eng.dma_start(out=out, in_=in_, **kw).then_inc(self.S(sem), 16)
        return self.n(sem)

    def dmat(self, q, out, in_, sem):
        if self.kind == "probe":
            self.dma_names.add(sem)
        self.bump(sem, 16)
        if self.kind == q:
            self.eng.dma_start_transpose(out, in_).then_inc(self.S(sem), 16)
        return self.n(sem)

    def op(self, who, name, *a, sem=None, **kw):
        if sem is not None:
            self.bump(sem, 1)
        if self.kind == who:
            r = getattr(self.eng, name)(*a, **kw)
            if sem is not None:
                r.then_inc(self.S(sem), 1)

    def mark(self, who, sem):
        self.bump(sem, 1)
        if self.kind == who:
            self.eng.nop().then_inc(self.S(sem), 1)

    def wait(self, who, sem, val=None):
        self._use(sem)
        if self.kind == who:
            v = self.n(sem) if val is None else max(0, val)
            self.eng.wait_ge(self.S(sem), v)

    def wait_all(self, sem, val=None):
        for k in ENGINE_KINDS:
            self.wait(k, sem, val)


def run_ctx(nc, emitters, n_psum_banks=8, with_pbt=False):
    c = Ctx(nc)
    for i in range(n_psum_banks):
        c.pb.append(nc.alloc_psum_tensor(f"pb{i}", [128, 512], F32))
    if with_pbt:
        c.pbt = c.pb[6][:, :].bitcast(BF16)
    c.kind = "probe"
    c.eng = None
    c.cnt = {}
    for em in emitters:
        em(c)
    names_order = []
    seen = set()
    for ph in c.sem_usage:
        for nm in sorted(c.sem_usage[ph]):
            if nm not in seen:
                seen.add(nm)
                names_order.append(nm)
    conflicts = {nm: set() for nm in seen}
    for ph, nms in c.sem_usage.items():
        for a in nms:
            for b in nms:
                if a != b:
                    conflicts[a].add(b)
    color = {}
    comp = [nm for nm in names_order if nm not in c.swdge_names and nm not in c.dma_names]
    for nm in comp:
        used = {color[o] for o in conflicts[nm] if o in color}
        k = 0
        while k in used:
            k += 1
        color[nm] = k
    base = (max(color.values()) + 1) if color else 0
    hwd = [nm for nm in names_order if nm in c.dma_names and nm not in c.swdge_names]
    color2 = {}
    for nm in hwd:
        used = {color2[o] for o in conflicts[nm] if o in color2}
        k = 0
        while k in used:
            k += 1
        color2[nm] = k
    for nm, k in color2.items():
        color[nm] = base + k
    base2 = base + ((max(color2.values()) + 1) if color2 else 0)
    for i, nm in enumerate(sorted(c.swdge_names)):
        color[nm] = base2 + i
    c.alias = {nm: f"sem{color[nm]}" for nm in names_order}
    ncolors = max(color.values()) + 1 if color else 0
    assert ncolors <= 100, f"too many semaphores: {ncolors}"
    with nc.Block() as block:
        def mk(kind):
            def body(eng):
                c.kind = kind
                c.eng = eng
                c.cnt = {}
                for em in emitters:
                    em(c)
            return body
        block.sync(mk("sync"))
        block.tensor(mk("tensor"))
        block.vector(mk("vector"))
        block.scalar(mk("scalar"))
        block.gpsimd(mk("gpsimd"))
    return c



def alloc_consts(nc, c):
    if "ident_dram" not in c.tiles:
        eye = np.eye(128, dtype=np.float32)
        c.tiles["ident_dram"] = nc.inline_tensor(eye.astype(ml_dtypes.bfloat16), name="ident_c").ap()
        c.tiles["ones_row_dram"] = nc.inline_tensor(np.ones((1, 128), np.float32), name="ones_row_c").ap()
        c.tiles["ones_col_dram"] = nc.inline_tensor(np.ones((128, 1), np.float32), name="ones_col_c").ap()
    if "masks_dram" not in c.tiles:
        nc = c.nc
        m = np.zeros((128, 4 * 512), np.float32)
        for r in range(4):
            for p in range(128):
                q0 = 128 * r + p
                if q0 < 512:
                    m[p, r * 512 + q0:(r + 1) * 512] = 1.0
        c.tiles["masks_dram"] = nc.inline_tensor(m.astype(ml_dtypes.bfloat16), name="masks_c").ap()
        hs = np.zeros((128, 2), np.float32)
        hs[0:64, 0] = 1.0
        hs[64:128, 1] = 1.0
        c.tiles["hsel_dram"] = nc.inline_tensor(hs.astype(ml_dtypes.bfloat16), name="hsel_c").ap()
        c.tiles["ones_colb_dram"] = nc.inline_tensor(np.ones((128, 1), ml_dtypes.bfloat16), name="ones_colb_c").ap()
        c.tiles["ones_rowb_dram"] = nc.inline_tensor(np.ones((1, 512), ml_dtypes.bfloat16), name="ones_rowb_c").ap()
        c.tiles["ones_rowb128_dram"] = nc.inline_tensor(np.ones((1, 128), ml_dtypes.bfloat16), name="ones_rowb128_c").ap()
    c.sb("ones_row", [1, 128], F32)
    c.sb("ones_col", [128, 1], F32)
    c.sb("ident", [128, 128], BF16)
    c.sb("masks", [128, 4 * 512], BF16)
    c.sb("hsel", [128, 2], BF16)
    c.sb("ones_colb", [128, 1], BF16)
    c.sb("ones_rowb", [1, 512], BF16)
    c.sb("ones_rowb128", [1, 128], BF16)
    c.tiles["identb8"] = c.sb("ident", [128, 128], BF16)


def emit_load_consts(c):
    t = c.tiles
    c.dma("sync", t["ones_row"][:], t["ones_row_dram"], "cst")
    c.dma("sync", t["ones_col"][:], t["ones_col_dram"], "cst")
    c.dma("sync", t["ident"][:], t["ident_dram"], "cst")
    c.dma("sync", t["masks"][:], t["masks_dram"], "cst")
    c.dma("sync", t["hsel"][:], t["hsel_dram"], "cst")
    c.dma("sync", t["ones_colb"][:], t["ones_colb_dram"], "cst")
    c.dma("sync", t["ones_rowb"][:], t["ones_rowb_dram"], "cst")
    c.dma("sync", t["ones_rowb128"][:], t["ones_rowb128_dram"], "cst")
    c.wait_all("cst")


def chunked(ap_dram, p=128):
    return ap_dram.rearrange("(c p) w -> p c w", p=p)


def load_wtile(c, q, sbuf_tile, tiled_ap, fc, nch, sem):
    c.dma(q, sbuf_tile[:, 0:nch * 128].rearrange("p (cc w) -> p cc w", cc=nch),
          tiled_ap[fc].rearrange("p (cc w) -> p cc w", cc=nch), sem)


def load_vtile(c, q, sbuf_tile, tiled_ap, f0, nf, nch, sem, W=128):
    c.dma(q, sbuf_tile[:, 0:nch * nf * W].rearrange("p (cc fj w) -> p cc fj w",
                                                    cc=nch, fj=nf, w=W),
          tiled_ap[f0:f0 + nf].rearrange("f p (cc w) -> p cc f w", w=W), sem)


def emit_cast_tiled(c, src, dst, R, C, W=128):
    srcv = src.rearrange("(cc p) (fc w) -> fc p cc w", p=128, w=W)
    for fc in range(C // W):
        c.dma("gpsimd", dst[fc].rearrange("p (cc w) -> p cc w", w=W),
              srcv[fc], "pcs")


def load_chunks(c, q, sbuf_tile, dram_ap, col_slice, nch, sem):
    W = col_slice.stop - col_slice.start
    c.dma(q, sbuf_tile[:, 0:nch * W].rearrange("p (cc w) -> p cc w", cc=nch),
          chunked(dram_ap)[:, :, col_slice], sem)


def store_chunks(c, q, dram_ap, col_slice, sbuf_tile, nch, sem):
    W = col_slice.stop - col_slice.start
    c.dma(q, chunked(dram_ap)[:, :, col_slice],
          sbuf_tile[:, 0:nch * W].rearrange("p (cc w) -> p cc w", cc=nch), sem)



def emit_cast_dram(c, src, dst):
    c.dma("gpsimd", dst[:, :], src[:, :], "pcs")


def emit_transpose_scratch(c, src_bf, dst, R, C):
    strip = c.sb("at_K", [128, DC * T], BF16)[:, 0:3072]
    for cc in range(C // 128):
        c.wait("scalar", "pts", None)
        for rc in range(R // 128):
            c.dmat("scalar", strip[:, rc * 128:(rc + 1) * 128],
                   src_bf[rc * 128:(rc + 1) * 128, cc * 128:(cc + 1) * 128], "ptt")
        c.wait("sync", "ptt")
        c.dma("sync", dst.rearrange("f p (cc w) -> p f cc w", w=128)[:, :, cc, :],
              strip[:, 0:R], "pts")
    c.wait_all("pts")


def emit_vec_rows(c, src_row, dst_tile, n):
    row = c.sb("ln_r1", [1, TT], F32)
    nch = n // 128
    src2 = src_row if len(src_row.shape) == 2 else src_row.rearrange("(o n) -> o n", o=1)
    c.wait("tensor", "vrc", None)
    for seg in range(n // 512):
        c.wait("sync", "vrm", None)
        c.dma("sync", row[0:1, :], src2[0:1, seg * 512:(seg + 1) * 512], "vrl")
        c.wait("tensor", "vrl")
        for ch in range(4):
            c.op("tensor", "matmul", c.pb[5][:, seg * 4 + ch:seg * 4 + ch + 1],
                 row[0:1, ch * 128:(ch + 1) * 128], c.tiles["ones_row"][0:1, 0:1],
                 sem="vrm", start=True, stop=True)
    c.wait("vector", "vrm")
    c.op("vector", "tensor_copy", dst_tile[:, 0:nch], c.pb[5][:, 0:nch], sem="vrc")
    c.wait_all("vrc")



def emit_ln_stats(c, xT, rowA, rowB, eps=1e-5):
    xb = c.sb("ln_xb", [128, 2 * TT], F32)
    sq = c.sb("ln_sq", [128, 2 * TT], BF16)
    r1 = c.sb("ln_r1", [1, TT], F32)
    r2 = c.sb("ln_r2", [1, TT], F32)
    xc = chunked(xT)
    for tt in range(NTT):
        sl = slice(tt * TT, (tt + 1) * TT)
        c.wait("tensor", "lnb", None)
        c.wait("sync", "lnm", None)
        for cc in range(DC):
            pi = cc % 2
            xbt = xb[:, pi * TT:(pi + 1) * TT]
            sqt = sq[:, pi * TT:(pi + 1) * TT]
            c.wait("sync", "lnxm", c.n("lnxm") - 1)
            c.wait("sync", "lns", c.n("lns") - 1)
            c.dma("sync", xbt, xc[:, cc, sl], f"lnx{pi}")
            c.wait("tensor", f"lnx{pi}")
            c.op("tensor", "matmul", c.pb[0][0:1, :], c.tiles["ones_col"][:, 0:1],
                 xbt, sem=("lnm" if cc == DC - 1 else "lnxm"), start=(cc == 0),
                 stop=(cc == DC - 1))
            c.wait("scalar", f"lnx{pi}")
            c.wait("scalar", "lnq", c.n("lnq") - 1)
            c.op("scalar", "activation", sqt, xbt, AF.Square, sem="lns")
            c.wait("tensor", "lns")
            c.op("tensor", "matmul", c.pb[1][0:1, :], c.tiles["ones_colb"][:, 0:1],
                 sqt, sem="lnq", start=(cc == 0), stop=(cc == DC - 1))
        c.wait("vector", "lnm")
        c.wait("vector", "lnq")
        inv = 1.0 / D
        c.op("vector", "tensor_scalar", r1[:, :], c.pb[0][0:1, :], inv, None, ALU.mult)
        c.op("vector", "tensor_scalar", r2[:, :], c.pb[1][0:1, :], inv, None, ALU.mult, sem="lnv")
        t1r = c.sb("z_t1", [128, TT], F32)[0:1, :]
        c.wait("vector", "lnv")
        c.op("vector", "tensor_tensor", t1r, r1[:, :], r1[:, :], ALU.mult, sem="lnv")
        c.wait("vector", "lnv")
        c.op("vector", "tensor_tensor", r2[:, :], r2[:, :], t1r, ALU.subtract, sem="lnv")
        c.wait("vector", "lnv")
        c.op("vector", "tensor_scalar", r2[:, :], r2[:, :], eps, None, ALU.add, sem="lnv")
        c.wait("vector", "lnv")
        c.op("vector", "reciprocal", r2[:, :], r2[:, :], sem="lnv")
        c.wait("scalar", "lnv")
        c.op("scalar", "activation", rowA[:, sl], r2[:, :], AF.Sqrt, sem="lnsq")
        c.wait("vector", "lnsq")
        c.op("vector", "tensor_tensor", t1r, r1[:, :], rowA[:, sl], ALU.mult, sem="lnv")
        c.wait("vector", "lnv")
        c.op("vector", "tensor_scalar", rowB[:, sl], t1r, -1.0, None, ALU.mult, sem="lnb")
    c.wait_all("lnb")
    c.wait_all("lns")
    c.wait_all("lnq")


def emit_z_tile_stream(c, xT, rowA, rowB, tt, zset, nchunks=DC, bankA=4, bankB=5,
                       zstride=None, zoff=0):
    if zstride is None:
        zstride = TT
    sl = slice(tt * TT, (tt + 1) * TT)
    xb = c.sb("ln_xb", [128, 2 * TT], F32)
    t1 = c.sb("z_t1", [128, TT], F32)
    xc = chunked(xT)
    c.op("tensor", "matmul", c.pb[bankA][:, :], c.tiles["ones_rowb128"][0:1, :], rowA[0:1, sl],
         sem="zbm", start=True, stop=True)
    c.op("tensor", "matmul", c.pb[bankB][:, :], c.tiles["ones_rowb128"][0:1, :], rowB[0:1, sl],
         sem="zbm", start=True, stop=True)
    c.wait("vector", "zbm")
    for cc in range(nchunks):
        pi = cc % 2
        xbt = xb[:, pi * TT:(pi + 1) * TT]
        c.wait("sync", "zbc", c.n("zbc") - 3)
        c.dma("sync", xbt, xc[:, cc, sl], f"zxl{pi}")
        c.wait("vector", f"zxl{pi}")
        c.wait("vector", "zbc")
        c.op("vector", "tensor_tensor", t1[:, :], xbt, c.pb[bankA][:, :], ALU.mult, sem="zbc")
        c.wait("vector", "zbc")
        c.op("vector", "tensor_tensor", zset[:, cc * zstride + zoff:cc * zstride + zoff + TT],
             t1[:, :], c.pb[bankB][:, :], ALU.add, sem="zbc")


def emit_z_tile(c, xs_tile, rowA, rowB, tt, zset, nchunks=DC, bankA=4, bankB=5,
                zstride=None, zoff=0):
    if zstride is None:
        zstride = TT
    sl = slice(tt * TT, (tt + 1) * TT)
    t1 = c.sb("z_t1", [128, TT], F32)
    c.op("tensor", "matmul", c.pb[bankA][:, :], c.tiles["ones_rowb128"][0:1, :], rowA[0:1, sl],
         sem="zbm", start=True, stop=True)
    c.op("tensor", "matmul", c.pb[bankB][:, :], c.tiles["ones_rowb128"][0:1, :], rowB[0:1, sl],
         sem="zbm", start=True, stop=True)
    c.wait("vector", "zbm")
    for cc in range(nchunks):
        c.wait("vector", "zbc")
        c.op("vector", "tensor_tensor", t1[:, :],
             xs_tile[:, cc * zstride + zoff:cc * zstride + zoff + TT],
             c.pb[bankA][:, :], ALU.mult, sem="zbc")
        c.wait("vector", "zbc")
        c.op("vector", "tensor_tensor", zset[:, cc * zstride + zoff:cc * zstride + zoff + TT],
             t1[:, :], c.pb[bankB][:, :], ALU.add, sem="zbc")



def build_l0_ffn(nc, xT, yT, wgT, wuT, wdT):
    H = 3072
    HC = H // 128
    s = 1.0 / 127.0

    NTP = 2
    NPS = NTT // NTP
    W = NTP * TT

    def emit(c):
        rowA = c.sb("rowA", [1, T], BF16)
        rowB = c.sb("rowB", [1, T], BF16)
        zset = c.sb("zset", [128, DC * W], BF16)
        hset = c.sb("hset", [128, HC * W], BF16)
        h1 = c.sb("h1", [128, 2 * W], BF16)
        wgb = [c.sb(f"wgb{i}", [128, DC * 128], BF16) for i in range(2)]
        wub = [c.sb(f"wub{i}", [128, DC * 128], BF16) for i in range(2)]
        wdb = [c.sb(f"wdb{i}", [128, HC * 128], BF16) for i in range(2)]
        xs = c.sb("ffn_x", [128, DC * W], F32)
        emit_ln_stats(c, xT, rowA, rowB)
        for ps in range(NPS):
            sl = slice(ps * W, (ps + 1) * W)
            c.wait("sync", "res", None)
            load_chunks(c, "sync", xs, xT, sl, DC, "xl")
            c.wait("vector", "xl")
            c.wait("vector", "gum", None)
            c.wait("vector", "dnm", None)
            c.wait("tensor", "zbc", None)
            c.wait("tensor", "sil", None)
            c.wait("tensor", "hev", None)
            for t in range(NTP):
                if t > 0:
                    c.wait("tensor", "zbc", None)
                emit_z_tile(c, xs, rowA, rowB, ps * NTP + t, zset, zstride=W, zoff=t * TT)
            gum0 = c.n("gum")
            sil0 = c.n("sil")
            hev0 = c.n("hev")
            for fc in range(HC):
                pi = fc % 2
                gbk = (c.pb[0 + 4 * pi], c.pb[1 + 4 * pi])
                ubk = (c.pb[2 + 4 * pi], c.pb[3 + 4 * pi])
                c.wait("sync", "gum", gum0 + fc - 1)
                load_wtile(c, "sync", wgb[pi], wgT, fc, DC, f"wgl{pi}")
                load_wtile(c, "sync", wub[pi], wuT, fc, DC, f"wul{pi}")
                c.wait("tensor", f"wgl{pi}")
                c.wait("tensor", f"wul{pi}")
                if fc == 0:
                    c.wait("tensor", "dnv", None)
                    c.wait("tensor", "zbc")
                elif fc == 1:
                    c.wait("tensor", "zbc", None)
                else:
                    c.wait("tensor", "sil", sil0 + fc - 1)
                    c.wait("tensor", "hev", hev0 + 2 * fc - 2)
                for t in range(NTP):
                    for cc in range(DC):
                        c.op("tensor", "matmul", gbk[t][:, :], wgb[pi][:, cc * 128:(cc + 1) * 128],
                             zset[:, cc * W + t * TT:cc * W + (t + 1) * TT],
                             start=(cc == 0), stop=(cc == DC - 1))
                for t in range(NTP):
                    for cc in range(DC):
                        c.op("tensor", "matmul", ubk[t][:, :], wub[pi][:, cc * 128:(cc + 1) * 128],
                             zset[:, cc * W + t * TT:cc * W + (t + 1) * TT],
                             sem=("gum" if (cc == DC - 1 and t == NTP - 1) else None),
                             start=(cc == 0), stop=(cc == DC - 1))
                c.wait("scalar", "gum", gum0 + fc + 1)
                c.wait("scalar", "hev", hev0 + 2 * fc - 2)
                for t in range(NTP):
                    c.op("scalar", "activation", h1[:, (pi * NTP + t) * TT:(pi * NTP + t + 1) * TT],
                         gbk[t][:, :], AF.Silu, sem=("sil" if t == NTP - 1 else None), scale=s)
                c.wait("vector", "sil", sil0 + fc + 1)
                for t in range(NTP):
                    c.op("vector", "tensor_tensor", hset[:, fc * W + t * TT:fc * W + (t + 1) * TT],
                         ubk[t][:, :], h1[:, (pi * NTP + t) * TT:(pi * NTP + t + 1) * TT],
                         ALU.mult, sem="hev")
            c.wait("tensor", "hev")
            dnm0 = c.n("dnm")
            dnv0 = c.n("dnv")
            for dc in range(DC):
                pj = dc % 2
                dbk = (c.pb[0 + 2 * pj], c.pb[1 + 2 * pj])
                c.wait("sync", "dnm", dnm0 + dc - 1)
                load_wtile(c, "sync", wdb[pj], wdT, dc, HC, f"wdl{pj}")
                c.wait("tensor", f"wdl{pj}")
                if dc <= 1:
                    c.wait("tensor", "sil", None)
                    c.wait("tensor", "hev", None)
                else:
                    c.wait("tensor", "dnv", dnv0 + 2 * dc - 2)
                for t in range(NTP):
                    for hc in range(HC):
                        c.op("tensor", "matmul", dbk[t][:, :], wdb[pj][:, hc * 128:(hc + 1) * 128],
                             hset[:, hc * W + t * TT:hc * W + (t + 1) * TT],
                             sem=("dnm" if (hc == HC - 1 and t == NTP - 1) else None),
                             start=(hc == 0), stop=(hc == HC - 1))
                c.wait("vector", "dnm", dnm0 + dc + 1)
                for t in range(NTP):
                    c.op("vector", "scalar_tensor_tensor",
                         xs[:, dc * W + t * TT:dc * W + (t + 1) * TT],
                         dbk[t][:, :], s * s, xs[:, dc * W + t * TT:dc * W + (t + 1) * TT],
                         ALU.mult, ALU.add, sem="dnv")
            c.wait("sync", "dnv")
            store_chunks(c, "sync", yT, sl, xs, DC, "res")
        c.wait_all("res")

    return emit


def build_l0_ffn_pre(nc, wg_i, wu_i, wd_i, wgT, wuT, wdT, wg_bf, wu_bf, wd_bf):
    def emit(c):
        emit_cast_dram(c, wg_i, wg_bf)
        emit_cast_dram(c, wu_i, wu_bf)
        emit_cast_dram(c, wd_i, wd_bf)
        c.wait_all("pcs")
        emit_transpose_scratch(c, wg_bf, wgT, 3072, 1024)
        emit_transpose_scratch(c, wu_bf, wuT, 3072, 1024)
        emit_transpose_scratch(c, wd_bf, wdT, 1024, 3072)
    return emit





def emit_permute_qkv_pre(c, src, dst):
    pool = c.sb("at_Q", [128, 4096], BF16)
    tin = pool[:, 0:1024]
    tout = pool[:, 1024:2048]
    for cc in range(8):
        c.wait("gpsimd", "ppqv", None)
        c.dma("gpsimd", tin[:], src[cc * 128:(cc + 1) * 128, :], "ppql")
        c.wait("vector", "ppql")
        c.wait("vector", "ppqs", None)
        c.op("vector", "tensor_copy", tout[:].rearrange("p (h i) -> p i h", h=16, i=64),
             tin[:].rearrange("p (i h) -> p i h", i=64, h=16), sem="ppqv")
        c.wait("sync", "ppqv")
        c.dma("sync", dst.rearrange("f p (cc w) -> p f cc w", w=128)[:, :, cc, :],
              tout[:], "ppqs")
    c.wait_all("ppqs")


def emit_gather_ow_pre(c, src, dst):
    tin = c.sb("at_Q", [128, 4096], BF16)[:, 0:1024]
    srcg = src.rearrange("(i h) k -> h i k", i=64, h=16)
    for cc in range(8):
        c.wait("gpsimd", "ppos", None)
        c.dma("gpsimd", tin[:], srcg[2 * cc:2 * cc + 2, :, :], "ppol")
        c.wait("sync", "ppol")
        c.dma("sync", dst.rearrange("f p (cc w) -> p f cc w", w=128)[:, :, cc, :],
              tin[:], "ppos")
    c.wait_all("ppos")


def rope_apply(c, dst_sb, tabC, tabS, bank, msem, vsem):
    tmp = c.sb("at_qsq", [128, TT], BF16)
    P2 = c.tiles["ropeP2"]
    c.op("tensor", "matmul", c.pb[bank][:, :], P2[:, :], dst_sb, start=True, stop=True,
         sem=msem)
    c.wait("vector", vsem)
    c.op("vector", "tensor_tensor", tmp[:, :], dst_sb, tabC, ALU.mult, sem=vsem)
    c.wait("vector", msem)
    c.op("vector", "tensor_tensor", c.pb[bank][:, :], c.pb[bank][:, :], tabS, ALU.mult,
         sem=vsem)
    c.wait("vector", vsem)
    c.op("vector", "tensor_tensor", dst_sb, tmp[:, :], c.pb[bank][:, :], ALU.add, sem=vsem)


def emit_causal_attn(c, P):
    nc = c.nc
    rowA = c.sb("rowA", [1, T], BF16)
    rowB = c.sb("rowB", [1, T], BF16)
    zq = c.sb("at_z", [128, DC * TT], BF16)
    Ksb = c.sb("at_K", [128, DC * T], BF16)
    Vsb = c.sb("at_V", [128, (T // 128) * 16 * 65], BF16)
    Vv = Vsb[:, :].rearrange("p (x w) -> p x w", w=65)
    Qsb = c.sb("at_Q", [128, DC * TT], BF16)
    att = zq
    Ebuf = c.sb("at_E", [128, 4 * TT], BF16)
    qsq = c.sb("at_qsq", [128, TT], BF16)
    kmg = c.sb("at_kmg", [1, 2], F32)
    qmx = c.sb("at_qmx", [1, 16], F32)
    crow = c.sb("at_crow", [1, 16], F32)
    biasc = c.sb("at_bias", [128, 16], F32)
    rA_row = c.sb("ln_r1", [1, TT], F32)
    rB_row = c.sb("ln_r2", [1, TT], F32)
    rbc = c.sb("at_rbc", [128, TT], BF16)
    rbc2 = c.sb("at_rbc2", [64, TT], BF16)
    wpool = c.sb("at_wpool", [128, DC * 512 + 2 * DC * 128], BF16)
    vwb = [wpool[:, 0:DC * 512] for i in range(2)]
    wb = [wpool[:, DC * 512 + i * DC * 128:DC * 512 + (i + 1) * DC * 128] for i in range(2)]
    rxb = c.sb("ln_xb", [128, 2 * TT], F32)
    hsel = c.tiles["hsel"]
    onesr = c.tiles["ones_row"]
    masks = c.tiles["masks"]
    rope = P.get("rope")

    emit_ln_stats(c, P["xT"], rowA, rowB)
    c.op("vector", "memset", kmg[:, :], 0.0, sem="akg")
    c.op("vector", "memset", Vv[:, :, 64:65], 1.0, sem="avon")
    c.wait("tensor", "avon")

    for tt in range(NTT):
        sl = slice(tt * TT, (tt + 1) * TT)
        c.wait("vector", "akm", None)
        c.wait("vector", "avm", None)
        c.wait("tensor", "zbc", None)
        emit_z_tile_stream(c, P["xT"], rowA, rowB, tt, zq, bankA=4, bankB=5)
        for fc in range(DC):
            pi = fc % 2
            kbank = c.pb[0 + pi]
            if fc == 0:
                c.wait("sync", "avm", None)
            c.wait("sync", "akm", c.n("akm") - 1)
            load_wtile(c, "sync", wb[pi], P["kw"], fc, DC, f"akw{pi}")
            c.wait("tensor", f"akw{pi}")
            if fc == 0:
                c.wait("tensor", "zbc")
            c.wait("tensor", "akev", c.n("akev") - 1)
            last = (P.get("kb_sb") is None)
            for cc in range(DC):
                c.op("tensor", "matmul", kbank[:, :], wb[pi][:, cc * 128:(cc + 1) * 128],
                     zq[:, cc * TT:(cc + 1) * TT], start=(cc == 0),
                     stop=(cc == DC - 1 and last),
                     sem=("akm" if (cc == DC - 1 and last) else None))
            if P.get("kb_sb") is not None:
                c.op("tensor", "matmul", kbank[:, :], P["kb_sb"][0:1, fc * 128:(fc + 1) * 128],
                     c.tiles["ones_rowb"][0:1, :], start=False, stop=True, sem="akm")
            c.wait("scalar", "akm")
            c.op("scalar", "activation", Ksb[:, fc * T + tt * TT:fc * T + (tt + 1) * TT],
                 kbank[:, :], AF.Copy, sem="akev")
        c.wait("sync", "akm", None)
        for kb in range(TT // 128):
            for half in range(2):
                pi = (kb * 2 + half) % 2
                vbank = c.pb[2 + pi]
                c.wait("sync", "avm", None)
                load_vtile(c, "sync", vwb[pi], P["vw"], half * 4, 4, DC, f"avw{pi}")
                c.wait("tensor", f"avw{pi}")
                c.wait("tensor", "avev", c.n("avev") - 1)
                last = (P.get("vb_sb") is None)
                for cc in range(DC):
                    c.op("tensor", "matmul", vbank[:, :],
                         zq[:, cc * TT + kb * 128:cc * TT + (kb + 1) * 128],
                         vwb[pi][:, cc * 512:(cc + 1) * 512],
                         start=(cc == 0), stop=(cc == DC - 1 and last),
                         sem=("avm" if (cc == DC - 1 and last) else None))
                if P.get("vb_sb") is not None:
                    c.op("tensor", "matmul", vbank[:, :], c.tiles["ones_rowb"][0:1, 0:128],
                         P["vb_sb"][0:1, half * 512:(half + 1) * 512], start=False, stop=True,
                         sem="avm")
                c.wait("vector", "avm")
                xv = (tt * 4 + kb) * 16 + half * 8
                c.op("vector", "tensor_copy", Vv[:, xv:xv + 8, 0:64], vbank[:, :], sem="avev")
    if rope is not None:
        tabC = c.sb("ln_sq", [128, 2 * TT], BF16)[:, 0:TT]
        tabS = c.sb("ln_sq", [128, 2 * TT], BF16)[:, TT:2 * TT]
        c.wait("vector", "akev")
        c.wait("scalar", "akev")
        c.wait("tensor", "akev")
        for tt in range(NTT):
            c.wait("sync", "akr", None)
            c.dma("sync", tabC, rope[0, :, tt * TT:(tt + 1) * TT], "artl")
            c.dma("sync", tabS, rope[1, :, tt * TT:(tt + 1) * TT], "artl")
            c.wait("vector", "artl")
            c.wait("tensor", "artl")
            for fc in range(DC):
                bank = fc % 2
                c.wait("tensor", "akr", c.n("akr") - 3)
                rope_apply(c, Ksb[:, fc * T + tt * TT:fc * T + (tt + 1) * TT],
                           tabC, tabS, bank, "akrm", "akr")
    c.wait("scalar", "akev")
    c.wait("tensor", "akev")
    if rope is not None:
        c.wait("scalar", "akr")
    c.wait_all("avev")
    for fc in range(DC):
        for tt in range(NTT):
            ks = Ksb[:, fc * T + tt * TT:fc * T + (tt + 1) * TT]
            c.wait("scalar", "akn", None)
            c.op("scalar", "activation", qsq[:, :], ks, AF.Square, sem="aks")
            c.wait("tensor", "aks")
            c.wait("tensor", "aknv", None)
            c.op("tensor", "matmul", c.pb[0][0:1, :], hsel[:, 0:1], qsq[:, :], start=True, stop=True)
            c.op("tensor", "matmul", c.pb[1][0:1, :], hsel[:, 1:2], qsq[:, :], sem="akn",
                 start=True, stop=True)
            c.wait("vector", "akn")
            c.op("vector", "reduce_max", rA_row[0:1, 0:1], c.pb[0][0:1, :], AX.X)
            c.op("vector", "reduce_max", rA_row[0:1, 1:2], c.pb[1][0:1, :], AX.X, sem="aknr")
            c.wait("vector", "aknr")
            c.wait("vector", "akg", None)
            c.op("vector", "tensor_max", kmg[0:1, 0:1], kmg[0:1, 0:1], rA_row[0:1, 0:1], sem="aknv")
            c.wait("vector", "aknv")
            c.op("vector", "tensor_max", kmg[0:1, 0:1], kmg[0:1, 0:1], rA_row[0:1, 1:2], sem="aknv")
    c.wait_all("aknv")
    c.wait_all("akn")
    c.wait_all("aks")

    for qt in range(NTT):
        sl = slice(qt * TT, (qt + 1) * TT)
        c.wait("vector", "aqm", None)
        c.wait("vector", "aom", None)
        c.wait("tensor", "zbc", None)
        c.wait("tensor", "aav", None)
        c.wait("tensor", "alv", None)
        c.wait("tensor", "aov", None)
        c.wait("sync", "aov", None)
        c.wait("sync", "ares0", None)
        c.wait("sync", "ares1", None)
        emit_z_tile_stream(c, P["xT"], rowA, rowB, qt, zq, bankA=4, bankB=5)
        c.wait("scalar", "asc", None)
        for fc in range(DC):
            pi = fc % 2
            qbank = c.pb[0 + pi]
            if fc == 0:
                c.wait("sync", "avm", None)
                c.wait("sync", "aom", None)
            c.wait("sync", "aqm", c.n("aqm") - 1)
            load_wtile(c, "sync", wb[pi], P["qw"], fc, DC, f"aqw{pi}")
            c.wait("tensor", f"aqw{pi}")
            if fc == 0:
                c.wait("tensor", "zbc")
            c.wait("tensor", "aqev", c.n("aqev") - 1)
            last = (P.get("qb_sb") is None)
            for cc in range(DC):
                c.op("tensor", "matmul", qbank[:, :], wb[pi][:, cc * 128:(cc + 1) * 128],
                     zq[:, cc * TT:(cc + 1) * TT], start=(cc == 0),
                     stop=(cc == DC - 1 and last),
                     sem=("aqm" if (cc == DC - 1 and last) else None))
            if P.get("qb_sb") is not None:
                c.op("tensor", "matmul", qbank[:, :], P["qb_sb"][0:1, fc * 128:(fc + 1) * 128],
                     c.tiles["ones_rowb"][0:1, :], start=False, stop=True, sem="aqm")
            c.wait("scalar", "aqm")
            c.op("scalar", "activation", Qsb[:, fc * TT:(fc + 1) * TT], qbank[:, :],
                 AF.Copy, sem="aqev", scale=P["scale"])
        if rope is not None:
            tabC = c.sb("ln_sq", [128, 2 * TT], BF16)[:, 0:TT]
            tabS = c.sb("ln_sq", [128, 2 * TT], BF16)[:, TT:2 * TT]
            c.wait("vector", "aqev")
            c.wait("vector", "aqn", None)
            c.wait("tensor", "aqn", None)
            c.wait("sync", "aqr", None)
            c.dma("sync", tabC, rope[0, :, sl], "artl")
            c.dma("sync", tabS, rope[1, :, sl], "artl")
            c.wait("vector", "artl")
            c.wait("tensor", "artl")
            c.wait("tensor", "aqev", None)
            for fc in range(DC):
                bank = 2 + fc % 2
                c.wait("tensor", "aqr", c.n("aqr") - 3)
                rope_apply(c, Qsb[:, fc * TT:(fc + 1) * TT], tabC, tabS, bank, "aqrm", "aqr")
            c.wait("scalar", "aqr")
        else:
            c.wait("scalar", "aqev")
        for fc in range(DC):
            c.wait("scalar", "aqn", None)
            c.op("scalar", "activation", qsq[:, :], Qsb[:, fc * TT:(fc + 1) * TT], AF.Square, sem="aqs")
            c.wait("tensor", "aqs")
            c.wait("tensor", "aqnr", None)
            c.op("tensor", "matmul", c.pb[2][0:1, :], hsel[:, 0:1], qsq[:, :], start=True, stop=True)
            c.op("tensor", "matmul", c.pb[3][0:1, :], hsel[:, 1:2], qsq[:, :], sem="aqn",
                 start=True, stop=True)
            c.wait("vector", "aqn")
            c.op("vector", "reduce_max", qmx[0:1, 2 * fc:2 * fc + 1], c.pb[2][0:1, :], AX.X)
            c.op("vector", "reduce_max", qmx[0:1, 2 * fc + 1:2 * fc + 2], c.pb[3][0:1, :], AX.X,
                 sem="aqnr")
        c.wait("scalar", "aqnr")
        c.op("scalar", "activation", crow[0:1, 0:16], qmx[0:1, 0:16], AF.Ln, sem="acr")
        c.wait("scalar", "acr")
        c.op("scalar", "activation", crow[0:1, 0:16], crow[0:1, 0:16], AF.Exp, sem="acr", scale=0.5)
        c.op("scalar", "activation", kmg[0:1, 1:2], kmg[0:1, 0:1], AF.Ln, sem="acr")
        c.wait("scalar", "acr")
        c.op("scalar", "activation", kmg[0:1, 1:2], kmg[0:1, 1:2], AF.Exp, sem="acr", scale=0.5)
        c.wait("vector", "acr")
        c.op("vector", "tensor_scalar", crow[0:1, 0:16], crow[0:1, 0:16], kmg[0:1, 1:2], -1.0,
             ALU.mult, ALU.mult, sem="acv")
        c.wait("tensor", "acv")
        c.wait("tensor", "alv", None)
        c.wait("tensor", "abv", None)
        for h in range(16):
            c.op("tensor", "matmul", c.pb[6][:, h:h + 1], onesr[0:1, :], crow[0:1, h:h + 1],
                 sem=("abm" if h == 15 else None), start=True, stop=True)
        c.wait("vector", "abm")
        c.op("vector", "tensor_copy", biasc[:, 0:16], c.pb[6][:, 0:16], sem="abv")
        c.wait("tensor", "abv")
        c.wait("tensor", "aqnr", None)
        nkc = 4 * qt + 4
        for hp in range(DC):
            hA, hB = 2 * hp, 2 * hp + 1
            c.wait("tensor", "aav", None)
            c.wait("tensor", "alv", None)
            for kc in range(nkc):
                pi = kc % 2
                sA = c.pb[0 + pi]
                sB = c.pb[2 + pi]
                if kc == 0:
                    c.wait("tensor", "arb", None)
                c.wait("tensor", "aex", c.n("aex") - 1)
                c.op("tensor", "matmul", sA[:, :],
                     Ksb[0:64, hp * T + kc * 128:hp * T + (kc + 1) * 128],
                     Qsb[0:64, hp * TT:(hp + 1) * TT], start=True, stop=True,
                     tile_position=(0, 0))
                c.op("tensor", "matmul", sB[:, :],
                     Ksb[64:128, hp * T + kc * 128:hp * T + (kc + 1) * 128],
                     Qsb[64:128, hp * TT:(hp + 1) * TT], sem="asc", start=True, stop=True,
                     tile_position=(64, 0))
                c.wait("scalar", "asc")
                c.wait("scalar", "aeu", c.n("aeu") - 1)
                eA = Ebuf[:, (pi * 2) * TT:(pi * 2 + 1) * TT]
                eB = Ebuf[:, (pi * 2 + 1) * TT:(pi * 2 + 2) * TT]
                c.op("scalar", "activation", eA, sA[:, :], AF.Exp, bias=biasc[:, hA:hA + 1])
                c.op("scalar", "activation", eB, sB[:, :], AF.Exp, bias=biasc[:, hB:hB + 1],
                     sem="aex")
                r = kc - 4 * qt
                if r >= 0:
                    c.wait("vector", "aex")
                    c.op("vector", "tensor_tensor", eA, eA, masks[:, r * 512:(r + 1) * 512], ALU.mult)
                    c.op("vector", "tensor_tensor", eB, eB, masks[:, r * 512:(r + 1) * 512], ALU.mult,
                         sem="aem")
                    c.wait("tensor", "aem")
                else:
                    c.wait("tensor", "aex")
                xoff = kc * 16 + hp * 2
                c.op("tensor", "matmul", c.pb[4][0:65, :], Vv[:, xoff, :], eA,
                     start=(kc == 0), stop=(kc == nkc - 1))
                c.op("tensor", "matmul", c.pb[5][0:65, :], Vv[:, xoff + 1, :], eB,
                     start=(kc == 0), stop=(kc == nkc - 1), sem="aeu")
            c.wait("vector", "aeu")
            c.op("vector", "reciprocal", rA_row[0:1, :], c.pb[4][64:65, :], sem="alv")
            c.wait("vector", "alv")
            c.op("vector", "reciprocal", rB_row[0:1, :], c.pb[5][64:65, :], sem="alv")
            c.wait("tensor", "alv")
            c.wait("tensor", "aex", None)
            c.op("tensor", "matmul", c.pb[2][0:64, :], onesr[0:1, 0:64], rA_row[0:1, :],
                 start=True, stop=True, tile_position=(0, 0))
            c.op("tensor", "matmul", c.pb[3][0:64, :], onesr[0:1, 0:64], rB_row[0:1, :],
                 sem="arm", start=True, stop=True, tile_position=(0, 0))
            c.wait("vector", "arm")
            c.op("vector", "tensor_copy", rbc[0:64, :], c.pb[2][0:64, :])
            c.op("vector", "tensor_copy", rbc2[0:64, :], c.pb[3][0:64, :], sem="arb")
            c.wait("vector", "arb")
            c.op("vector", "tensor_tensor", att[0:64, hp * TT:(hp + 1) * TT],
                 c.pb[4][0:64, :], rbc[0:64, :], ALU.mult)
            c.op("vector", "tensor_tensor", att[64:128, hp * TT:(hp + 1) * TT],
                 c.pb[5][0:64, :], rbc2[0:64, :], ALU.mult, sem="aav")
        c.wait("tensor", "aex", None)
        c.wait("vector", "aav")
        xc_in = chunked(P["xT"])
        yc_out = chunked(P["yT"])
        for oc in range(DC):
            pi = oc % 2
            obank = c.pb[0 + pi]
            if oc == 0:
                c.wait("sync", "aqm", None)
                c.wait("sync", "zbc", None)
            c.wait("sync", "aom", c.n("aom") - 1)
            load_wtile(c, "sync", wb[pi], P["ow"], oc, DC, f"aow{pi}")
            c.wait("sync", f"ares{pi}", None)
            c.dma("sync", rxb[:, pi * TT:(pi + 1) * TT], xc_in[:, oc, sl], f"arxl{pi}")
            c.wait("tensor", f"aow{pi}")
            if oc == 0:
                c.wait("tensor", "aav")
            c.wait("tensor", "aov", c.n("aov") - 1)
            last = (P.get("ob_sb") is None)
            for cc in range(DC):
                c.op("tensor", "matmul", obank[:, :], wb[pi][:, cc * 128:(cc + 1) * 128],
                     att[:, cc * TT:(cc + 1) * TT], start=(cc == 0),
                     stop=(cc == DC - 1 and last),
                     sem=("aom" if (cc == DC - 1 and last) else None))
            if P.get("ob_sb") is not None:
                c.op("tensor", "matmul", obank[:, :], P["ob_sb"][0:1, oc * 128:(oc + 1) * 128],
                     c.tiles["ones_rowb"][0:1, :], start=False, stop=True, sem="aom")
            c.wait("vector", "aom")
            c.wait("vector", f"arxl{pi}")
            c.op("vector", "tensor_tensor", rxb[:, pi * TT:(pi + 1) * TT],
                 rxb[:, pi * TT:(pi + 1) * TT], obank[:, :], ALU.add, sem="aov")
            c.wait("sync", "aov")
            c.dma("sync", yc_out[:, oc, sl], rxb[:, pi * TT:(pi + 1) * TT], f"ares{pi}")
    c.wait_all("ares0")
    c.wait_all("ares1")



def build_step_l0_ffn():
    nc = bass.Bass("TRN2", target_bir_lowering=False)
    xT = nc.dram_tensor("xT", [D, T], F32, kind="ExternalInput").ap()
    wg = nc.dram_tensor("wg", [3072, 1024], I32, kind="ExternalInput").ap()
    wu = nc.dram_tensor("wu", [3072, 1024], I32, kind="ExternalInput").ap()
    wd = nc.dram_tensor("wd", [1024, 3072], I32, kind="ExternalInput").ap()
    yT = nc.dram_tensor("yT", [D, T], F32, kind="ExternalOutput").ap()
    wg_bf = nc.dram_tensor("wg_bf", [3072, 1024], BF16, kind="Internal").ap()
    wu_bf = nc.dram_tensor("wu_bf", [3072, 1024], BF16, kind="Internal").ap()
    wd_bf = nc.dram_tensor("wd_bf", [1024, 3072], BF16, kind="Internal").ap()
    wgT = nc.dram_tensor("wgT", [1024, 3072], BF16, kind="Internal").ap()
    wuT = nc.dram_tensor("wuT", [1024, 3072], BF16, kind="Internal").ap()
    wdT = nc.dram_tensor("wdT", [3072, 1024], BF16, kind="Internal").ap()

    def setup(c):
        alloc_consts(nc, c)
        emit_load_consts(c)
    ems = [setup,
           build_l0_ffn_pre(nc, wg, wu, wd, wgT, wuT, wdT, wg_bf, wu_bf, wd_bf),
           build_l0_ffn(nc, xT, yT, wgT, wuT, wdT)]
    run_ctx(nc, ems)
    return nc


def build_step_l0_attn_a():
    nc = bass.Bass("TRN2", target_bir_lowering=False)
    xT = nc.dram_tensor("xT", [D, T], F32, kind="ExternalInput").ap()
    qw = nc.dram_tensor("qw", [1024, 1024], F32, kind="ExternalInput").ap()
    kw = nc.dram_tensor("kw", [1024, 1024], F32, kind="ExternalInput").ap()
    vw = nc.dram_tensor("vw", [1024, 1024], F32, kind="ExternalInput").ap()
    ow = nc.dram_tensor("ow", [1024, 1024], F32, kind="ExternalInput").ap()
    yT = nc.dram_tensor("yT", [D, T], F32, kind="ExternalOutput").ap()
    qwp = nc.dram_tensor("qwp", [8, 128, 1024], BF16, kind="Internal").ap()
    kwp = nc.dram_tensor("kwp", [8, 128, 1024], BF16, kind="Internal").ap()
    vwp = nc.dram_tensor("vwp", [8, 128, 1024], BF16, kind="Internal").ap()
    owp = nc.dram_tensor("owp", [8, 128, 1024], BF16, kind="Internal").ap()

    def setup(c):
        alloc_consts(nc, c)
        emit_load_consts(c)
        emit_permute_qkv_pre(c, qw, qwp)
        emit_permute_qkv_pre(c, kw, kwp)
        emit_permute_qkv_pre(c, vw, vwp)
        emit_gather_ow_pre(c, ow, owp)

    P = {"xT": xT, "yT": yT, "qw": qwp, "kw": kwp, "vw": vwp, "ow": owp,
         "scale": 1.0 / 32.0, "rope": None}
    ems = [setup, lambda c: emit_causal_attn(c, P)]
    run_ctx(nc, ems)
    return nc



def moe_sel_np():
    s = np.zeros((8, 8 * 128), np.float32)
    for e in range(8):
        s[e, e * 128:(e + 1) * 128] = 1.0
    return s.astype(ml_dtypes.bfloat16)


def emit_moe(c, P):
    nc = c.nc
    E = 8
    H = 2048
    FCH = H // 128
    NTP = 2
    NPS = NTT // NTP
    W = NTP * TT
    rowA = c.sb("rowA", [1, T], BF16)
    rowB = c.sb("rowB", [1, T], BF16)
    zq = c.sb("mo_z", [128, DC * W], BF16)
    xacc = c.sb("mo_xacc", [128, DC * W], F32)
    hset = c.sb("mo_h", [128, FCH * W], BF16)
    h1 = c.sb("mo_h1", [128, 2 * W], BF16)
    dtmp = c.sb("mo_dt", [128, 2 * W], F32)
    gebc = c.sb("mo_ge", [128, W], F32)
    rsb = c.sb("mo_rsb", [8, 1024], F32)
    routerT = c.sb("mo_routerT", [128, DC * 8], BF16)
    lgf = c.sb("mo_lgf", [8, W], F32)
    lgtm = c.sb("mo_lgtm", [128, 8 * 8], F32)
    gts = c.sb("mo_gts", [128, 64], F32)
    gesb = c.sb("mo_gesb", [8, W], BF16)
    selsb = c.sb("mo_sel", [8, 8 * 128], BF16)
    wgb = [c.sb(f"mo_wgb{i}", [128, DC * 128], BF16) for i in range(2)]
    wub = [c.sb(f"mo_wub{i}", [128, DC * 128], BF16) for i in range(2)]
    wdb = [c.sb(f"mo_wdb{i}", [128, FCH * 128], BF16) for i in range(2)]
    ident = c.tiles["ident"]

    emit_ln_stats(c, P["xT"], rowA, rowB)

    c.dma("sync", rsb[:, :], P["router"], "mrl")
    c.dma("sync", selsb[:, :], c.tiles["moe_sel_dram"], "mrl")
    c.wait("tensor", "mrl")
    idf = c.sb("mo_idf", [128, 128], F32)
    c.op("vector", "tensor_copy", idf[:, :], ident[:, :], sem="mid")
    c.wait("tensor", "mid")
    for cc in range(DC):
        c.wait("tensor", "mrc", None)
        c.op("tensor", "transpose", c.pb[0][:, 0:8], rsb[:, cc * 128:(cc + 1) * 128],
             idf[0:8, 0:8], sem="mrt")
        c.wait("vector", "mrt")
        c.op("vector", "tensor_copy", routerT[:, cc * 8:(cc + 1) * 8], c.pb[0][:, 0:8], sem="mrc")
    c.wait_all("mrc")

    for ps in range(NPS):
        sl = slice(ps * W, (ps + 1) * W)
        c.wait("vector", "mgum", None)
        c.wait("vector", "mdnm", None)
        c.wait("tensor", "zbc", None)
        c.wait("tensor", "msil", None)
        c.wait("tensor", "mhev", None)
        c.wait("tensor", "mgec2", None)
        for t in range(NTP):
            if t > 0:
                c.wait("tensor", "zbc", None)
            emit_z_tile_stream(c, P["xT"], rowA, rowB, ps * NTP + t, zq,
                               bankA=4, bankB=5, zstride=W, zoff=t * TT)
        c.wait("sync", "zbc", None)
        c.wait("sync", "mres", None)
        load_chunks(c, "sync", xacc, P["xT"], sl, DC, "mxl")
        c.wait("tensor", "mrc", None)
        c.wait("tensor", "mlg", None)
        c.wait("tensor", "mdnv", None)
        c.wait("tensor", "zbc")
        for t in range(NTP):
            for cc in range(DC):
                c.op("tensor", "matmul", c.pb[t][0:8, :], routerT[:, cc * 8:(cc + 1) * 8],
                     zq[:, cc * W + t * TT:cc * W + (t + 1) * TT],
                     start=(cc == 0), stop=(cc == DC - 1),
                     sem=("mlgm" if (cc == DC - 1 and t == NTP - 1) else None))
        c.wait("vector", "mlgm")
        for t in range(NTP):
            c.op("vector", "tensor_copy", lgf[:, t * TT:(t + 1) * TT], c.pb[t][0:8, :],
                 sem=("mlg" if t == NTP - 1 else None))
        c.wait("tensor", "mlg")
        c.wait("tensor", "mlgc", None)
        for g in range(NTP * 4):
            c.op("tensor", "transpose", c.pb[2][:, g * 8:(g + 1) * 8],
                 lgf[:, g * 128:(g + 1) * 128], idf[0:8, 0:8],
                 sem=("mlgt" if g == NTP * 4 - 1 else None))
        c.wait("vector", "mlgt")
        c.op("vector", "tensor_copy", lgtm[:, :], c.pb[2][:, 0:NTP * 4 * 8], sem="mlgc")
        c.wait("vector", "mlgc")
        c.wait("vector", "mgb", None)
        BIG = 1.0e30
        for gh in range(NTP):
            for g4 in range(4):
                g = gh * 4 + g4
                lg = lgtm[:, g * 8:(g + 1) * 8]
                v1 = gts[:, 0:1]; v2 = gts[:, 1:2]; eq1 = gts[:, 8:16]; eq2 = gts[:, 16:24]
                lg2 = gts[:, 24:32]; gg = gts[:, 2:4]; srs = gts[:, 4:5]
                a1 = gts[:, 5:6]; a2 = gts[:, 6:7]; tge = gts[:, 32:40]
                c.wait("vector", "mgg", None)
                c.wait("vector", "mget", None)
                c.op("vector", "reduce_max", v1, lg, AX.X, sem="mgg")
                c.wait("vector", "mgg")
                c.op("vector", "tensor_scalar", eq1, lg, v1, None, ALU.is_equal, sem="mgg")
                c.wait("vector", "mgg")
                c.op("vector", "scalar_tensor_tensor", lg2, eq1, -BIG, lg, ALU.mult, ALU.add, sem="mgg")
                c.wait("vector", "mgg")
                c.op("vector", "reduce_max", v2, lg2, AX.X, sem="mgg")
                c.wait("vector", "mgg")
                c.op("vector", "tensor_scalar", eq2, lg, v2, None, ALU.is_equal, sem="mgg")
                c.wait("scalar", "mgg")
                c.op("scalar", "activation", gg, gts[:, 0:2], AF.Sigmoid, sem="mgs")
                c.wait("vector", "mgs")
                c.op("vector", "reduce_sum", srs, gg, AX.X, sem="mgg")
                c.wait("vector", "mgg")
                c.op("vector", "tensor_scalar", srs, srs, 1.0e-10, None, ALU.add, sem="mgg")
                c.wait("vector", "mgg")
                c.op("vector", "reciprocal", srs, srs, sem="mgg")
                c.wait("vector", "mgg")
                c.op("vector", "tensor_scalar", a1, gg[:, 0:1], srs, None, ALU.mult)
                c.op("vector", "tensor_scalar", a2, gg[:, 1:2], srs, None, ALU.mult, sem="mgg")
                c.wait("vector", "mgg")
                c.op("vector", "tensor_scalar", tge, eq1, a1, None, ALU.mult, sem="mgg")
                c.wait("vector", "mgg")
                c.op("vector", "scalar_tensor_tensor", gts[:, 40:48], eq2, a2, tge,
                     ALU.mult, ALU.add, sem="mgg")
                c.wait("tensor", "mgg")
                c.op("tensor", "transpose", c.pb[6 + gh][0:8, g4 * 128:(g4 + 1) * 128],
                     gts[:, 40:48], idf[:, :], sem="mget")
            c.wait("vector", "mget")
            c.op("vector", "tensor_copy", gesb[:, gh * TT:(gh + 1) * TT],
                 c.pb[6 + gh][0:8, 0:512], sem="mgec")
        for e in range(E):
            sil_e = c.n("msil")
            hev_e = c.n("mhev")
            gum_e = c.n("mgum")
            for fc in range(FCH):
                pi = fc % 2
                gbk = (c.pb[0 + 4 * pi], c.pb[1 + 4 * pi])
                ubk = (c.pb[2 + 4 * pi], c.pb[3 + 4 * pi])
                c.wait("sync", "mgum", c.n("mgum") - 1)
                load_wtile(c, "sync", wgb[pi], P["wg"], e * FCH + fc, DC, f"mwg{pi}")
                load_wtile(c, "sync", wub[pi], P["wu"], e * FCH + fc, DC, f"mwu{pi}")
                c.wait("tensor", f"mwg{pi}")
                c.wait("tensor", f"mwu{pi}")
                if fc == 0:
                    c.wait("tensor", "mdnv", None)
                    if e == 0:
                        c.wait("tensor", "mlg")
                        c.wait("tensor", "mlgc")
                elif fc == 1:
                    c.wait("tensor", "msil", sil_e)
                    c.wait("tensor", "mhev", hev_e)
                    c.wait("tensor", "mgec2", c.n("mgec2"))
                    if e == 0:
                        c.wait("tensor", "zbc", None)
                        c.wait("tensor", "mgec")
                else:
                    c.wait("tensor", "msil", sil_e + fc - 1)
                    c.wait("tensor", "mhev", hev_e + 2 * fc - 2)
                for t in range(NTP):
                    for cc in range(DC):
                        c.op("tensor", "matmul", gbk[t][:, :], wgb[pi][:, cc * 128:(cc + 1) * 128],
                             zq[:, cc * W + t * TT:cc * W + (t + 1) * TT],
                             start=(cc == 0), stop=(cc == DC - 1))
                for t in range(NTP):
                    for cc in range(DC):
                        c.op("tensor", "matmul", ubk[t][:, :], wub[pi][:, cc * 128:(cc + 1) * 128],
                             zq[:, cc * W + t * TT:cc * W + (t + 1) * TT],
                             sem=("mgum" if (cc == DC - 1 and t == NTP - 1) else None),
                             start=(cc == 0), stop=(cc == DC - 1))
                c.wait("scalar", "mgum", gum_e + fc + 1)
                c.wait("scalar", "mhev", hev_e + 2 * fc - 2)
                for t in range(NTP):
                    c.op("scalar", "activation",
                         h1[:, (pi * NTP + t) * TT:(pi * NTP + t + 1) * TT],
                         gbk[t][:, :], AF.Silu, sem=("msil" if t == NTP - 1 else None))
                c.wait("vector", "msil", sil_e + fc + 1)
                for t in range(NTP):
                    c.op("vector", "tensor_tensor",
                         hset[:, fc * W + t * TT:fc * W + (t + 1) * TT], ubk[t][:, :],
                         h1[:, (pi * NTP + t) * TT:(pi * NTP + t + 1) * TT],
                         ALU.mult, sem="mhev")
            c.wait("tensor", "mgec")
            c.wait("tensor", "mhev", None)
            for t in range(NTP):
                c.op("tensor", "matmul", c.pb[6 + t][:, :], selsb[:, e * 128:(e + 1) * 128],
                     gesb[:, t * TT:(t + 1) * TT], start=True, stop=True,
                     sem=("mgb" if t == NTP - 1 else None))
            c.wait("vector", "mgb")
            for t in range(NTP):
                c.op("vector", "tensor_copy", gebc[:, t * TT:(t + 1) * TT], c.pb[6 + t][:, :],
                     sem=("mgec2" if t == NTP - 1 else None))
            dn0 = c.n("mdnm")
            dv0 = c.n("mdnv")
            for dc in range(DC):
                pj = dc % 2
                dbk = (c.pb[0 + 2 * pj], c.pb[1 + 2 * pj])
                c.wait("sync", "mdnm", c.n("mdnm") - 1)
                load_wtile(c, "sync", wdb[pj], P["wd"], e * DC + dc, FCH, f"mwd{pj}")
                c.wait("tensor", f"mwd{pj}")
                if dc <= 1:
                    c.wait("tensor", "msil", None)
                    c.wait("tensor", "mhev", None)
                else:
                    c.wait("tensor", "mdnv", dv0 + 2 * dc - 2)
                for t in range(NTP):
                    for hc in range(FCH):
                        c.op("tensor", "matmul", dbk[t][:, :], wdb[pj][:, hc * 128:(hc + 1) * 128],
                             hset[:, hc * W + t * TT:hc * W + (t + 1) * TT],
                             sem=("mdnm" if (hc == FCH - 1 and t == NTP - 1) else None),
                             start=(hc == 0), stop=(hc == FCH - 1))
                c.wait("vector", "mdnm", dn0 + dc + 1)
                if dc == 0:
                    c.wait("vector", "mgec2")
                if e == 0 and dc == 0:
                    c.wait("vector", "mxl")
                for t in range(NTP):
                    c.op("vector", "tensor_tensor", dtmp[:, (pj * NTP + t) * TT:(pj * NTP + t + 1) * TT],
                         dbk[t][:, :], gebc[:, t * TT:(t + 1) * TT], ALU.mult,
                         sem=("mdge" if t == NTP - 1 else None))
                c.wait("vector", "mdge")
                for t in range(NTP):
                    c.op("vector", "tensor_tensor", xacc[:, dc * W + t * TT:dc * W + (t + 1) * TT],
                         xacc[:, dc * W + t * TT:dc * W + (t + 1) * TT],
                         dtmp[:, (pj * NTP + t) * TT:(pj * NTP + t + 1) * TT],
                         ALU.add, sem="mdnv")
        c.wait("sync", "mdnv", None)
        store_chunks(c, "sync", P["yT"], sl, xacc, DC, "mres")
    c.wait_all("mres")


def rope_table_np():
    half = 32
    inv_freq = 1.0 / 10000.0 ** (np.arange(half, dtype=np.float64) / half)
    pos = np.arange(T, dtype=np.float64)
    ang = np.outer(inv_freq, pos)
    C = np.tile(np.cos(ang), (4, 1)).astype(np.float32)
    S = np.tile(np.sin(ang), (4, 1)).astype(np.float32)
    return np.stack([C, S]).astype(ml_dtypes.bfloat16)


def rope_swap_np():
    P2 = np.zeros((128, 128), np.float32)
    for hb in (0, 64):
        for j in range(32):
            P2[hb + 32 + j, hb + j] = -1.0
            P2[hb + j, hb + 32 + j] = 1.0
    return P2.astype(ml_dtypes.bfloat16)


def build_step_l0_attn_b():
    nc = bass.Bass("TRN2", target_bir_lowering=False)
    xT = nc.dram_tensor("xT", [D, T], F32, kind="ExternalInput").ap()
    qkv_w = nc.dram_tensor("qkv_w", [3072, 1024], F32, kind="ExternalInput").ap()
    qkv_b = nc.dram_tensor("qkv_b", [3072], F32, kind="ExternalInput").ap()
    out_w = nc.dram_tensor("out_w", [1024, 1024], F32, kind="ExternalInput").ap()
    out_b = nc.dram_tensor("out_b", [1024], F32, kind="ExternalInput").ap()
    yT = nc.dram_tensor("yT", [D, T], F32, kind="ExternalOutput").ap()
    qkv_bf = nc.dram_tensor("qkv_bf", [3072, 1024], BF16, kind="Internal").ap()
    qkvT = nc.dram_tensor("qkvT", [1024, 3072], BF16, kind="Internal").ap()
    ow_bf = nc.dram_tensor("ow_bf", [1024, 1024], BF16, kind="Internal").ap()
    owT = nc.dram_tensor("owT", [1024, 1024], BF16, kind="Internal").ap()
    ropetab = nc.inline_tensor(rope_table_np(), name="ropetab").ap()
    ropeP2 = nc.inline_tensor(rope_swap_np(), name="ropeP2").ap()

    def setup(c):
        alloc_consts(nc, c)
        c.tiles["ropeP2"] = c.sb("ropeP2_sb", [128, 128], BF16)
        c.dma("sync", c.tiles["ropeP2"][:], ropeP2, "cst")
        c.wait_all("cst")
        emit_load_consts(c)
        emit_cast_dram(c, qkv_w, qkv_bf)
        emit_cast_dram(c, out_w, ow_bf)
        bias_row = c.sb("ab_bias", [1, 4096], BF16)
        c.dma("gpsimd", bias_row[0:1, 0:3072], qkv_b.rearrange("(o n) -> o n", o=1), "pcs")
        c.dma("gpsimd", bias_row[0:1, 3072:4096], out_b.rearrange("(o n) -> o n", o=1), "pcs")
        c.wait_all("pcs")
        emit_transpose_scratch(c, qkv_bf, qkvT, 3072, 1024)
        emit_transpose_scratch(c, ow_bf, owT, 1024, 1024)

    def main(c):
        bias_row = c.sb("ab_bias", [1, 4096], BF16)
        P = {"xT": xT, "yT": yT,
             "qw": qkvT[:, 0:1024], "kw": qkvT[:, 1024:2048], "vw": qkvT[:, 2048:3072],
             "ow": owT,
             "qb_sb": bias_row[:, 0:1024], "kb_sb": bias_row[:, 1024:2048],
             "vb_sb": bias_row[:, 2048:3072], "ob_sb": bias_row[:, 3072:4096],
             "scale": 1.0 / 8.0, "rope": ropetab}
        emit_causal_attn(c, P)

    run_ctx(nc, [setup, main])
    return nc


def build_step_l1_ffn():
    nc = bass.Bass("TRN2", target_bir_lowering=False)
    xT = nc.dram_tensor("xT", [D, T], F32, kind="ExternalInput").ap()
    router = nc.dram_tensor("router", [8, 1024], F32, kind="ExternalInput").ap()
    wg = nc.dram_tensor("wg", [8, 1024, 2048], F32, kind="ExternalInput").ap()
    wu = nc.dram_tensor("wu", [8, 1024, 2048], F32, kind="ExternalInput").ap()
    wd = nc.dram_tensor("wd", [8, 2048, 1024], F32, kind="ExternalInput").ap()
    yT = nc.dram_tensor("yT", [D, T], F32, kind="ExternalOutput").ap()
    wg_bf = nc.dram_tensor("wg_bf", [128, 128, 1024], BF16, kind="Internal").ap()
    wu_bf = nc.dram_tensor("wu_bf", [128, 128, 1024], BF16, kind="Internal").ap()
    wd_bf = nc.dram_tensor("wd_bf", [64, 128, 2048], BF16, kind="Internal").ap()
    sel = nc.inline_tensor(moe_sel_np(), name="moe_sel").ap()

    def setup(c):
        alloc_consts(nc, c)
        c.tiles["moe_sel_dram"] = sel
        emit_load_consts(c)
        for wnm, src in [(wg_bf, wg), (wu_bf, wu), (wd_bf, wd)]:
            srcv = src.rearrange("e (cc p) (fc w) -> e fc p cc w", p=128, w=128)
            nfc = srcv.shape[1]
            for e in range(8):
                for fc in range(nfc):
                    c.dma("gpsimd", wnm[e * nfc + fc].rearrange("p (cc w) -> p cc w", w=128),
                          srcv[e, fc], "pcs")
        c.wait_all("pcs")

    P = {"xT": xT, "yT": yT, "router": router, "wg": wg_bf, "wu": wu_bf, "wd": wd_bf}
    run_ctx(nc, [setup, lambda c: emit_moe(c, P)])
    return nc



def emit_perf(c, P):
    nc = c.nc
    H = 16
    M = 256
    FFC = 32
    scale = float(64) ** (-0.25)
    LN16 = float(np.log(16.0))
    rowA = c.sb("rowA", [1, T], BF16)
    rowB = c.sb("rowB", [1, T], BF16)
    zq = c.sb("pf_z", [128, DC * TT], BF16)
    qfm = c.sb("pf_qfm", [128, DC * TT], BF16)
    qtm = c.sb("pf_qtm", [128, 4 * 1024], BF16)
    vext = c.sb("pf_vext", [128, 4 * 1024], BF16)
    sqc = c.sb("pf_sqc", [128, 4 * 16], F32)
    pq = c.sb("pf_pq", [128, 256], BF16)
    kv = c.sb("pf_kv", [128, 2048], BF16)
    zb = c.sb("pf_zb", [1, 4096], F32)
    zcol = c.sb("pf_zcol", [128, 32], BF16)
    att = c.sb("pf_att", [128, DC * TT], BF16)
    vbuf = c.sb("pf_v", [128, DC * TT], F32)
    ubuf = c.sb("pf_u", [128, DC * TT], BF16)
    hh = c.sb("pf_hh", [128, FFC * TT], BF16)
    omt = c.sb("pf_omt", [128, 256], BF16)
    bcol = c.sb("pf_bcol", [128, 4], F32)
    riA = c.sb("pf_riA", [1, TT], F32)
    riB = c.sb("pf_riB", [1, TT], F32)
    gvec = c.sb("pf_gvec", [128, 4 * DC], F32)
    b1v = c.sb("pf_b1v", [128, FFC], F32)
    b2r = c.sb("pf_b2r", [1, 1024], BF16)
    wb = [c.sb(f"pf_wb{i}", [128, DC * 128], BF16) for i in range(2)]
    w2b = [c.sb(f"pf_w2b{i}", [128, FFC * 128], BF16) for i in range(2)]
    vwb = c.sb("pf_vwb", [128, DC * 512], BF16)
    rxb = c.sb("ln_xb", [128, 2 * TT], F32)
    onesr = c.tiles["ones_row"]
    r1 = c.sb("ln_r1", [1, TT], F32)
    r2 = c.sb("ln_r2", [1, TT], F32)
    t1s = c.sb("z_t1", [128, TT], F32)

    emit_ln_stats(c, P["xT"], rowA, rowB)

    c.dma("gpsimd", b2r[0:1, :], P["b2"].rearrange("(o n) -> o n", o=1), "pol")
    emit_vec_rows(c, P["ln1_g"], gvec[:, 0 * DC:1 * DC], 1024)
    emit_vec_rows(c, P["ln1_b"], gvec[:, 1 * DC:2 * DC], 1024)
    emit_vec_rows(c, P["ln2_g"], gvec[:, 2 * DC:3 * DC], 1024)
    emit_vec_rows(c, P["ln2_b"], gvec[:, 3 * DC:4 * DC], 1024)
    emit_vec_rows(c, P["b1"], b1v[:, 0:FFC], 4096)
    c.wait_all("pol")

    def fm_proj(w_ap, out_sb, s, chainsem, evsem):
        for fc in range(DC):
            pi = fc % 2
            bank = c.pb[4 + pi]
            c.wait("sync", chainsem, c.n(chainsem) - 1)
            load_wtile(c, "sync", wb[pi], w_ap, fc, DC, f"pkw{pi}")
            c.wait("tensor", f"pkw{pi}")
            if fc == 0:
                c.wait("tensor", "zbc")
            c.wait("tensor", evsem, c.n(evsem) - 1)
            for cc in range(DC):
                c.op("tensor", "matmul", bank[:, :], wb[pi][:, cc * 128:(cc + 1) * 128],
                     zq[:, cc * TT:(cc + 1) * TT], start=(cc == 0), stop=(cc == DC - 1),
                     sem=(chainsem if cc == DC - 1 else None))
            c.wait("scalar", chainsem)
            c.op("scalar", "activation", out_sb[:, fc * TT:(fc + 1) * TT], bank[:, :],
                 AF.Copy, sem=evsem, scale=s)

    def tm_proj(w_ap, out_sb, s):
        c.wait("tensor", "pkev", None)
        for half in range(2):
            c.wait("sync", "pvm", None)
            load_vtile(c, "sync", vwb, w_ap, half * 4, 4, DC, "pvw")
            c.wait("tensor", "pvw")
            for kb in range(4):
                bank = c.pb[4 + kb % 2]
                c.wait("tensor", "pvev", c.n("pvev") - 1)
                for cc in range(DC):
                    c.op("tensor", "matmul", bank[:, :],
                         zq[:, cc * TT + kb * 128:cc * TT + (kb + 1) * 128],
                         vwb[:, cc * 512:(cc + 1) * 512],
                         start=(cc == 0), stop=(cc == DC - 1),
                         sem=("pvm" if cc == DC - 1 else None))
                c.wait("scalar", "pvm")
                c.op("scalar", "activation", out_sb[:, kb * 1024 + half * 512:
                                                    kb * 1024 + (half + 1) * 512],
                     bank[:, :], AF.Copy, sem="pvev", scale=s)

    def sq_cols(src_sb, scr_off):
        for kb in range(4):
            c.wait("scalar", "phsq", None)
            c.wait("scalar", "psqr", None)
            c.wait("scalar", "pwom", None)
            c.op("scalar", "activation", att[:, scr_off:scr_off + 1024],
                 src_sb[:, kb * 1024:(kb + 1) * 1024], AF.Square, sem="phsq")
            c.wait("vector", "phsq")
            for h in range(H):
                c.op("vector", "reduce_sum", sqc[:, kb * 16 + h:kb * 16 + h + 1],
                     att[:, scr_off + h * 64:scr_off + (h + 1) * 64], AX.X,
                     sem=("psqr" if h == H - 1 else None))
            c.wait("vector", "psqr")
            c.op("vector", "tensor_scalar", sqc[:, kb * 16:(kb + 1) * 16],
                 sqc[:, kb * 16:(kb + 1) * 16], 0.5, LN16, ALU.mult, ALU.add, sem="psqs")

    def phi_head(kb, h, consume):
        hc = h // 2
        hb = (h % 2) * 64
        c.wait("tensor", "pex", None)
        c.wait("tensor", "pzv", None)
        c.wait("tensor", "pkev", None)
        c.wait("tensor", "ptqc", None)
        c.op("tensor", "matmul", c.pb[6][:, 0:256],
             qfm[hb:hb + 64, hc * TT + kb * 128:hc * TT + (kb + 1) * 128],
             omt[hb:hb + 64, :], start=True, stop=True, sem="ppj")
        c.wait("vector", "ppj")
        c.op("vector", "reduce_max", bcol[:, 0:1], c.pb[6][:, 0:256], AX.X, sem="pst")
        c.wait("vector", "pst")
        c.wait("vector", "psqs", None)
        c.op("vector", "tensor_tensor", bcol[:, 1:2], bcol[:, 0:1],
             sqc[:, kb * 16 + h:kb * 16 + h + 1], ALU.add, sem="pst")
        c.wait("vector", "pst")
        c.op("vector", "tensor_scalar", bcol[:, 2:3], bcol[:, 1:2], -1.0, None,
             ALU.mult, sem="pst")
        c.wait("scalar", "pst")
        c.wait("scalar", "ppk", None)
        c.op("scalar", "activation", pq[:, :], c.pb[6][:, 0:256], AF.Exp,
             bias=bcol[:, 2:3], sem="pex")
        c.wait("vector", "pex")
        c.op("vector", "tensor_scalar", pq[:, :], pq[:, :], 1.0e-6, None,
             ALU.add, sem="ppk")
        consume(kb, h)

    c.op("vector", "memset", zb[:, :], 0.0, sem="pzb")

    def consume_k(kb, h):
        c.wait("tensor", "ppk")
        first = (c._tt == 0 and kb == 0)
        lastk = (c._tt == NTT - 1 and kb == 3)
        for mc in range(2):
            slot = mc * 16 + h
            bk = c.pb[slot // 8]
            off = (slot % 8) * 64
            c.op("tensor", "matmul", bk[:, off:off + 64],
                 pq[:, mc * 128:(mc + 1) * 128], vext[:, kb * 1024 + h * 64:kb * 1024 + (h + 1) * 64],
                 start=(first and slot % 8 == 0), stop=lastk, skip_group_check=True)
        c.op("tensor", "matmul", c.pb[6][0:1, 256:512], c.tiles["ones_colb"][:, 0:1],
             pq[:, :], start=True, stop=True, sem="pzm")
        c.wait("vector", "pzm")
        c.wait("vector", "pzb", None)
        c.op("vector", "tensor_tensor", zb[0:1, h * 256 + 0:h * 256 + 256],
             zb[0:1, h * 256 + 0:h * 256 + 256], c.pb[6][0:1, 256:512], ALU.add, sem="pzv")

    for tt in range(NTT):
        c._tt = tt
        c.wait("vector", "pchain", None)
        c.wait("vector", "pvm", None)
        c.wait("tensor", "zbc", None)
        c.wait("tensor", "pvev", None)
        c.wait("tensor", "pkev", None)
        emit_z_tile_stream(c, P["xT"], rowA, rowB, tt, zq, bankA=4, bankB=5)
        fm_proj(P["wk"], qfm, scale, "pchain", "pkev")
        tm_proj(P["wk"], qtm, scale)
        tm_proj(P["wv"], vext, 1.0)
        sq_cols(qtm, 0)
        for kb in range(4):
            for h in range(H):
                phi_head(kb, h, consume_k)
    c.wait_all("pzv")
    c.wait_all("ppk")
    c.wait("vector", "pex", None)
    for b in range(4):
        c.op("vector", "tensor_copy", kv[:, b * 512:(b + 1) * 512], c.pb[b][:, :], sem="pkvc")
    c.wait_all("pkvc")
    c.wait("tensor", "pkvc")
    c.wait("tensor", "pzv", None)
    for mc in range(2):
        for h in range(16):
            c.op("tensor", "matmul", c.pb[0][:, mc * 16 + h:mc * 16 + h + 1],
                 zb[0:1, h * 256 + mc * 128:h * 256 + (mc + 1) * 128],
                 onesr[0:1, 0:1], start=True, stop=True,
                 sem=("pzc" if (mc == 1 and h == 15) else None))
    c.wait("vector", "pzc")
    c.op("vector", "tensor_copy", zcol[:, 0:32], c.pb[0][:, 0:32], sem="pzcc")
    c.wait_all("pzcc")

    def consume_q(kb, h):
        c.wait("tensor", "ppk")
        c.wait("tensor", "ptqc", None)
        for mc in range(2):
            c.op("tensor", "transpose", c.pbt[:, mc * 128:(mc + 1) * 128],
                 pq[:, mc * 128:(mc + 1) * 128], c.tiles["ident"][:, :],
                 sem=("ptq" if mc == 1 else None))
        c.wait("vector", "ptq")
        for mc in range(2):
            c.op("vector", "tensor_copy",
                 hh[:, h * 1024 + mc * TT + kb * 128:h * 1024 + mc * TT + (kb + 1) * 128],
                 c.pbt[:, mc * 128:(mc + 1) * 128],
                 sem=("ptqc" if mc == 1 else None))

    for qt in range(NTT):
        c._tt = qt
        sl = slice(qt * TT, (qt + 1) * TT)
        c.wait("vector", "pchain", None)
        c.wait("vector", "pvm", None)
        c.wait("tensor", "zbc", None)
        c.wait("tensor", "presv", None)
        c.wait("tensor", "pvev", None)
        c.wait("tensor", "pkev", None)
        c.wait("tensor", "pgel", None)
        c.wait("sync", "pres0", None)
        c.wait("sync", "pres1", None)
        c.wait("sync", "pmlm", None)
        emit_z_tile_stream(c, P["xT"], rowA, rowB, qt, zq, bankA=4, bankB=5)
        fm_proj(P["wq"], qfm, scale, "pchain", "pkev")
        tm_proj(P["wq"], qtm, scale)
        sq_cols(qtm, 0)
        c.wait("vector", "pw2m", None)
        for kb in range(4):
            for h in range(H):
                phi_head(kb, h, consume_q)
        c.wait("tensor", "ptqc")
        c.wait("tensor", "pvev", None)
        c.wait("tensor", "pkev", None)
        for h in range(H):
            hb = (h % 2) * 64
            hc = h // 2
            c.wait("tensor", "patt", c.n("patt") - 1)
            for mc in range(2):
                c.op("tensor", "matmul", c.pb[6][hb:hb + 64, :],
                     kv[:, (mc * 16 + h) * 64:(mc * 16 + h + 1) * 64],
                     hh[:, h * 1024 + mc * TT:h * 1024 + (mc + 1) * TT],
                     start=(mc == 0), stop=(mc == 1), tile_position=(0, hb),
                     skip_group_check=True)
            for mc in range(2):
                c.op("tensor", "matmul", c.pb[5][0:1, :],
                     zcol[:, mc * 16 + h:mc * 16 + h + 1],
                     hh[:, h * 1024 + mc * TT:h * 1024 + (mc + 1) * TT],
                     start=(mc == 0), stop=(mc == 1),
                     sem=("pden" if mc == 1 else None))
            c.wait("vector", "pden")
            c.op("vector", "reciprocal", r1[0:1, :], c.pb[5][0:1, :], sem="pdr")
            c.wait("tensor", "pdr")
            c.op("tensor", "matmul", c.pb[4][hb:hb + 64, :], onesr[0:1, 0:64],
                 r1[0:1, :], start=True, stop=True, tile_position=(0, hb), sem="pdb")
            c.wait("scalar", "pdb")
            c.op("scalar", "activation", att[hb:hb + 64, hc * TT:(hc + 1) * TT],
                 c.pb[6][hb:hb + 64, :], AF.Copy, sem="pnev")
            c.wait("vector", "pnev")
            c.op("vector", "tensor_tensor", att[hb:hb + 64, hc * TT:(hc + 1) * TT],
                 att[hb:hb + 64, hc * TT:(hc + 1) * TT], c.pb[4][hb:hb + 64, :],
                 ALU.mult, sem="patt")
        c.wait("tensor", "patt", None)
        c.wait("sync", "pchain", None)
        c.wait("sync", "pvm", None)
        for oc in range(DC):
            pi = oc % 2
            bank = c.pb[4 + pi]
            c.wait("sync", "pwom", c.n("pwom") - 1)
            load_wtile(c, "sync", wb[pi], P["wo"], oc, DC, f"pkw{pi}")
            c.wait("tensor", f"pkw{pi}")
            c.wait("tensor", "pvoev", c.n("pvoev") - 1)
            for cc in range(DC):
                c.op("tensor", "matmul", bank[:, :], wb[pi][:, cc * 128:(cc + 1) * 128],
                     att[:, cc * TT:(cc + 1) * TT], start=(cc == 0), stop=(cc == DC - 1),
                     sem=("pwom" if cc == DC - 1 else None))
            c.wait("vector", "pwom")
            c.op("vector", "tensor_tensor", vbuf[:, oc * TT:(oc + 1) * TT],
                 zq[:, oc * TT:(oc + 1) * TT], bank[:, :], ALU.add, sem="pvoev")

        def inner_ln(src, dst, gcol0, bcol0, ssem, rdy):
            c.wait("scalar", rdy, None)
            c.wait("tensor", rdy, None)
            c.wait("vector", rdy, None)
            c.wait("tensor", ssem + "b", None)
            c.wait("tensor", "pl1v", None)
            c.wait("tensor", "pl2v", None)
            for cc in range(DC):
                c.wait("scalar", ssem + "q", None)
                c.wait("scalar", ssem + "s", None)
                c.op("scalar", "activation", t1s[:, :], src[:, cc * TT:(cc + 1) * TT],
                     AF.Square, sem=ssem + "s")
                c.wait("tensor", ssem + "s")
                c.op("tensor", "matmul", c.pb[0][0:1, :], c.tiles["ones_col"][:, 0:1],
                     src[:, cc * TT:(cc + 1) * TT], start=(cc == 0), stop=(cc == DC - 1),
                     sem=(ssem + "m" if cc == DC - 1 else None))
                c.op("tensor", "matmul", c.pb[1][0:1, :], c.tiles["ones_col"][:, 0:1],
                     t1s[:, :], start=(cc == 0), stop=(cc == DC - 1),
                     sem=ssem + "q")
            c.wait("vector", ssem + "m")
            c.wait("vector", ssem + "q")
            inv = 1.0 / D
            c.op("vector", "tensor_scalar", riA[0:1, :], c.pb[0][0:1, :], inv, None, ALU.mult)
            c.op("vector", "tensor_scalar", riB[0:1, :], c.pb[1][0:1, :], inv, None,
                 ALU.mult, sem=ssem + "v")
            c.wait("vector", ssem + "v")
            c.op("vector", "tensor_tensor", r2[0:1, :], riA[0:1, :], riA[0:1, :],
                 ALU.mult, sem=ssem + "v")
            c.wait("vector", ssem + "v")
            c.op("vector", "tensor_tensor", riB[0:1, :], riB[0:1, :], r2[0:1, :],
                 ALU.subtract, sem=ssem + "v")
            c.wait("vector", ssem + "v")
            c.op("vector", "tensor_scalar", riB[0:1, :], riB[0:1, :], 1.0e-6, None,
                 ALU.add, sem=ssem + "v")
            c.wait("vector", ssem + "v")
            c.op("vector", "reciprocal", riB[0:1, :], riB[0:1, :], sem=ssem + "v")
            c.wait("scalar", ssem + "v")
            c.op("scalar", "activation", r2[0:1, :], riB[0:1, :], AF.Sqrt, sem=ssem + "sq")
            c.wait("vector", ssem + "sq")
            c.op("vector", "tensor_tensor", riB[0:1, :], riA[0:1, :], r2[0:1, :],
                 ALU.mult, sem=ssem + "v")
            c.wait("vector", ssem + "v")
            c.op("vector", "tensor_scalar", riB[0:1, :], riB[0:1, :], -1.0, None,
                 ALU.mult, sem=ssem + "v")
            c.wait("tensor", ssem + "v")
            c.wait("tensor", "pv2ev", None)
            c.wait("tensor", "pl1b", None)
            c.wait("tensor", "pl2b", None)
            c.op("tensor", "matmul", c.pb[2][:, :], onesr[0:1, :], r2[0:1, :],
                 start=True, stop=True)
            c.op("tensor", "matmul", c.pb[3][:, :], onesr[0:1, :], riB[0:1, :],
                 start=True, stop=True, sem=ssem + "bm")
            c.wait("vector", ssem + "bm")
            for cc in range(DC):
                c.wait("vector", ssem + "b")
                c.op("vector", "tensor_tensor", t1s[:, :], src[:, cc * TT:(cc + 1) * TT],
                     c.pb[2][:, :], ALU.mult, sem=ssem + "b")
                c.wait("vector", ssem + "b")
                c.op("vector", "tensor_tensor", t1s[:, :], t1s[:, :], c.pb[3][:, :],
                     ALU.add, sem=ssem + "b")
                c.wait("vector", ssem + "b")
                c.op("vector", "tensor_scalar", dst[:, cc * TT:(cc + 1) * TT], t1s[:, :],
                     gvec[:, gcol0 + cc:gcol0 + cc + 1], gvec[:, bcol0 + cc:bcol0 + cc + 1],
                     ALU.mult, ALU.add, sem=ssem + "b")

        inner_ln(vbuf, ubuf, 0, DC, "pl1", "pvoev")
        c.wait("tensor", "pvoev", None)
        c.wait("sync", "pwom", None)
        for ffc in range(FFC):
            pi = ffc % 2
            bank = c.pb[4 + pi]
            c.wait("sync", "pmlm", c.n("pmlm") - 1)
            load_wtile(c, "sync", wb[pi], P["w1"], ffc, DC, f"pkw{pi}")
            c.wait("tensor", f"pkw{pi}")
            if ffc == 0:
                c.wait("tensor", "pl1b", None)
            c.wait("tensor", "pgel", c.n("pgel") - 1)
            for cc in range(DC):
                c.op("tensor", "matmul", bank[:, :], wb[pi][:, cc * 128:(cc + 1) * 128],
                     ubuf[:, cc * TT:(cc + 1) * TT], start=(cc == 0), stop=(cc == DC - 1),
                     sem=("pmlm" if cc == DC - 1 else None))
            c.wait("scalar", "pmlm")
            c.op("scalar", "activation", hh[:, ffc * TT:(ffc + 1) * TT], bank[:, :],
                 AF.Gelu, bias=b1v[:, ffc:ffc + 1], sem="pgel")
        c.wait("tensor", "pgel", None)
        for oc in range(DC):
            pi = oc % 2
            bank = c.pb[2 + pi]
            c.wait("sync", "pw2m", c.n("pw2m") - 1)
            load_wtile(c, "sync", w2b[pi], P["w2"], oc, FFC, f"pw2w{pi}")
            c.wait("tensor", f"pw2w{pi}")
            c.wait("tensor", "pv2ev", c.n("pv2ev") - 1)
            if oc <= 1:
                c.wait("tensor", "pl1b", None)
            for cc in range(FFC):
                c.op("tensor", "matmul", bank[:, :], w2b[pi][:, cc * 128:(cc + 1) * 128],
                     hh[:, cc * TT:(cc + 1) * TT], start=(cc == 0), stop=False)
            c.op("tensor", "matmul", bank[:, :], b2r[0:1, oc * 128:(oc + 1) * 128],
                 c.tiles["ones_rowb"][0:1, :], start=False, stop=True, sem="pw2m")
            c.wait("vector", "pw2m")
            c.op("vector", "tensor_tensor", vbuf[:, oc * TT:(oc + 1) * TT],
                 ubuf[:, oc * TT:(oc + 1) * TT], bank[:, :], ALU.add, sem="pv2ev")
        c.wait("vector", "pv2ev", None)
        inner_ln(vbuf, ubuf, 2 * DC, 3 * DC, "pl2", "pv2ev")
        xc_in = chunked(P["xT"])
        yc_out = chunked(P["yT"])
        c.wait("sync", "zbc", None)
        for oc in range(DC):
            pi = oc % 2
            c.wait("sync", f"pres{pi}", None)
            c.dma("sync", rxb[:, pi * TT:(pi + 1) * TT], xc_in[:, oc, sl], f"prxl{pi}")
            c.wait("vector", f"prxl{pi}")
            c.wait("vector", "pl2b", None)
            c.op("vector", "tensor_tensor", rxb[:, pi * TT:(pi + 1) * TT],
                 rxb[:, pi * TT:(pi + 1) * TT], ubuf[:, oc * TT:(oc + 1) * TT],
                 ALU.add, sem="presv")
            c.wait("sync", "presv")
            c.dma("sync", yc_out[:, oc, sl], rxb[:, pi * TT:(pi + 1) * TT], f"pres{pi}")
    c.wait_all("pres0")
    c.wait_all("pres1")


def build_step_l1_attn_a():
    nc = bass.Bass("TRN2", target_bir_lowering=False)
    xT = nc.dram_tensor("xT", [D, T], F32, kind="ExternalInput").ap()
    wq = nc.dram_tensor("wq", [1024, 16, 64], F32, kind="ExternalInput").ap()
    wk = nc.dram_tensor("wk", [1024, 16, 64], F32, kind="ExternalInput").ap()
    wv = nc.dram_tensor("wv", [1024, 16, 64], F32, kind="ExternalInput").ap()
    wo = nc.dram_tensor("wo", [1024, 1024], F32, kind="ExternalInput").ap()
    omega = nc.dram_tensor("omega", [256, 64], F32, kind="ExternalInput").ap()
    ln1_g = nc.dram_tensor("ln1_g", [1024], F32, kind="ExternalInput").ap()
    ln1_b = nc.dram_tensor("ln1_b", [1024], F32, kind="ExternalInput").ap()
    w1 = nc.dram_tensor("w1", [1024, 4096], F32, kind="ExternalInput").ap()
    b1 = nc.dram_tensor("b1", [4096], F32, kind="ExternalInput").ap()
    w2 = nc.dram_tensor("w2", [4096, 1024], F32, kind="ExternalInput").ap()
    b2 = nc.dram_tensor("b2", [1024], F32, kind="ExternalInput").ap()
    ln2_g = nc.dram_tensor("ln2_g", [1024], F32, kind="ExternalInput").ap()
    ln2_b = nc.dram_tensor("ln2_b", [1024], F32, kind="ExternalInput").ap()
    yT = nc.dram_tensor("yT", [D, T], F32, kind="ExternalOutput").ap()
    wq_bf = nc.dram_tensor("wq_bf", [1024, 1024], BF16, kind="Internal").ap()
    wk_bf = nc.dram_tensor("wk_bf", [1024, 1024], BF16, kind="Internal").ap()
    wv_bf = nc.dram_tensor("wv_bf", [1024, 1024], BF16, kind="Internal").ap()
    wo_bf = nc.dram_tensor("wo_bf", [1024, 1024], BF16, kind="Internal").ap()
    w1_bf = nc.dram_tensor("w1_bf", [1024, 4096], BF16, kind="Internal").ap()
    w2_bf = nc.dram_tensor("w2_bf", [4096, 1024], BF16, kind="Internal").ap()
    om_bf = nc.dram_tensor("om_bf", [256, 64], BF16, kind="Internal").ap()
    omtT_dram = nc.dram_tensor("omtT", [64, 256], BF16, kind="Internal").ap()

    def setup(c):
        alloc_consts(nc, c)
        emit_load_consts(c)
        emit_cast_dram(c, wq.rearrange("d h k -> d (h k)"), wq_bf)
        emit_cast_dram(c, wk.rearrange("d h k -> d (h k)"), wk_bf)
        emit_cast_dram(c, wv.rearrange("d h k -> d (h k)"), wv_bf)
        emit_cast_dram(c, wo, wo_bf)
        emit_cast_dram(c, w1, w1_bf)
        emit_cast_dram(c, w2, w2_bf)
        emit_cast_dram(c, omega, om_bf)
        c.wait_all("pcs")
        omt = c.sb("pf_omt", [128, 256], BF16)
        om_sb = c.sb("pf_omsb", [128, 128], BF16)
        c.dma("sync", om_sb[:, :].rearrange("p (cc w) -> p cc w", cc=2),
              om_bf.rearrange("(cc p) w -> p cc w", p=128), "pot")
        c.wait("tensor", "pot")
        for mc in range(2):
            c.op("tensor", "transpose", c.pbt[0:64, mc * 128:(mc + 1) * 128],
                 om_sb[:, mc * 64:(mc + 1) * 64], c.tiles["ident"][:, :],
                 sem=("potr" if mc == 1 else None))
        c.wait("vector", "potr")
        c.op("vector", "tensor_copy", omt[0:64, :], c.pbt[0:64, 0:256], sem="potc")
        c.wait("sync", "potc")
        c.dma("sync", omtT_dram[:, :], omt[0:64, :], "pot2")
        c.wait("sync", "pot2")
        c.dma("sync", omt[64:128, :], omtT_dram[:, :], "pot3")
        c.wait_all("pot3")
        c.wait_all("potc")

    P = {"xT": xT, "yT": yT, "wq": wq_bf, "wk": wk_bf, "wv": wv_bf, "wo": wo_bf,
         "w1": w1_bf, "w2": w2_bf, "omt": None, "b2": b2,
         "ln1_g": ln1_g, "ln1_b": ln1_b, "ln2_g": ln2_g, "ln2_b": ln2_b, "b1": b1}
    run_ctx(nc, [setup, lambda c: emit_perf(c, P)], with_pbt=True)
    return nc



def pe_table_np():
    S_, D_ = 8, 1024
    pos = np.arange(S_, dtype=np.float64)[:, None]
    div = np.exp(np.arange(0, D_, 2, dtype=np.float64) * (-np.log(10000.0) / D_))
    pe = np.zeros((S_, D_), np.float64)
    pe[:, 0::2] = np.sin(pos * div)
    pe[:, 1::2] = np.cos(pos * div)
    return np.ascontiguousarray(pe.T.astype(np.float32)).astype(ml_dtypes.bfloat16)


def emit_lab(c, P):
    nc = c.nc
    NH = 8
    EPS = 1.0e-6
    rowA = c.sb("rowA", [1, T], BF16)
    rowB = c.sb("rowB", [1, T], BF16)
    zf = c.sb("lb_z", [128, DC * T], BF16)
    qh = c.sb("lb_qh", [128, T], BF16)
    pkh = c.sb("lb_pkh", [128, T], BF16)
    vh = c.sb("lb_vh", [128, T], BF16)
    pqh = qh
    atth = c.sb("lb_atth", [128, T], BF16)
    numacc = c.sb("lb_num", [128, TT], F32)
    tbuf = c.sb("lb_tbuf", [128, TT], BF16)
    arow = c.sb("ln_r2", [1, TT], F32)
    prod = c.sb("lb_prod", [128, TT], BF16)
    pet = c.sb("lb_pet", [128, DC * 8], BF16)
    cq = c.sb("lb_cq", [128, 3 * 64], F32)
    bvec = c.sb("lb_bvec", [128, DC], F32)
    bor = c.sb("lb_bor", [1, 1024], BF16)
    ostage = c.sb("lb_ost", [128, TT], F32)
    wb = [c.sb(f"lb_wb{i}", [128, DC * 128], BF16) for i in range(2)]
    onesr = c.tiles["ones_row"]
    r1 = c.sb("ln_r1", [1, TT], F32)

    emit_ln_stats(c, P["xT"], rowA, rowB)

    c.dma("sync", pet[:, :].rearrange("p (cc w) -> p cc w", cc=DC),
          P["peT"].rearrange("(cc p) w -> p cc w", p=128), "lbl")
    c.dma("gpsimd", bor[0:1, :], P["bo"].rearrange("(o n) -> o n", o=1), "lblg")
    for wi, (wT, brow) in enumerate([(P["wqT"], P["bq"]), (P["wkT"], P["bk"]),
                                     (P["wvT"], P["bv"])]):
        emit_vec_rows(c, brow, bvec[:, 0:DC], 1024)
        for fc in range(DC):
            pi = fc % 2
            c.wait("sync", "lpw", c.n("lpw") - 1)
            load_wtile(c, "sync", wb[pi], wT, fc, DC, f"lbw{pi}")
            c.wait("tensor", f"lbw{pi}")
            c.wait("tensor", "lbl")
            c.wait("tensor", "lpc", c.n("lpc") - 1)
            for cc in range(DC):
                c.op("tensor", "matmul", c.pb[0 + pi][:, 0:8],
                     wb[pi][:, cc * 128:(cc + 1) * 128], pet[:, cc * 8:(cc + 1) * 8],
                     start=(cc == 0), stop=(cc == DC - 1),
                     sem=("lpw" if cc == DC - 1 else None))
            c.wait("vector", "lpw")
            c.wait("vector", "lbv", None)
            c.op("vector", "tensor_scalar", cq[:, wi * 64 + fc * 8:wi * 64 + (fc + 1) * 8],
                 c.pb[0 + pi][:, 0:8], bvec[:, fc:fc + 1], None, ALU.add, sem="lpc")
        c.wait_all("lpc")
        c.wait_all("lpw")
    c.wait_all("lbl")
    c.wait_all("lblg")

    xcz = chunked(P["xT"])
    xbz = c.sb("ln_xb", [128, 2 * TT], F32)
    t1z = c.sb("z_t1", [128, TT], F32)
    for tt in range(NTT):
        slz = slice(tt * TT, (tt + 1) * TT)
        c.wait("tensor", "zbc", None)
        c.op("tensor", "matmul", c.pb[4][:, :], c.tiles["ones_rowb128"][0:1, :],
             rowA[0:1, slz], sem="zbm", start=True, stop=True)
        c.op("tensor", "matmul", c.pb[5][:, :], c.tiles["ones_rowb128"][0:1, :],
             rowB[0:1, slz], sem="zbm", start=True, stop=True)
        c.wait("vector", "zbm")
        for cc in range(DC):
            pi = cc % 2
            xbt = xbz[:, pi * TT:(pi + 1) * TT]
            c.wait("sync", "zbc", c.n("zbc") - 3)
            c.dma("sync", xbt, xcz[:, cc, slz], f"zxl{pi}")
            c.wait("vector", f"zxl{pi}")
            c.wait("vector", "zbc")
            c.op("vector", "tensor_tensor", t1z[:, :], xbt, c.pb[4][:, :], ALU.mult, sem="zbc")
            c.wait("vector", "zbc")
            c.op("vector", "tensor_tensor", zf[:, cc * T + tt * TT:cc * T + (tt + 1) * TT],
                 t1z[:, :], c.pb[5][:, :], ALU.add, sem="zbc")

    c.dma("gpsimd", P["yT"][:, :], P["xT"][:, :], "lyc")
    c.wait_all("lyc")

    for h in range(NH):
        for wi, (wT, dst) in enumerate([(P["wqT"], qh), (P["wkT"], pkh), (P["wvT"], vh)]):
            pi = wi % 2
            c.wait("sync", "lqm", None)
            c.wait("sync", "lwom", None)
            load_wtile(c, "sync", wb[pi], wT, h, DC, f"lbw{pi}")
            c.wait("tensor", f"lbw{pi}")
            for tt in range(NTT):
                bank = c.pb[4 + (tt % 2)]
                c.wait("tensor", "lqev", c.n("lqev") - 1)
                if wi == 0 and tt == 0:
                    c.wait("tensor", "zbc", None)
                    c.wait("tensor", "lprod", None)
                    c.wait("tensor", "lelu", None)
                    c.wait("tensor", "labc", None)
                    c.wait("tensor", "ldbc", None)
                    c.wait("tensor", "loev", None)
                for cc in range(DC):
                    c.op("tensor", "matmul", bank[:, :],
                         wb[pi][:, cc * 128:(cc + 1) * 128],
                         zf[:, cc * T + tt * TT:cc * T + (tt + 1) * TT],
                         start=(cc == 0), stop=(cc == DC - 1),
                         sem=("lqm" if cc == DC - 1 else None))
                c.wait("vector", "lqm")
                c.op("vector", "tensor_scalar", dst[:, tt * TT:(tt + 1) * TT], bank[:, :],
                     cq[:, wi * 64 + h * 8 + tt:wi * 64 + h * 8 + tt + 1], None,
                     ALU.add, sem="lqev")
        for (srcb, dstb) in [(qh, pqh), (pkh, pkh)]:
            for tt in range(NTT):
                slt = slice(tt * TT, (tt + 1) * TT)
                c.wait("vector", "lqev", None)
                c.wait("vector", "lelu2", None)
                c.wait("vector", "lelu", None)
                c.op("vector", "tensor_scalar_min", tbuf[:, :], srcb[:, slt], 0.0, sem="lelu2")
                c.wait("scalar", "lelu2")
                c.op("scalar", "activation", prod[:, :], tbuf[:, :], AF.Exp, sem="lelu3")
                c.wait("vector", "lelu3")
                c.op("vector", "tensor_relu", tbuf[:, :], srcb[:, slt], sem="lelu2")
                c.wait("vector", "lelu2")
                c.op("vector", "tensor_tensor", dstb[:, slt], tbuf[:, :], prod[:, :],
                     ALU.add, sem="lelu")
        for s in range(NTT):
            ssl = slice(s * TT, (s + 1) * TT)
            if s == 0:
                c.wait("vector", "lwom", None)
            c.wait("vector", "lnum", None)
            c.wait("vector", "ldbc", None)
            c.op("vector", "memset", numacc[:, :], 0.0, sem="lnum")
            for sp in range(NTT):
                spl = slice(sp * TT, (sp + 1) * TT)
                c.wait("vector", "lelu", None)
                c.wait("vector", "lprodm", None)
                c.op("vector", "tensor_tensor", prod[:, :], pqh[:, ssl], pkh[:, spl],
                     ALU.mult, sem="lprod")
                c.wait("tensor", "lprod")
                c.wait("tensor", "larc", None)
                c.op("tensor", "matmul", c.pb[6][0:1, 0:512], c.tiles["ones_colb"][:, 0:1],
                     prod[:, :], start=True, stop=True, sem="lprodm")
                c.op("tensor", "matmul", c.pb[6][0:1, 0:512] if False else c.pb[5][0:1, 0:512],
                     c.tiles["ones_colb"][:, 0:1], prod[:, :],
                     start=(sp == 0), stop=(sp == NTT - 1), skip_group_check=True,
                     sem=("lden" if sp == NTT - 1 else None))
                c.wait("vector", "lprodm")
                c.op("vector", "tensor_copy", arow[0:1, :], c.pb[6][0:1, 0:512], sem="larc")
                c.wait("tensor", "larc")
                c.wait("tensor", "labc", None)
                c.wait("tensor", "ldbc", None)
                c.op("tensor", "matmul", c.pb[4][:, :], onesr[0:1, :], arow[0:1, :],
                     start=True, stop=True, sem="labm")
                c.wait("vector", "labm")
                c.op("vector", "tensor_tensor", tbuf[:, :], vh[:, spl], c.pb[4][:, :],
                     ALU.mult, sem="labc")
                c.wait("vector", "labc")
                c.op("vector", "tensor_tensor", numacc[:, :], numacc[:, :], tbuf[:, :],
                     ALU.add, sem="lnum")
            c.wait("vector", "lden")
            c.op("vector", "tensor_scalar", r1[0:1, :], c.pb[5][0:1, 0:512], EPS, None,
                 ALU.add, sem="lrec")
            c.wait("vector", "lrec")
            c.op("vector", "reciprocal", r1[0:1, :], r1[0:1, :], sem="lrec")
            c.wait("tensor", "lrec")
            c.wait("tensor", "ldbc", None)
            c.op("tensor", "matmul", c.pb[4][:, :], onesr[0:1, :], r1[0:1, :],
                 start=True, stop=True, sem="ldbm")
            c.wait("vector", "ldbm")
            c.wait("vector", "lnum", None)
            c.op("vector", "tensor_tensor", atth[:, ssl], numacc[:, :], c.pb[4][:, :],
                 ALU.mult, sem="ldbc")
        c.wait("sync", "lqm", None)
        c.wait("gpsimd", "lyc", None)
        c.wait("gpsimd", "lobs", None)
        for oc in range(DC):
            pi = oc % 2
            c.wait("sync", "lwom", c.n("lwom") - 1)
            load_wtile(c, "sync", wb[pi], P["woT"], oc, DC, f"lbw{pi}")
            c.wait("tensor", f"lbw{pi}")
            for s in range(NTT):
                sslc = slice(s * TT, (s + 1) * TT)
                bank = c.pb[4 + (s % 2)]
                c.wait("tensor", "ldbc", None)
                c.wait("tensor", "loev", c.n("loev") - 1)
                last = not (h == 0)
                c.op("tensor", "matmul", bank[:, :], wb[pi][:, h * 128:(h + 1) * 128],
                     atth[:, sslc], start=True, stop=last,
                     sem=("lwom" if last else None))
                if h == 0:
                    c.op("tensor", "matmul", bank[:, :], bor[0:1, oc * 128:(oc + 1) * 128],
                         c.tiles["ones_rowb"][0:1, :], start=False, stop=True, sem="lwom")
                c.wait("scalar", "lwom")
                c.wait("scalar", "lobs", None)
                c.op("scalar", "activation", ostage[:, :],
                     bank[:, :], AF.Copy, sem="loev")
                c.wait("gpsimd", "loev")
                c.dma("gpsimd", chunked(P["yT"])[:, oc, sslc], ostage[:, :], "lobs",
                      accum_op=ALU.add)
    c.wait_all("lobs")


def build_step_l1_attn_b():
    nc = bass.Bass("TRN2", target_bir_lowering=False)
    xT = nc.dram_tensor("xT", [D, T], F32, kind="ExternalInput").ap()
    Wq = nc.dram_tensor("Wq", [1024, 1024], F32, kind="ExternalInput").ap()
    bq = nc.dram_tensor("bq", [1024], F32, kind="ExternalInput").ap()
    Wk = nc.dram_tensor("Wk", [1024, 1024], F32, kind="ExternalInput").ap()
    bk = nc.dram_tensor("bk", [1024], F32, kind="ExternalInput").ap()
    Wv = nc.dram_tensor("Wv", [1024, 1024], F32, kind="ExternalInput").ap()
    bv = nc.dram_tensor("bv", [1024], F32, kind="ExternalInput").ap()
    Wo = nc.dram_tensor("Wo", [1024, 1024], F32, kind="ExternalInput").ap()
    bo = nc.dram_tensor("bo", [1024], F32, kind="ExternalInput").ap()
    yT = nc.dram_tensor("yT", [D, T], F32, kind="ExternalOutput").ap()
    scr = {}
    for nm in ("q", "k", "v", "o"):
        scr[nm + "_bf"] = nc.dram_tensor(nm + "_bf", [1024, 1024], BF16, kind="Internal").ap()
        scr[nm + "T"] = nc.dram_tensor(nm + "T", [1024, 1024], BF16, kind="Internal").ap()
    peT = nc.inline_tensor(pe_table_np(), name="peT").ap()

    def setup(c):
        alloc_consts(nc, c)
        emit_load_consts(c)
        for nm, w in (("q", Wq), ("k", Wk), ("v", Wv), ("o", Wo)):
            emit_cast_dram(c, w, scr[nm + "_bf"])
        c.wait_all("pcs")
        for nm in ("q", "k", "v", "o"):
            emit_transpose_scratch(c, scr[nm + "_bf"], scr[nm + "T"], 1024, 1024)

    P = {"xT": xT, "yT": yT, "wqT": scr["qT"], "wkT": scr["kT"], "wvT": scr["vT"],
         "woT": scr["oT"], "bq": bq, "bk": bk, "bv": bv, "bo": bo, "peT": peT}
    run_ctx(nc, [setup, lambda c: emit_lab(c, P)])
    return nc



def _persist_shared(c):
    c.sb("ln_xb", [128, 2 * TT], F32)
    c.sb("ln_r1", [1, TT], F32)
    c.sb("ln_r2", [1, TT], F32)
    c.sb("z_t1", [128, TT], F32)
    c.sb("ln_sq", [128, 2 * TT], BF16)
    c.sb("rowA", [1, T], BF16)
    c.sb("rowB", [1, T], BF16)


def build_fused_part1():
    nc = bass.Bass("TRN2", target_bir_lowering=False)
    I = {}
    for nm, shp, dt in [
        ("xT", [D, T], F32),
        ("L0_attn_a_qw", [1024, 1024], F32), ("L0_attn_a_kw", [1024, 1024], F32),
        ("L0_attn_a_vw", [1024, 1024], F32), ("L0_attn_a_ow", [1024, 1024], F32),
        ("L0_attn_b_qkv_w", [3072, 1024], F32), ("L0_attn_b_qkv_b", [3072], F32),
        ("L0_attn_b_out_w", [1024, 1024], F32), ("L0_attn_b_out_b", [1024], F32),
        ("L0_ffn_w_gate", [3072, 1024], I32), ("L0_ffn_w_up", [3072, 1024], I32),
        ("L0_ffn_w_down", [1024, 3072], I32),
        ("L1_attn_a_wq", [1024, 16, 64], F32), ("L1_attn_a_wk", [1024, 16, 64], F32),
        ("L1_attn_a_wv", [1024, 16, 64], F32), ("L1_attn_a_wo", [1024, 1024], F32),
        ("L1_attn_a_omega", [256, 64], F32),
        ("L1_attn_a_ln1_g", [1024], F32), ("L1_attn_a_ln1_b", [1024], F32),
        ("L1_attn_a_w1", [1024, 4096], F32), ("L1_attn_a_b1", [4096], F32),
        ("L1_attn_a_w2", [4096, 1024], F32), ("L1_attn_a_b2", [1024], F32),
        ("L1_attn_a_ln2_g", [1024], F32), ("L1_attn_a_ln2_b", [1024], F32),
        ("L1_ffn_router_w", [8, 1024], F32),
        ("L1_ffn_w_gate", [8, 1024, 2048], F32), ("L1_ffn_w_up", [8, 1024, 2048], F32),
        ("L1_ffn_w_down", [8, 2048, 1024], F32),
    ]:
        I[nm] = nc.dram_tensor(nm, shp, dt, kind="ExternalInput").ap()
    yT = nc.dram_tensor("yT", [D, T], F32, kind="ExternalOutput").ap()
    sA = nc.dram_tensor("sA", [D, T], F32, kind="Internal").ap()
    sB = nc.dram_tensor("sB", [D, T], F32, kind="Internal").ap()
    S = {}
    def scratch(nm, shp):
        S[nm] = nc.dram_tensor(nm, shp, BF16, kind="Internal").ap()
        return S[nm]
    for nm, shp in [("aa_qwp", [8, 128, 1024]), ("aa_kwp", [8, 128, 1024]),
                    ("aa_vwp", [8, 128, 1024]), ("aa_owp", [8, 128, 1024]),
                    ("f0_wg_bf", [3072, 1024]), ("f0_wu_bf", [3072, 1024]),
                    ("f0_wd_bf", [1024, 3072]),
                    ("f0_wgT", [24, 128, 1024]), ("f0_wuT", [24, 128, 1024]),
                    ("f0_wdT", [8, 128, 3072]),
                    ("ab_qkv_bf", [3072, 1024]), ("ab_qkvT", [24, 128, 1024]),
                    ("ab_ow_bf", [1024, 1024]), ("ab_owT", [8, 128, 1024]),
                    ("pf_wq_bf", [8, 128, 1024]), ("pf_wk_bf", [8, 128, 1024]),
                    ("pf_wv_bf", [8, 128, 1024]), ("pf_wo_bf", [8, 128, 1024]),
                    ("pf_w1_bf", [32, 128, 1024]), ("pf_w2_bf", [8, 128, 4096]),
                    ("pf_om_bf", [256, 64]), ("pf_omtT", [64, 256]),
                    ("mo_wg_bf", [128, 128, 1024]), ("mo_wu_bf", [128, 128, 1024]),
                    ("mo_wd_bf", [64, 128, 2048])]:
        scratch(nm, shp)
    sel = nc.inline_tensor(moe_sel_np(), name="moe_sel").ap()

    def main(c):
        alloc_consts(nc, c)
        c.tiles["moe_sel_dram"] = sel
        emit_load_consts(c)
        _persist_shared(c)
        ab_bias = c.sb("ab_bias", [1, 4096], BF16)
        pf_omt = c.sb("pf_omt", [128, 256], BF16)
        c.tiles["ropeP2"] = c.sb("ropeP2_sb", [128, 128], BF16)
        c.dma("sync", c.tiles["ropeP2"][:], _ROPEP2_AP[0], "cst")
        c.wait_all("cst")
        c.begin_arena(165 * 1024)
        c.new_phase("pre")
        emit_permute_qkv_pre(c, I["L0_attn_a_qw"], S["aa_qwp"])
        emit_permute_qkv_pre(c, I["L0_attn_a_kw"], S["aa_kwp"])
        emit_permute_qkv_pre(c, I["L0_attn_a_vw"], S["aa_vwp"])
        emit_gather_ow_pre(c, I["L0_attn_a_ow"], S["aa_owp"])
        emit_cast_dram(c, I["L0_ffn_w_gate"], S["f0_wg_bf"])
        emit_cast_dram(c, I["L0_ffn_w_up"], S["f0_wu_bf"])
        emit_cast_dram(c, I["L0_ffn_w_down"], S["f0_wd_bf"])
        emit_cast_dram(c, I["L0_attn_b_qkv_w"], S["ab_qkv_bf"])
        emit_cast_dram(c, I["L0_attn_b_out_w"], S["ab_ow_bf"])
        emit_cast_tiled(c, I["L1_attn_a_wq"].rearrange("d h k -> d (h k)"), S["pf_wq_bf"], 1024, 1024)
        emit_cast_tiled(c, I["L1_attn_a_wk"].rearrange("d h k -> d (h k)"), S["pf_wk_bf"], 1024, 1024)
        emit_cast_tiled(c, I["L1_attn_a_wv"].rearrange("d h k -> d (h k)"), S["pf_wv_bf"], 1024, 1024)
        emit_cast_tiled(c, I["L1_attn_a_wo"], S["pf_wo_bf"], 1024, 1024)
        emit_cast_tiled(c, I["L1_attn_a_w1"], S["pf_w1_bf"], 1024, 4096)
        emit_cast_tiled(c, I["L1_attn_a_w2"], S["pf_w2_bf"], 4096, 1024)
        emit_cast_dram(c, I["L1_attn_a_omega"], S["pf_om_bf"])
        for wnm, inm in [("mo_wg_bf", "L1_ffn_w_gate"), ("mo_wu_bf", "L1_ffn_w_up"),
                         ("mo_wd_bf", "L1_ffn_w_down")]:
            srcv = I[inm].rearrange("e (cc p) (fc w) -> e fc p cc w", p=128, w=128)
            nfc = srcv.shape[1]
            for e in range(8):
                for fc in range(nfc):
                    c.dma("gpsimd", S[wnm][e * nfc + fc].rearrange("p (cc w) -> p cc w", w=128),
                          srcv[e, fc], "pcs")
        c.dma("gpsimd", ab_bias[0:1, 0:3072],
              I["L0_attn_b_qkv_b"].rearrange("(o n) -> o n", o=1), "pcsb")
        c.dma("gpsimd", ab_bias[0:1, 3072:4096],
              I["L0_attn_b_out_b"].rearrange("(o n) -> o n", o=1), "pcsb")
        c.wait_all("pcs")
        c.wait_all("pcsb")
        emit_transpose_scratch(c, S["f0_wg_bf"], S["f0_wgT"], 3072, 1024)
        emit_transpose_scratch(c, S["f0_wu_bf"], S["f0_wuT"], 3072, 1024)
        emit_transpose_scratch(c, S["f0_wd_bf"], S["f0_wdT"], 1024, 3072)
        emit_transpose_scratch(c, S["ab_qkv_bf"], S["ab_qkvT"], 3072, 1024)
        emit_transpose_scratch(c, S["ab_ow_bf"], S["ab_owT"], 1024, 1024)
        om_sb = c.sb("pf_omsb", [128, 128], BF16)
        c.dma("sync", om_sb[:, :].rearrange("p (cc w) -> p cc w", cc=2),
              S["pf_om_bf"].rearrange("(cc p) w -> p cc w", p=128), "pot")
        c.wait("tensor", "pot")
        for mc in range(2):
            c.op("tensor", "transpose", c.pbt[0:64, mc * 128:(mc + 1) * 128],
                 om_sb[:, mc * 64:(mc + 1) * 64], c.tiles["ident"][:, :],
                 sem=("potr" if mc == 1 else None))
        c.wait("vector", "potr")
        c.op("vector", "tensor_copy", pf_omt[0:64, :], c.pbt[0:64, 0:256], sem="potc")
        c.wait("sync", "potc")
        c.dma("sync", S["pf_omtT"][:, :], pf_omt[0:64, :], "pot2")
        c.wait("sync", "pot2")
        c.dma("sync", pf_omt[64:128, :], S["pf_omtT"][:, :], "pot3")
        c.wait_all("pot3")
        c.wait_all("potc")
        c.new_phase("s1")
        emit_causal_attn(c, {"xT": I["xT"], "yT": sA, "qw": S["aa_qwp"], "kw": S["aa_kwp"],
                             "vw": S["aa_vwp"], "ow": S["aa_owp"], "scale": 1.0 / 32.0,
                             "rope": None})
        c.new_phase("s2")
        build_l0_ffn(nc, sA, sB, S["f0_wgT"], S["f0_wuT"], S["f0_wdT"])(c)
        c.new_phase("s3")
        emit_causal_attn(c, {"xT": sB, "yT": sA,
                             "qw": S["ab_qkvT"][0:8], "kw": S["ab_qkvT"][8:16],
                             "vw": S["ab_qkvT"][16:24], "ow": S["ab_owT"],
                             "qb_sb": ab_bias[:, 0:1024], "kb_sb": ab_bias[:, 1024:2048],
                             "vb_sb": ab_bias[:, 2048:3072], "ob_sb": ab_bias[:, 3072:4096],
                             "scale": 1.0 / 8.0, "rope": _ROPETAB_AP[0]})
        c.new_phase("s4")
        build_l0_ffn(nc, sA, sB, S["f0_wgT"], S["f0_wuT"], S["f0_wdT"])(c)
        c.new_phase("s5")
        emit_perf(c, {"xT": sB, "yT": sA, "wq": S["pf_wq_bf"], "wk": S["pf_wk_bf"],
                      "wv": S["pf_wv_bf"], "wo": S["pf_wo_bf"], "w1": S["pf_w1_bf"],
                      "w2": S["pf_w2_bf"], "b2": I["L1_attn_a_b2"],
                      "ln1_g": I["L1_attn_a_ln1_g"], "ln1_b": I["L1_attn_a_ln1_b"],
                      "ln2_g": I["L1_attn_a_ln2_g"], "ln2_b": I["L1_attn_a_ln2_b"],
                      "b1": I["L1_attn_a_b1"]})
        c.new_phase("s6")
        emit_moe(c, {"xT": sA, "yT": yT, "router": I["L1_ffn_router_w"],
                     "wg": S["mo_wg_bf"], "wu": S["mo_wu_bf"], "wd": S["mo_wd_bf"]})

    _ROPETAB_AP = [nc.inline_tensor(rope_table_np(), name="ropetab").ap()]
    _ROPEP2_AP = [nc.inline_tensor(rope_swap_np(), name="ropeP2").ap()]
    run_ctx(nc, [main], with_pbt=True)
    return nc


def build_fused_part2():
    nc = bass.Bass("TRN2", target_bir_lowering=False)
    I = {}
    for nm, shp, dt in [
        ("xT", [D, T], F32),
        ("L1_attn_b_Wq", [1024, 1024], F32), ("L1_attn_b_bq", [1024], F32),
        ("L1_attn_b_Wk", [1024, 1024], F32), ("L1_attn_b_bk", [1024], F32),
        ("L1_attn_b_Wv", [1024, 1024], F32), ("L1_attn_b_bv", [1024], F32),
        ("L1_attn_b_Wo", [1024, 1024], F32), ("L1_attn_b_bo", [1024], F32),
        ("L1_ffn_router_w", [8, 1024], F32),
        ("L1_ffn_w_gate", [8, 1024, 2048], F32), ("L1_ffn_w_up", [8, 1024, 2048], F32),
        ("L1_ffn_w_down", [8, 2048, 1024], F32),
    ]:
        I[nm] = nc.dram_tensor(nm, shp, dt, kind="ExternalInput").ap()
    yT = nc.dram_tensor("yT", [D, T], F32, kind="ExternalOutput").ap()
    sA = nc.dram_tensor("sA", [D, T], F32, kind="Internal").ap()
    S = {}
    for nm, shp in [("lb_q_bf", [1024, 1024]), ("lb_qT", [1024, 1024]),
                    ("lb_k_bf", [1024, 1024]), ("lb_kT", [1024, 1024]),
                    ("lb_v_bf", [1024, 1024]), ("lb_vT", [1024, 1024]),
                    ("lb_o_bf", [1024, 1024]), ("lb_oT", [1024, 1024]),
                    ("mo_wg_bf", [8 * 1024, 2048]), ("mo_wu_bf", [8 * 1024, 2048]),
                    ("mo_wd_bf", [8 * 2048, 1024])]:
        S[nm] = nc.dram_tensor(nm, shp, BF16, kind="Internal").ap()
    peT = nc.inline_tensor(pe_table_np(), name="peT").ap()
    sel = nc.inline_tensor(moe_sel_np(), name="moe_sel").ap()

    def main(c):
        alloc_consts(nc, c)
        c.tiles["moe_sel_dram"] = sel
        emit_load_consts(c)
        _persist_shared(c)
        c.begin_arena(165 * 1024)
        c.new_phase("pre")
        for pair in [("L1_attn_b_Wq", "q"), ("L1_attn_b_Wk", "k"),
                     ("L1_attn_b_Wv", "v"), ("L1_attn_b_Wo", "o")]:
            emit_cast_dram(c, I[pair[0]], S[f"lb_{pair[1]}_bf"])
        emit_cast_dram(c, I["L1_ffn_w_gate"].rearrange("e d f -> (e d) f"), S["mo_wg_bf"])
        emit_cast_dram(c, I["L1_ffn_w_up"].rearrange("e d f -> (e d) f"), S["mo_wu_bf"])
        emit_cast_dram(c, I["L1_ffn_w_down"].rearrange("e h f -> (e h) f"), S["mo_wd_bf"])
        c.wait_all("pcs")
        for nm in ("q", "k", "v", "o"):
            emit_transpose_scratch(c, S[f"lb_{nm}_bf"], S[f"lb_{nm}T"], 1024, 1024)
        c.new_phase("s7")
        emit_lab(c, {"xT": I["xT"], "yT": sA, "wqT": S["lb_qT"], "wkT": S["lb_kT"],
                     "wvT": S["lb_vT"], "woT": S["lb_oT"], "bq": I["L1_attn_b_bq"],
                     "bk": I["L1_attn_b_bk"], "bv": I["L1_attn_b_bv"],
                     "bo": I["L1_attn_b_bo"], "peT": peT})
        c.new_phase("s8")
        emit_moe(c, {"xT": sA, "yT": yT, "router": I["L1_ffn_router_w"],
                     "wg": S["mo_wg_bf"], "wu": S["mo_wu_bf"], "wd": S["mo_wd_bf"]})

    run_ctx(nc, [main])
    return nc



def build_fused_full():
    nc = bass.Bass("TRN2", target_bir_lowering=False)
    I = {}
    for nm, shp, dt in [
        ("xT", [D, T], F32),
        ("L0_attn_a_qw", [1024, 1024], F32), ("L0_attn_a_kw", [1024, 1024], F32),
        ("L0_attn_a_vw", [1024, 1024], F32), ("L0_attn_a_ow", [1024, 1024], F32),
        ("L0_attn_b_qkv_w", [3072, 1024], F32), ("L0_attn_b_qkv_b", [3072], F32),
        ("L0_attn_b_out_w", [1024, 1024], F32), ("L0_attn_b_out_b", [1024], F32),
        ("L0_ffn_w_gate", [3072, 1024], I32), ("L0_ffn_w_up", [3072, 1024], I32),
        ("L0_ffn_w_down", [1024, 3072], I32),
        ("L1_attn_a_wq", [1024, 16, 64], F32), ("L1_attn_a_wk", [1024, 16, 64], F32),
        ("L1_attn_a_wv", [1024, 16, 64], F32), ("L1_attn_a_wo", [1024, 1024], F32),
        ("L1_attn_a_omega", [256, 64], F32),
        ("L1_attn_a_ln1_g", [1024], F32), ("L1_attn_a_ln1_b", [1024], F32),
        ("L1_attn_a_w1", [1024, 4096], F32), ("L1_attn_a_b1", [4096], F32),
        ("L1_attn_a_w2", [4096, 1024], F32), ("L1_attn_a_b2", [1024], F32),
        ("L1_attn_a_ln2_g", [1024], F32), ("L1_attn_a_ln2_b", [1024], F32),
        ("L1_ffn_router_w", [8, 1024], F32),
        ("L1_ffn_w_gate", [8, 1024, 2048], F32), ("L1_ffn_w_up", [8, 1024, 2048], F32),
        ("L1_ffn_w_down", [8, 2048, 1024], F32),
        ("L1_attn_b_Wq", [1024, 1024], F32), ("L1_attn_b_bq", [1024], F32),
        ("L1_attn_b_Wk", [1024, 1024], F32), ("L1_attn_b_bk", [1024], F32),
        ("L1_attn_b_Wv", [1024, 1024], F32), ("L1_attn_b_bv", [1024], F32),
        ("L1_attn_b_Wo", [1024, 1024], F32), ("L1_attn_b_bo", [1024], F32),
    ]:
        I[nm] = nc.dram_tensor(nm, shp, dt, kind="ExternalInput").ap()
    yT = nc.dram_tensor("yT", [D, T], F32, kind="ExternalOutput").ap()
    sA = nc.dram_tensor("sA", [D, T], F32, kind="Internal").ap()
    sB = nc.dram_tensor("sB", [D, T], F32, kind="Internal").ap()
    x6a = nc.dram_tensor("x6a", [D, T], F32, kind="Internal").ap()
    xg = nc.dram_tensor("xg", [8 * D, T], F32, kind="Internal").ap()
    x7in = nc.dram_tensor("x7in", [D, T], F32, kind="Internal").ap()
    S = {}
    for nm, shp in [("aa_qwp", [8, 128, 1024]), ("aa_kwp", [8, 128, 1024]),
                    ("aa_vwp", [8, 128, 1024]), ("aa_owp", [8, 128, 1024]),
                    ("f0_wg_bf", [3072, 1024]), ("f0_wu_bf", [3072, 1024]),
                    ("f0_wd_bf", [1024, 3072]),
                    ("f0_wgT", [24, 128, 1024]), ("f0_wuT", [24, 128, 1024]),
                    ("f0_wdT", [8, 128, 3072]),
                    ("ab_qkv_bf", [3072, 1024]), ("ab_qkvT", [24, 128, 1024]),
                    ("ab_ow_bf", [1024, 1024]), ("ab_owT", [8, 128, 1024]),
                    ("pf_wq_bf", [8, 128, 1024]), ("pf_wk_bf", [8, 128, 1024]),
                    ("pf_wv_bf", [8, 128, 1024]), ("pf_wo_bf", [8, 128, 1024]),
                    ("pf_w1_bf", [32, 128, 1024]), ("pf_w2_bf", [8, 128, 4096]),
                    ("pf_om_bf", [256, 64]), ("pf_omtT", [64, 256]),
                    ("mo_wg_bf", [128, 128, 1024]), ("mo_wu_bf", [128, 128, 1024]),
                    ("mo_wd_bf", [64, 128, 2048]),
                    ("lb_q_bf", [1024, 1024]), ("lb_qT", [8, 128, 1024]),
                    ("lb_k_bf", [1024, 1024]), ("lb_kT", [8, 128, 1024]),
                    ("lb_v_bf", [1024, 1024]), ("lb_vT", [8, 128, 1024]),
                    ("lb_o_bf", [1024, 1024]), ("lb_oT", [8, 128, 1024])]:
        S[nm] = nc.dram_tensor(nm, shp, BF16, kind="Internal").ap()
    sel = nc.inline_tensor(moe_sel_np(), name="moe_sel").ap()
    ropetab = nc.inline_tensor(rope_table_np(), name="ropetab").ap()
    ropeP2 = nc.inline_tensor(rope_swap_np(), name="ropeP2").ap()
    peT = nc.inline_tensor(pe_table_np(), name="peT").ap()

    def main(c):
        alloc_consts(nc, c)
        c.tiles["moe_sel_dram"] = sel
        emit_load_consts(c)
        _persist_shared(c)
        ab_bias = c.sb("ab_bias", [1, 4096], BF16)
        pf_omt = c.sb("pf_omt", [128, 256], BF16)
        c.tiles["ropeP2"] = c.sb("ropeP2_sb", [128, 128], BF16)
        c.dma("sync", c.tiles["ropeP2"][:], ropeP2, "cst")
        c.wait_all("cst")
        c.begin_arena(165 * 1024)
        c.new_phase("pre")
        emit_permute_qkv_pre(c, I["L0_attn_a_qw"], S["aa_qwp"])
        emit_permute_qkv_pre(c, I["L0_attn_a_kw"], S["aa_kwp"])
        emit_permute_qkv_pre(c, I["L0_attn_a_vw"], S["aa_vwp"])
        emit_gather_ow_pre(c, I["L0_attn_a_ow"], S["aa_owp"])
        emit_cast_dram(c, I["L0_ffn_w_gate"], S["f0_wg_bf"])
        emit_cast_dram(c, I["L0_ffn_w_up"], S["f0_wu_bf"])
        emit_cast_dram(c, I["L0_ffn_w_down"], S["f0_wd_bf"])
        emit_cast_dram(c, I["L0_attn_b_qkv_w"], S["ab_qkv_bf"])
        emit_cast_dram(c, I["L0_attn_b_out_w"], S["ab_ow_bf"])
        emit_cast_dram(c, I["L1_attn_a_omega"], S["pf_om_bf"])
        emit_cast_dram(c, I["L1_attn_b_Wq"], S["lb_q_bf"])
        emit_cast_dram(c, I["L1_attn_b_Wk"], S["lb_k_bf"])
        emit_cast_dram(c, I["L1_attn_b_Wv"], S["lb_v_bf"])
        emit_cast_dram(c, I["L1_attn_b_Wo"], S["lb_o_bf"])
        snap_pre = c.n("pcs")
        c.dma("gpsimd", ab_bias[0:1, 0:3072],
              I["L0_attn_b_qkv_b"].rearrange("(o n) -> o n", o=1), "pcsb")
        c.dma("gpsimd", ab_bias[0:1, 3072:4096],
              I["L0_attn_b_out_b"].rearrange("(o n) -> o n", o=1), "pcsb")
        emit_cast_tiled(c, I["L1_attn_a_wq"].rearrange("d h k -> d (h k)"), S["pf_wq_bf"], 1024, 1024)
        emit_cast_tiled(c, I["L1_attn_a_wk"].rearrange("d h k -> d (h k)"), S["pf_wk_bf"], 1024, 1024)
        emit_cast_tiled(c, I["L1_attn_a_wv"].rearrange("d h k -> d (h k)"), S["pf_wv_bf"], 1024, 1024)
        emit_cast_tiled(c, I["L1_attn_a_wo"], S["pf_wo_bf"], 1024, 1024)
        emit_cast_tiled(c, I["L1_attn_a_w1"], S["pf_w1_bf"], 1024, 4096)
        emit_cast_tiled(c, I["L1_attn_a_w2"], S["pf_w2_bf"], 4096, 1024)
        for wnm, inm in [("mo_wg_bf", "L1_ffn_w_gate"), ("mo_wu_bf", "L1_ffn_w_up"),
                         ("mo_wd_bf", "L1_ffn_w_down")]:
            srcv = I[inm].rearrange("e (cc p) (fc w) -> e fc p cc w", p=128, w=128)
            nfc = srcv.shape[1]
            for e in range(8):
                for fc in range(nfc):
                    c.dma("gpsimd", S[wnm][e * nfc + fc].rearrange("p (cc w) -> p cc w", w=128),
                          srcv[e, fc], "pcs")
        snap_mo = c.n("pcs")
        c.wait_all("pcsb")
        c.wait("scalar", "pcs", snap_pre)
        c.wait("sync", "pcs", snap_pre)
        emit_transpose_scratch(c, S["f0_wg_bf"], S["f0_wgT"], 3072, 1024)
        emit_transpose_scratch(c, S["f0_wu_bf"], S["f0_wuT"], 3072, 1024)
        emit_transpose_scratch(c, S["f0_wd_bf"], S["f0_wdT"], 1024, 3072)
        emit_transpose_scratch(c, S["ab_qkv_bf"], S["ab_qkvT"], 3072, 1024)
        emit_transpose_scratch(c, S["ab_ow_bf"], S["ab_owT"], 1024, 1024)
        for nm in ("q", "k", "v", "o"):
            emit_transpose_scratch(c, S[f"lb_{nm}_bf"], S[f"lb_{nm}T"], 1024, 1024)
        om_sb = c.sb("pf_omsb", [128, 128], BF16)
        c.dma("sync", om_sb[:, :].rearrange("p (cc w) -> p cc w", cc=2),
              S["pf_om_bf"].rearrange("(cc p) w -> p cc w", p=128), "pot")
        c.wait("tensor", "pot")
        for mc in range(2):
            c.op("tensor", "transpose", c.pbt[0:64, mc * 128:(mc + 1) * 128],
                 om_sb[:, mc * 64:(mc + 1) * 64], c.tiles["ident"][:, :],
                 sem=("potr" if mc == 1 else None))
        c.wait("vector", "potr")
        c.op("vector", "tensor_copy", pf_omt[0:64, :], c.pbt[0:64, 0:256], sem="potc")
        c.wait("sync", "potc")
        c.dma("sync", S["pf_omtT"][:, :], pf_omt[0:64, :], "pot2")
        c.wait("sync", "pot2")
        c.dma("sync", pf_omt[64:128, :], S["pf_omtT"][:, :], "pot3")
        c.wait_all("pot3")
        c.wait_all("potc")
        c.new_phase("s1")
        emit_causal_attn(c, {"xT": I["xT"], "yT": sA, "qw": S["aa_qwp"], "kw": S["aa_kwp"],
                             "vw": S["aa_vwp"], "ow": S["aa_owp"], "scale": 1.0 / 32.0,
                             "rope": None})
        c.new_phase("s2")
        build_l0_ffn(nc, sA, sB, S["f0_wgT"], S["f0_wuT"], S["f0_wdT"])(c)
        c.new_phase("s3")
        emit_causal_attn(c, {"xT": sB, "yT": sA,
                             "qw": S["ab_qkvT"][0:8], "kw": S["ab_qkvT"][8:16],
                             "vw": S["ab_qkvT"][16:24], "ow": S["ab_owT"],
                             "qb_sb": ab_bias[:, 0:1024], "kb_sb": ab_bias[:, 1024:2048],
                             "vb_sb": ab_bias[:, 2048:3072], "ob_sb": ab_bias[:, 3072:4096],
                             "scale": 1.0 / 8.0, "rope": ropetab})
        c.new_phase("s4")
        build_l0_ffn(nc, sA, sB, S["f0_wgT"], S["f0_wuT"], S["f0_wdT"])(c)
        c.new_phase("s5")
        c.wait("sync", "pcs", snap_mo)
        emit_perf(c, {"xT": sB, "yT": sA, "wq": S["pf_wq_bf"], "wk": S["pf_wk_bf"],
                      "wv": S["pf_wv_bf"], "wo": S["pf_wo_bf"], "w1": S["pf_w1_bf"],
                      "w2": S["pf_w2_bf"], "b2": I["L1_attn_a_b2"],
                      "ln1_g": I["L1_attn_a_ln1_g"], "ln1_b": I["L1_attn_a_ln1_b"],
                      "ln2_g": I["L1_attn_a_ln2_g"], "ln2_b": I["L1_attn_a_ln2_b"],
                      "b1": I["L1_attn_a_b1"]})
        c.new_phase("s6")
        emit_moe(c, {"xT": sA, "yT": sB, "router": I["L1_ffn_router_w"],
                     "wg": S["mo_wg_bf"], "wu": S["mo_wu_bf"], "wd": S["mo_wd_bf"]})
        c.new_phase("rs")
        if c.kind == "probe":
            c.swdge_names.add("a2a")
        c.dma("sync", x6a[:, :], sB[:, :], "rsc")
        c.wait("gpsimd", "rsc")
        c.op("gpsimd", "collective_compute", "AllGather", ALU.bypass,
             [[0, 1, 2, 3, 4, 5, 6, 7]], [x6a.opt()], [xg.opt()], sem="a2a")
        c.wait("sync", "a2a")
        if c.kind == "sync":
            pid = nc.partition_id()
        for s in range(8):
            if c.kind == "sync":
                c.eng.dma_start(out=x7in[:, s * 512:(s + 1) * 512],
                                in_=xg[s * D:(s + 1) * D, bass.ts(pid, 512)]
                                ).then_inc(c.S("rsc"), 16)
            c.bump("rsc", 16)
        c.wait_all("rsc")
        c.wait_all("a2a")
        c.new_phase("s7")
        emit_lab(c, {"xT": x7in, "yT": sA, "wqT": S["lb_qT"], "wkT": S["lb_kT"],
                     "wvT": S["lb_vT"], "woT": S["lb_oT"], "bq": I["L1_attn_b_bq"],
                     "bk": I["L1_attn_b_bk"], "bv": I["L1_attn_b_bv"],
                     "bo": I["L1_attn_b_bo"], "peT": peT})
        c.new_phase("s8")
        emit_moe(c, {"xT": sA, "yT": yT, "router": I["L1_ffn_router_w"],
                     "wg": S["mo_wg_bf"], "wu": S["mo_wu_bf"], "wd": S["mo_wd_bf"]})

    run_ctx(nc, [main], with_pbt=True)
    return nc


_NC_CACHE = {}


def _cached(name, fn):
    if name not in _NC_CACHE:
        _NC_CACHE[name] = fn()
    return _NC_CACHE[name]


def kernel(**inputs):
    cores = list(range(8))
    x = np.asarray(inputs["L0_attn_a_x"], dtype=np.float32)
    keys = ["L0_attn_a_qw", "L0_attn_a_kw", "L0_attn_a_vw", "L0_attn_a_ow",
            "L0_attn_b_qkv_w", "L0_attn_b_qkv_b", "L0_attn_b_out_w", "L0_attn_b_out_b",
            "L0_ffn_w_gate", "L0_ffn_w_up", "L0_ffn_w_down",
            "L1_attn_a_wq", "L1_attn_a_wk", "L1_attn_a_wv", "L1_attn_a_wo",
            "L1_attn_a_omega", "L1_attn_a_ln1_g", "L1_attn_a_ln1_b",
            "L1_attn_a_w1", "L1_attn_a_b1", "L1_attn_a_w2", "L1_attn_a_b2",
            "L1_attn_a_ln2_g", "L1_attn_a_ln2_b",
            "L1_ffn_router_w", "L1_ffn_w_gate", "L1_ffn_w_up", "L1_ffn_w_down",
            "L1_attn_b_Wq", "L1_attn_b_bq", "L1_attn_b_Wk", "L1_attn_b_bk",
            "L1_attn_b_Wv", "L1_attn_b_bv", "L1_attn_b_Wo", "L1_attn_b_bo"]
    nc = _cached("full", build_fused_full)
    maps = []
    for i in range(8):
        m = {"xT": np.ascontiguousarray(x[i].T)}
        for k in keys:
            m[k] = inputs[k]
        maps.append(m)
    res = run_bass_kernel_spmd(nc, maps, core_ids=cores)
    out = np.empty((8, 4096, 1024), np.float32)
    for i in range(8):
        yT = np.asarray(res.results[i]["yT"])
        out[:, i * 512:(i + 1) * 512, :] = yT.reshape(1024, 8, 512).transpose(1, 2, 0)
    return out
